# version 16
# baseline (speedup 1.0000x reference)
import os
import sys
import numpy as np
from contextlib import ExitStack

sys.path.insert(0, "/opt/trn_rl_repo")

import concourse.bass as bass
import concourse.mybir as mybir
from concourse.bass_utils import run_bass_kernel_spmd
from concourse.bass_core_rust import IndirectOffsetOnAxis

F32 = mybir.dt.float32
BF16 = mybir.dt.bfloat16
F16 = mybir.dt.float16
I32 = mybir.dt.int32
U32 = mybir.dt.uint32
AF = mybir.ActivationFunctionType
ALU = mybir.AluOpType
AX = mybir.AxisListType

NCORE = 8
B, S, D = 32, 2048, 1024
BL = B // NCORE
DG = D // 128
E2 = 2048
NST = 64
EXP, TOPK, CAP = 64, 6, 384
FH, SF = 512, 2048
TC = 1024
NT = S // 128
TLOC = BL * S
NSLOT = EXP * CAP
NBLK = 4
BIGF = 1.0e30
OORF = float(NSLOT)

GELU_A = 1.5957691216057308
GELU_B = 0.07135481627803054


class Res:
    __slots__ = ("ws", "readers")

    def __init__(self):
        self.ws = []
        self.readers = []


SMALL_TAGS = {
    "ln_sum", "ln_sq2", "ln_mu", "ln_musq", "ln_ex2", "ln_var", "ln_ln",
    "ln_rstd", "ln_apply",
    "sbadd", "max8", "matchrep", "ohsum", "wsc", "wsc2", "rseps", "rinv",
    "locev", "locpre", "rcupd", "idxcp", "maxidx", "gpos", "gpos2r", "gsco",
    "gsco2r", "wraw", "cc0", "cccp", "baseev", "basebcev",
    "gbase", "gbase2r", "gpos2", "vmask", "wfin", "slote", "slotp", "mgw",
    "slotsub", "slotmul", "slotadd", "sloti",
    "hcar", "tailsv", "convcarry", "tail0", "gslot", "gslot2r", "tbl1", "tbl2",
    "cntev", "rcallev",
    "spz", "spneg", "spabs", "spexp", "spln", "sprelu", "spadd",
    "ge_u", "ge_arg", "ge_sig", "ge_cp", "ge_sq",
    "Aexp", "Aneg", "Aevac", "zrow", "cvsig", "cvmul", "zsig", "zmul",
}


class Op:
    __slots__ = ("eng", "fn", "deps", "inc", "need_inc", "seq", "tag", "hard",
                 "sm", "pool")

    def __init__(self, eng, fn, deps, inc, tag="", hard=()):
        self.eng = eng
        self.fn = fn
        self.deps = deps
        self.inc = inc
        self.need_inc = False
        self.seq = 0
        self.tag = tag
        self.hard = list(hard)
        self.sm = tag in SMALL_TAGS
        self.pool = None


class Sched:
    ENGS = ("sync", "vector", "scalar", "tensor", "gpsimd")

    GP_RING = 6
    NPOOL = 16

    def __init__(self, nc, ctx):
        self.nc = nc
        self.ops = []
        self.per_eng = {e: [] for e in self.ENGS}
        self.res = {}
        self.sems = {e: ctx.enter_context(nc.semaphore(f"sem_{e}")) for e in self.ENGS}
        self.gp_dmas = []
        self.pools = {}
        self.pool_rr = {}
        for e in ("sync", "scalar", "gpsimd"):
            self.pools[e] = [ctx.enter_context(nc.semaphore(f"dp_{e}_{j}"))
                             for j in range(self.NPOOL)]
            self.pool_rr[e] = 0

    def R(self, key):
        r = self.res.get(key)
        if r is None:
            r = self.res[key] = Res()
        return r

    def op(self, eng, fn, reads=(), writes=(), deps=(), dma=False, hard=(), tag=""):
        dl = []
        for k in reads:
            dl.extend(self.R(k).ws)
        for k in writes:
            r = self.R(k)
            for w in r.ws:
                if not (dma and w.inc == 16):
                    dl.append(w)
            dl.extend(r.readers)
        dl.extend(deps)
        if dma and eng == "gpsimd" and len(self.gp_dmas) >= self.GP_RING:
            dl.append(self.gp_dmas[-self.GP_RING])
        o = Op(eng, fn, dl, 16 if dma else 1, tag, hard)
        if dma:
            j = self.pool_rr[eng]
            self.pool_rr[eng] = (j + 1) % self.NPOOL
            o.pool = (eng, j)
        if dma and eng == "gpsimd":
            self.gp_dmas.append(o)
        self.per_eng[eng].append(o)
        self.ops.append(o)
        for k in reads:
            self.R(k).readers.append(o)
        for k in writes:
            r = self.R(k)
            if r.readers:
                r.ws = [o]
                r.readers = []
            else:
                r.ws.append(o)
        return o

    def finalize(self):
        for o in self.ops:
            if o.inc == 16:
                o.need_inc = True
            for d in o.deps:
                if d.eng != o.eng or d.inc == 16 or d.sm or o.sm:
                    d.need_inc = True
            for d in o.hard:
                d.need_inc = True
        for e in self.ENGS:
            c = 0
            for o in self.per_eng[e]:
                if o.pool is None:
                    if o.need_inc:
                        c += o.inc
                    o.seq = c
        pc = {}
        for o in self.ops:
            if o.pool is not None:
                pc[o.pool] = pc.get(o.pool, 0) + 16
                o.seq = pc[o.pool]
        self.waits = {}
        for o in self.ops:
            w = {}
            for d in o.deps:
                if d.pool is not None:
                    key = d.pool
                elif d.eng == o.eng and not (d.sm or o.sm):
                    continue
                else:
                    key = d.eng
                w[key] = max(w.get(key, 0), d.seq)
            for d in o.hard:
                key = d.pool if d.pool is not None else d.eng
                w[key] = max(w.get(key, 0), d.seq)
            self.waits[id(o)] = w

    def replay(self, eng_name, eng_obj):
        sems = self.sems
        last = {}
        for o in self.per_eng[eng_name]:
            for key, val in self.waits[id(o)].items():
                if last.get(key, -1) < val:
                    h = self.pools[key[0]][key[1]] if isinstance(key, tuple) \
                        else sems[key]
                    eng_obj.wait_ge(h, val)
                    last[key] = val
            inst = o.fn(eng_obj)
            if o.need_inc:
                if inst is None:
                    raise RuntimeError(f"op {o.tag} needs inc but returned None")
                if o.pool is not None:
                    inst.then_inc(self.pools[o.pool[0]][o.pool[1]], 16)
                else:
                    inst.then_inc(sems[eng_name], o.inc)


def build_host_inputs(inp):
    import ml_dtypes
    bf = ml_dtypes.bfloat16
    f32 = np.float32
    x = np.ascontiguousarray(inp["L0_attn_x"], dtype=f32)
    shards = [np.ascontiguousarray(x[c * BL:(c + 1) * BL].reshape(TLOC, D))
              for c in range(NCORE)]

    w = {}
    w["w_inT"] = np.ascontiguousarray(inp["L0_attn_W_in"].T.astype(f32))
    w["w_dtT"] = np.ascontiguousarray(inp["L0_attn_W_dt"].T.astype(f32))
    w["w_outT"] = np.ascontiguousarray(inp["L0_attn_W_out"].T.astype(f32))
    w["w_bT"] = np.ascontiguousarray(inp["L0_attn_W_B"].T.astype(f32))
    w["w_cT"] = np.ascontiguousarray(inp["L0_attn_W_C"].T.astype(f32))
    w["routT"] = np.ascontiguousarray(inp["L0_ffn_router_weight"].T.astype(f32))
    w["sw1T"] = np.ascontiguousarray(inp["L0_ffn_sw1"].T.astype(f32))
    w["sw2T"] = np.ascontiguousarray(inp["L0_ffn_sw2"].T.astype(f32))
    w["w1"] = np.ascontiguousarray(inp["L0_ffn_w1"].astype(f32))
    w["w2"] = np.ascontiguousarray(inp["L0_ffn_w2"].astype(f32))
    w["conv_w"] = np.ascontiguousarray(inp["L0_attn_conv_w"].reshape(D, 4).astype(f32))
    w["conv_b"] = np.ascontiguousarray(inp["L0_attn_conv_b"].reshape(D, 1).astype(f32))
    w["b_dt"] = np.ascontiguousarray(inp["L0_attn_b_dt"].reshape(D, 1).astype(f32))
    w["dp"] = np.ascontiguousarray(inp["L0_attn_Dp"].reshape(D, 1).astype(f32))
    w["a_log"] = np.ascontiguousarray(inp["L0_attn_A_log"].reshape(1, NST).astype(f32))
    w["bias_bc"] = np.ascontiguousarray(np.broadcast_to(
        inp["L0_ffn_expert_bias"].astype(f32).reshape(1, EXP), (128, EXP)))
    w["lng_row"] = np.ascontiguousarray(np.broadcast_to(
        inp["L0_attn_ln_g"].reshape(1, D), (128, D)).astype(bf))
    w["lnb_row"] = np.ascontiguousarray(np.broadcast_to(
        inp["L0_attn_ln_b"].reshape(1, D), (128, D)).astype(bf))

    c = {}
    c["ones_col_b"] = np.ones((1, 128), bf)
    c["ones_col_f"] = np.ones((1, 128), f32)
    c["onessq_b"] = np.ones((128, 128), bf)
    c["ltstrict_b"] = np.ascontiguousarray(
        np.tril(np.ones((128, 128), f32), k=-1).T.astype(bf))
    c["ident_b"] = np.eye(128, dtype=bf)
    oh = np.zeros((EXP, NST, 128), f32)
    for n in range(NST):
        oh[n, n, :] = 1.0
    c["onehot_b"] = np.ascontiguousarray(oh.reshape(EXP, NST * 128)).astype(bf)
    c["iota384_bc"] = np.ascontiguousarray(np.broadcast_to(
        (np.arange(EXP, dtype=f32) * CAP).reshape(1, EXP), (128, EXP)))
    c["iotaE_bc"] = np.ascontiguousarray(np.broadcast_to(
        np.arange(EXP, dtype=f32).reshape(1, EXP), (128, EXP)))
    c["oor_bc"] = np.full((128, EXP), OORF, f32)
    c["lt128f"] = np.ascontiguousarray(
        np.tril(np.ones((128, 128), f32), k=-1).T)
    c["ones128f"] = np.ones((128, 1), f32)
    kc = np.zeros((128, 2), f32); kc[:, 0] = 1e-5; kc[:, 1] = 1.0
    c["kcol"] = kc
    rm = np.zeros((NCORE, NCORE, 1), f32)
    for cid in range(NCORE):
        rm[cid, :cid, 0] = 1.0

    maps = []
    for cid in range(NCORE):
        m = {"x_in": shards[cid], "rankmask": np.ascontiguousarray(rm[cid])}
        m.update(w)
        m.update(c)
        maps.append(m)
    return maps


def kernel(**inputs):
    maps = build_host_inputs(inputs)
    nc = build_program()
    res = run_bass_kernel_spmd(nc, maps, core_ids=list(range(NCORE)))
    out = np.concatenate([np.asarray(r["x_out"]).reshape(BL, S, D)
                          for r in res.results], axis=0)
    return out.astype(np.float32)


def build_program():
    nc = bass.Bass("TRN2", target_bir_lowering=False, debug=False)
    with ExitStack() as ctx:
        P = Prog(nc, ctx)
        P.declare()
        P.emit_setup()
        P.emit_ffn(0)
        P.dbg_dump("d_xffn1", P.dsc["resid"], ["resid1"])
        P.dbg_dump("d_xntok", P.dsc["xn_tok"], [("xntd", g) for g in range(BL * NT)])
        P.dbg_dump("d_ccout", P.dsc["cc_out"], [("ccout", 0)])
        P.dbg_dump("d_wtab", P.dsc["w_tab"], [("wtab", 0)])
        P.dbg_dump("d_xs", P.dsc["xs_loc"][0:512, :], [("xs", 0)])
        P.dbg_dump("d_ytab", P.dsc["y_tab"][0:512, :], [("ytab", 0)])
        if P.debug:
            P.dma(P.dbg["d_rece"], P.rec_e[:], reads=[("rec_e", g) for g in range(BL * NT)], writes=[("dbg", "rece")])
            P.dma(P.dbg["d_recp"], P.rec_pos[:], reads=[("rec_pos", g) for g in range(BL * NT)], writes=[("dbg", "recp")])
            P.dma(P.dbg["d_recw"], P.rec_w[:], reads=[("rec_w", g) for g in range(BL * NT)], writes=[("dbg", "recw")])
            P.dma(P.dbg["d_ytf"], P.dsc["y_tab"], reads=[("ytab", 0), "ytab_trash"], writes=[("dbg", "ytf")])
            P.dma(P.dbg["d_cnts"], P.dsc["cnts_d"], reads=["cntd"], writes=[("dbg", "cnts")])
            P.dma(P.dbg["d_slot"], P.rec_slot[:], reads=[("rec_slot", g) for g in range(BL * NT)], writes=[("dbg", "slot")])
        P.emit_mamba()
        P.dbg_dump("d_xmam", P.dsc["resid"], ["resid1"])
        P.dbg_dump("d_xn1", P.dsc["xn1_tok"], [("xn1d", g) for g in range(BL * NT)])
        P.dbg_dump("d_zf", P.dsc["z_feat"][:, 0:TC], [("zd", 0, 0)])
        P.emit_ffn(1)
        P.emit_finish()
        P.sched.finalize()
        block = ctx.enter_context(nc.Block())

        @block.sync
        def _(e):
            P.sched.replay("sync", e)

        @block.vector
        def _(e):
            P.sched.replay("vector", e)

        @block.scalar
        def _(e):
            P.sched.replay("scalar", e)

        @block.tensor
        def _(e):
            P.sched.replay("tensor", e)

        @block.gpsimd
        def _(e):
            P.sched.replay("gpsimd", e)
    return nc


class Prog:
    def __init__(self, nc, ctx):
        self.nc = nc
        self.ctx = ctx
        self.sched = Sched(nc, ctx)

    def declare(self):
        nc = self.nc
        ap = lambda t: t.ap()
        self.x_in = ap(nc.dram_tensor("x_in", [TLOC, D], F32, kind="ExternalInput"))
        self.x_out = ap(nc.dram_tensor("x_out", [TLOC, D], F32, kind="ExternalOutput"))

        self.din = {}
        for name, shape, dt in [
            ("w_inT", [D, E2], F32), ("w_dtT", [D, D], F32), ("w_outT", [D, D], F32),
            ("w_bT", [D, NST], F32), ("w_cT", [D, NST], F32), ("routT", [D, EXP], F32),
            ("sw1T", [D, SF], F32), ("sw2T", [SF, D], F32),
            ("w1", [EXP, D, FH], F32), ("w2", [EXP, FH, D], F32),
            ("conv_w", [D, 4], F32), ("conv_b", [D, 1], F32), ("b_dt", [D, 1], F32),
            ("dp", [D, 1], F32), ("a_log", [1, NST], F32), ("bias_bc", [128, EXP], F32),
            ("ones_col_f", [1, 128], F32), ("iota384_bc", [128, EXP], F32),
            ("iotaE_bc", [128, EXP], F32), ("oor_bc", [128, EXP], F32),
            ("rankmask", [NCORE, 1], F32), ("kcol", [128, 2], F32),
            ("lt128f", [128, 128], F32), ("ones128f", [128, 1], F32),
            ("ones_col_b", [1, 128], BF16), ("onessq_b", [128, 128], BF16),
            ("ltstrict_b", [128, 128], BF16), ("ident_b", [128, 128], BF16),
            ("onehot_b", [EXP, NST * 128], BF16),
            ("lng_row", [128, D], BF16), ("lnb_row", [128, D], BF16),
        ]:
            self.din[name] = ap(nc.dram_tensor(name, shape, dt, kind="ExternalInput"))

        dsc = {}
        for name, shape, dt in [
            ("resid", [TLOC, D], F32), ("xn_tok", [TLOC, D], BF16),
            ("xn1_tok", [TLOC, D], BF16), ("xn_feat", [BL * D, S], BF16),
            ("z_feat", [D, BL * S], BF16),
            ("xs_loc", [NSLOT + 1, D], BF16), ("w_tab", [NSLOT + 1, 1], F32),
            ("y_tab", [NSLOT + 1, D], BF16),
            ("bw_inT", [D, E2], BF16), ("bw_dtT", [D, D], BF16),
            ("bw_outT", [D, D], BF16), ("bw_bT", [D, NST], BF16),
            ("bw_cT", [D, NST], BF16), ("broutT", [D, EXP], BF16),
            ("bsw1T", [D, SF], BF16), ("bsw2T", [SF, D], BF16),
            ("bw1", [EXP, 128, DG * FH], BF16), ("bw2", [EXP, 128, (FH // 128) * D], BF16),
            ("cc_in", [1, 128], F32), ("cc_out", [NCORE, 128], F32),
            ("sh_tok", [TLOC, D], BF16), ("cnts_d", [BL * NT, EXP], F32),
        ]:
            dsc[name] = ap(nc.dram_tensor(name, shape, dt))
        self.dsc = dsc
        self.debug = os.environ.get("BK_DEBUG", "") == "1"
        self.dbg = {}
        if self.debug:
            for name, shape, dt in [
                ("d_xffn1", [TLOC, D], F32), ("d_xmam", [TLOC, D], F32),
                ("d_xntok", [TLOC, D], BF16), ("d_ccout", [NCORE, 128], F32),
                ("d_wtab", [NSLOT + 1, 1], F32), ("d_xs", [512, D], BF16),
                ("d_ytab", [512, D], BF16), ("d_rece", [128, BL * NT * TOPK], F32),
                ("d_recp", [128, BL * NT * TOPK], F32),
                ("d_recw", [128, BL * NT * TOPK], F32),
                ("d_xc", [D, TC], BF16), ("d_dt", [D, TC], F16),
                ("d_bm", [NST, TC], BF16), ("d_ytot", [D, TC], F32),
                ("d_xn1", [TLOC, D], BF16), ("d_zf", [D, TC], BF16),
                ("d_ytf", [NSLOT + 1, D], BF16),
                ("d_slot", [128, BL * NT * TOPK], I32),
                ("d_cnts", [BL * NT, EXP], F32), ("d_rc", [64, EXP], F32),
            ]:
                self.dbg[name] = ap(nc.dram_tensor(name, shape, dt, kind="ExternalOutput"))

        sb = lambda n, s, d: self.ctx.enter_context(nc.sbuf_tensor(n, s, d))
        self.c_onescol_b = sb("c_onescol_b", [1, 128], BF16)
        self.c_onescol_f = sb("c_onescol_f", [1, 128], F32)
        self.c_onessq = sb("c_onessq", [128, 128], BF16)
        self.c_lt = sb("c_lt", [128, 128], BF16)
        self.c_id = sb("c_id", [128, 128], BF16)
        self.c_iota384 = sb("c_iota384", [128, EXP], F32)
        self.c_iotaE = sb("c_iotaE", [128, EXP], F32)
        self.c_oor = sb("c_oor", [128, EXP], F32)
        self.c_bias = sb("c_bias", [128, EXP], F32)
        self.c_rank = sb("c_rank", [NCORE, 1], F32)
        self.c_kcol = sb("c_kcol", [128, 2], F32)
        self.c_ltf = sb("c_ltf", [128, 128], F32)
        self.c_ones128f = sb("c_ones128f", [128, 1], F32)
        self.c_lng = sb("c_lng", [128, D], BF16)
        self.c_lnb = sb("c_lnb", [128, D], BF16)
        self.c_oneh = sb("c_oneh", [EXP, NST * 128], BF16)
        self.p_convw = sb("p_convw", [128, 4 * DG], F32)
        self.p_convb = sb("p_convb", [128, DG], F32)
        self.p_bdt = sb("p_bdt", [128, DG], F32)
        self.p_dp = sb("p_dp", [128, DG], F32)
        self.p_A = sb("p_A", [128, NST], F32)
        self.wr_rt = sb("wr_rt", [128, DG * EXP], BF16)
        self.t_base = sb("t_base", [128, EXP], F32)
        self.row_a = sb("row_a", [1, 128], F32)
        self.row_b = sb("row_b", [1, 128], F32)
        self.row_c = sb("row_c", [NCORE, 128], F32)
        self.rec_e = sb("rec_e", [128, BL * NT * TOPK], F32)
        self.rec_pos = sb("rec_pos", [128, BL * NT * TOPK], F32)
        self.rec_w = sb("rec_w", [128, BL * NT * TOPK], F32)
        self.rec_slot = sb("rec_slot", [128, BL * NT * TOPK], I32)
        self.idx8u = [sb("idx8u0", [128, 8], U32), sb("idx8u1", [128, 8], U32)]
        self.ar_b = sb("ar_b", [128, 45216], BF16)
        self.ar_f = sb("ar_f", [128, 11792], F32)
        self.ar_h = sb("ar_h", [128, 8192], F16)
        self.ar_w = sb("ar_w", [128, 12288], BF16)
        ps = lambda n, s, d=F32: self.ctx.enter_context(nc.psum_tensor(n, s, d))
        self.ps_a = ps("ps_a", [128, 512])
        self.ps_b = ps("ps_b", [128, 512])
        self.ps_c = ps("ps_c", [128, 512])
        self.ps_d = ps("ps_d", [128, 512])
        self.ps_e = ps("ps_e", [128, 512])
        self.ps_f = ps("ps_f", [128, 512])
        self.ps_t1 = ps("ps_t1", [128, 1024], BF16)
        self.ps_t2 = ps("ps_t2", [128, 1024], BF16)
        self._cur = {}

    def arena(self, which, cols, phase):
        key = (which, phase)
        cur = self._cur.get(key, 0)
        t = {"b": self.ar_b, "f": self.ar_f, "h": self.ar_h, "w": self.ar_w}[which]
        if cur + cols > t.shape[1]:
            raise RuntimeError(f"arena {which} overflow in {phase}: {cur}+{cols}>{t.shape[1]}")
        self._cur[key] = cur + cols
        return t[:, cur:cur + cols]

    def dma(self, out, in_, reads=(), writes=(), deps=(), q="sync", tag=""):
        return self.sched.op(q, lambda e: e.dma_start(out=out, in_=in_),
                             reads=reads, writes=writes, deps=deps, dma=True, tag=tag)

    def dbg_dump(self, name, src_ap, reads, tag="dbg"):
        if not self.debug:
            return
        self.dma(self.dbg[name], src_ap, reads=reads, writes=[("dbg", name)],
                 tag=tag)

    def emit_setup(self):
        sc = self.sched
        din = self.din
        for t, key in [
            (self.c_onescol_b, "ones_col_b"), (self.c_onescol_f, "ones_col_f"),
            (self.c_onessq, "onessq_b"), (self.c_lt, "ltstrict_b"),
            (self.c_id, "ident_b"), (self.c_iota384, "iota384_bc"),
            (self.c_iotaE, "iotaE_bc"), (self.c_oor, "oor_bc"),
            (self.c_bias, "bias_bc"), (self.c_rank, "rankmask"),
            (self.c_lng, "lng_row"), (self.c_lnb, "lnb_row"),
            (self.c_oneh, "onehot_b"), (self.c_kcol, "kcol"),
            (self.c_ltf, "lt128f"), (self.c_ones128f, "ones128f"),
        ]:
            self.dma(t[:], din[key], writes=[key], tag="cld")
        for g in range(DG):
            self.dma(self.p_convw[:, 4 * g:4 * g + 4],
                     din["conv_w"][g * 128:(g + 1) * 128, :], writes=["p_convw"])
            self.dma(self.p_convb[:, g:g + 1],
                     din["conv_b"][g * 128:(g + 1) * 128, :], writes=["p_convb"])
            self.dma(self.p_bdt[:, g:g + 1],
                     din["b_dt"][g * 128:(g + 1) * 128, :], writes=["p_bdt"])
            self.dma(self.p_dp[:, g:g + 1],
                     din["dp"][g * 128:(g + 1) * 128, :], writes=["p_dp"])
        self.dma(self.row_a[:, 0:NST], din["a_log"], writes=["row_a"])
        sc.op("scalar", lambda e: e.activation(out=self.row_b[:, 0:NST],
                                               in_=self.row_a[:, 0:NST], func=AF.Exp),
              reads=["row_a"], writes=["row_b"], tag="Aexp")
        sc.op("vector", lambda e: e.tensor_scalar_mul(out=self.row_b[:, 0:NST],
                                                      in0=self.row_b[:, 0:NST],
                                                      scalar1=-1.0),
              reads=["row_b"], writes=["row_b"], tag="Aneg")
        sc.op("tensor", lambda e: e.matmul(out=self.ps_a[:, 0:NST],
                                           lhsT=self.c_onescol_f[:],
                                           rhs=self.row_b[:, 0:NST],
                                           start=True, stop=True),
              reads=["row_b", "ones_col_f"], writes=["ps_a"], tag="Abc")
        fA = sc.op("tensor", lambda e: e.matmul(out=self.ps_b[:, 0:NST],
                                                lhsT=self.c_onescol_f[:],
                                                rhs=self.row_b[:, 0:NST],
                                                start=True, stop=True),
                   reads=["row_b", "ones_col_f"], writes=["ps_b"], tag="Abc2")
        sc.op("vector", lambda e: e.tensor_copy(out=self.p_A[:], in_=self.ps_a[:, 0:NST]),
              reads=["ps_a"], writes=["p_A", "ps_a", "ps_b"], deps=[fA], tag="Aevac")

        stage_f = self.arena("f", 4096, "setup")
        stage_b = self.arena("w", 4096, "setup")
        for (src, dst, R, C) in [
            ("w_inT", "bw_inT", D, E2), ("w_dtT", "bw_dtT", D, D),
            ("w_outT", "bw_outT", D, D), ("w_bT", "bw_bT", D, NST),
            ("w_cT", "bw_cT", D, NST), ("routT", "broutT", D, EXP),
            ("sw1T", "bsw1T", D, SF), ("sw2T", "bsw2T", SF, D),
        ]:
            sa = self.din[src].rearrange("(a p) c -> a p c", p=128)
            da = self.dsc[dst].rearrange("(a p) c -> a p c", p=128)
            for a in range(R // 128):
                step = min(4096, C)
                for c0 in range(0, C, step):
                    n = min(step, C - c0)
                    self.dma(stage_f[:, 0:n], sa[a, :, c0:c0 + n],
                             writes=["wc_f"], tag="wcld")
                    sc.op("vector", lambda e, n=n: e.tensor_copy(
                        out=stage_b[:, 0:n], in_=stage_f[:, 0:n]),
                        reads=["wc_f"], writes=["wc_b"], tag="wccast")
                    self.dma(da[a, :, c0:c0 + n], stage_b[:, 0:n],
                             reads=["wc_b"], writes=[("bwd", dst)], tag="wcst")
        zrow = self.arena("b", 1024, "setup")
        sc.op("vector", lambda e: e.memset(zrow[0:1, :], 0.0),
              writes=["zrow"], tag="zrow")
        self.dma(self.dsc["y_tab"][NSLOT:NSLOT + 1, :], zrow[0:1, :],
                 reads=["zrow"], writes=["ytab_trash"], tag="ztrash")
        rt8 = self.dsc["broutT"].rearrange("(a p) c -> a p c", p=128)
        for a in range(DG):
            self.dma(self.wr_rt[:, a * EXP:(a + 1) * EXP], rt8[a, :, :],
                     reads=[("bwd", "broutT")], writes=["wr_rt"], tag="rtld")

    def _ln_tile(self, xin, kx, st, kst, sq, ksq, xout, kxo):
        for _ in self._ln_tile_gen(xin, kx, st, kst, sq, ksq, xout, kxo):
            pass

    def _ln_tile_gen(self, xin, kx, st, kst, sq, ksq, xout, kxo):
        sc = self.sched
        yield sc.op("vector", lambda e: e.reduce_sum(out=st[:, 0:1], in_=xin[:], axis=AX.X),
              reads=[kx], writes=[kst], tag="ln_sum")
        yield sc.op("vector", lambda e: e.tensor_mul(out=sq[:], in0=xin[:], in1=xin[:]),
            reads=[kx], writes=[ksq], tag="ln_sq")
        yield sc.op("vector", lambda e: e.reduce_sum(out=st[:, 1:2], in_=sq[:], axis=AX.X),
            reads=[ksq], writes=[kst], tag="ln_sq2")
        yield sc.op("vector", lambda e: e.tensor_scalar_mul(
            out=st[:, 2:3], in0=st[:, 0:1], scalar1=-1.0 / D),
            reads=[kst], writes=[kst], tag="ln_mu")
        yield sc.op("vector", lambda e: e.scalar_tensor_tensor(
            out=st[:, 3:4], in0=st[:, 2:3], scalar=1.0, in1=st[:, 2:3],
            op0=ALU.mult, op1=ALU.mult),
            reads=[kst], writes=[kst], tag="ln_musq")
        yield sc.op("vector", lambda e: e.tensor_scalar(
            out=st[:, 1:2], in0=st[:, 1:2], scalar1=1.0 / D, scalar2=None,
            op0=ALU.mult),
            reads=[kst], writes=[kst], tag="ln_ex2")
        yield sc.op("vector", lambda e: e.tensor_sub(
            out=st[:, 1:2], in0=st[:, 1:2], in1=st[:, 3:4]),
            reads=[kst], writes=[kst], tag="ln_var")
        yield sc.op("scalar", lambda e: e.activation(
            out=st[:, 4:5], in_=st[:, 1:2], func=AF.Ln,
            bias=self.c_kcol[:, 0:1]),
            reads=[kst, "kcol"], writes=[kst], tag="ln_ln")
        yield sc.op("scalar", lambda e: e.activation(
            out=st[:, 5:6], in_=st[:, 4:5], func=AF.Exp, scale=-0.5),
            reads=[kst], writes=[kst], tag="ln_rstd")
        yield sc.op("vector", lambda e: e.tensor_scalar(
            out=xout[:], in0=xin[:], scalar1=st[:, 2:3], scalar2=st[:, 5:6],
            op0=ALU.add, op1=ALU.mult),
            reads=[kx, kst], writes=[kxo], tag="ln_apply")

    def _gelu(self, pp, ppk, cols, gxe, kgx, gsc_, kgs, gu, kgu, outap, kout):
        sc = self.sched
        sc.op("scalar", lambda e: e.activation(out=gxe[:, 0:cols], in_=pp[:, 0:cols],
                                               func=AF.Copy),
              reads=[ppk], writes=[kgx], tag="ge_cp")
        sc.op("scalar", lambda e: e.activation(out=gsc_[:, 0:cols], in_=pp[:, 0:cols],
                                               func=AF.Square),
              reads=[ppk], writes=[kgs, ppk], tag="ge_sq")
        sc.op("vector", lambda e: e.tensor_scalar(
            out=gu[:, 0:cols], in0=gsc_[:, 0:cols], scalar1=GELU_B, scalar2=GELU_A,
            op0=ALU.mult, op1=ALU.add),
            reads=[kgs], writes=[kgu], tag="ge_u")
        sc.op("vector", lambda e: e.tensor_mul(
            out=gu[:, 0:cols], in0=gu[:, 0:cols], in1=gxe[:, 0:cols]),
            reads=[kgu, kgx], writes=[kgu], tag="ge_arg")
        sc.op("scalar", lambda e: e.activation(out=gu[:, 0:cols], in_=gu[:, 0:cols],
                                               func=AF.Sigmoid),
              reads=[kgu], writes=[kgu], tag="ge_sig")
        sc.op("vector", lambda e: e.tensor_mul(
            out=outap, in0=gu[:, 0:cols], in1=gxe[:, 0:cols]),
            reads=[kgu, kgx], writes=[kout], tag="ge_mul")

    def emit_ffn(self, call):
        sc = self.sched
        ph = f"ffn{call}"
        A = self.arena
        src_x = self.x_in if call == 0 else self.dsc["resid"]
        dst_x = self.dsc["resid"] if call == 0 else self.x_out
        src_key = "x_in" if call == 0 else "resid1"
        dst_key = "resid1" if call == 0 else "xout"

        xb = [A("f", 1024, ph) for _ in range(2)]
        stat = [A("f", 8, ph) for _ in range(2)]
        sbt = [A("f", 66, ph) for _ in range(2)]
        sct = [A("f", 64, ph) for _ in range(2)]
        sbm = [A("f", 66, ph) for _ in range(2)]
        m8t = [A("f", 8, ph) for _ in range(2)]
        scot = [A("f", 8, ph) for _ in range(2)]
        idx8 = self.idx8u
        pbt = [A("f", 8, ph) for _ in range(2)]
        vmt = [A("f", 8, ph) for _ in range(2)]
        slf = [A("f", 8, ph) for _ in range(2)]
        wkc = [A("f", 8, ph) for _ in range(2)]
        ccs = A("f", 128, ph)
        gsc_ = [A("f", 512, ph) for _ in range(2)]
        gsf = [A("f", 64, ph) for _ in range(2)]

        sqs = [A("b", 1024, ph) for _ in range(2)]
        xnt = [A("b", 1024, ph) for _ in range(2)]
        xnf = [A("b", 1024, ph) for _ in range(2)]
        oht = [A("f", 64, ph) for _ in range(2)]
        wsct = [A("f", 64, ph) for _ in range(2)]
        xtk = [A("b", 1024, ph) for _ in range(2)]
        xstk = [A("b", 1024, ph) for _ in range(2)]
        xsT = A("b", 3072, ph)
        hidE = A("b", 1536, ph)
        ytk = [A("b", 1024, ph) for _ in range(2)]
        gxe = [A("b", 512, ph) for _ in range(2)]
        gub = [A("b", 512, ph) for _ in range(2)]
        xnfr = A("b", 4096, ph)
        hidS = A("b", 8192, ph)
        sht = [A("b", 1024, ph) for _ in range(4)]
        macc = [A("f", 1024, ph) for _ in range(2)]
        gbuf = [A("b", 1024, ph) for _ in range(4)]
        w1b = self.ar_w[:, 0:4096]
        w2b = self.ar_w[:, 4096:8192]
        sw1b = [self.ar_w[:, 8192:10240], self.ar_w[:, 10240:12288]]
        w1o = self.ar_w[:, 8192:12288]

        xn_tok_d = self.dsc["xn_tok"]
        xn_feat_d = self.dsc["xn_feat"]

        sc.op("vector", lambda e: e.memset(sbt[0][:, 64:66], BIGF),
              writes=[f"{ph}_sbp0"], tag="sbpad")
        sc.op("vector", lambda e: e.memset(sbt[1][:, 64:66], BIGF),
              writes=[f"{ph}_sbp1"], tag="sbpad")

        loc_all = A("h", BL * NT * EXP, ph)
        cnts_row = A("f", BL * NT * EXP, ph)
        rcst = [A("f", 384, ph) for _ in range(2)]
        tb1 = [A("f", 64, ph) for _ in range(2)]
        tb2 = [A("f", 64, ph) for _ in range(2)]
        rc6 = [A("f", 8, ph) for _ in range(2)]

        def passa_tile(g):
            s, i = g // NT, g % NT
            r = g % 2
            gk = g * TOPK
            row0 = g * 128
            kx = f"{ph}_xb{r}"
            yield self.dma(xb[r][:], src_x[row0:row0 + 128, :],
                           reads=[src_key], writes=[kx], tag="ldx")
            for _o in self._ln_tile_gen(xb[r], kx, stat[r], f"{ph}_st{r}",
                                        sqs[r], f"{ph}_sq{r}", xnt[r], f"{ph}_xnt{r}"):
                yield _o
            yield self.dma(xn_tok_d[row0:row0 + 128, :], xnt[r][:],
                           reads=[f"{ph}_xnt{r}"], writes=[("xntd", g)], tag="stxn")
            for gd in range(DG):
                yield sc.op("tensor", lambda e, r=r, gd=gd: e.transpose(
                    out=self.ps_t1[:, gd * 128:(gd + 1) * 128] if r == 0 else
                        self.ps_t2[:, gd * 128:(gd + 1) * 128],
                    in_=xnt[r][:, gd * 128:(gd + 1) * 128], identity=self.c_id[:]),
                    reads=[f"{ph}_xnt{r}", "ident_b"], writes=[("pt", r, gd)],
                    tag="xT")
            yield sc.op("vector", lambda e, r=r: e.tensor_copy(
                out=xnf[r][:], in_=self.ps_t1[:] if r == 0 else self.ps_t2[:]),
                reads=[("pt", r, gd) for gd in range(DG)],
                writes=[f"{ph}_xnf{r}"] + [("pt", r, gd) for gd in range(DG)],
                tag="xTev")
            for gd in range(DG):
                yield self.dma(xn_feat_d[s * D + gd * 128: s * D + (gd + 1) * 128,
                                         i * 128:(i + 1) * 128],
                               xnf[r][:, gd * 128:(gd + 1) * 128],
                               reads=[f"{ph}_xnf{r}"], writes=[("xnfd", s, gd, i)],
                               tag="stxf")
            ppl, pplk = (self.ps_a, "ps_a") if r == 0 else (self.ps_b, "ps_b")
            for kd in range(DG):
                yield sc.op("tensor", lambda e, r=r, kd=kd, ppl=ppl: e.matmul(
                    out=ppl[:, 0:EXP],
                    lhsT=xnf[r][:, kd * 128:(kd + 1) * 128],
                    rhs=self.wr_rt[:, kd * EXP:(kd + 1) * EXP],
                    start=(kd == 0), stop=(kd == DG - 1)),
                    reads=[f"{ph}_xnf{r}", "wr_rt"], writes=[pplk], tag="rtmm")
            yield sc.op("scalar", lambda e, r=r, ppl=ppl: e.activation(
                out=sct[r][:], in_=ppl[:, 0:EXP], func=AF.Sigmoid),
                reads=[pplk], writes=[f"{ph}_sc{r}", pplk], tag="sigm")
            o_sb = sc.op("vector", lambda e, r=r: e.tensor_add(
                out=sbt[r][:, 0:EXP], in0=sct[r][:], in1=self.c_bias[:]),
                reads=[f"{ph}_sc{r}", "bias_bc", f"{ph}_sbp{r}"],
                writes=[f"{ph}_sbt{r}"], tag="sbadd")
            yield o_sb
            o_m8 = sc.op("vector", lambda e, r=r: e.max(
                out=m8t[r][:], in_=sbt[r][:]),
                reads=[f"{ph}_sbt{r}"], writes=[f"{ph}_m8{r}"], hard=[o_sb],
                tag="max8")
            yield o_m8
            o_mr = sc.op("vector", lambda e, r=r: e.match_replace(
                out=sbm[r][:], in_to_replace=m8t[r][:], in_values=sbt[r][:],
                imm_value=-BIGF),
                reads=[f"{ph}_m8{r}", f"{ph}_sbt{r}"], writes=[f"{ph}_sbm{r}"],
                hard=[o_m8], tag="matchrep")
            yield o_mr
            o_oh = sc.op("vector", lambda e, r=r: e.tensor_scalar(
                out=oht[r][:], in0=sbm[r][:, 0:EXP], scalar1=-BIGF, scalar2=None,
                op0=ALU.is_equal),
                reads=[f"{ph}_sbm{r}"], writes=[f"{ph}_oh{r}"], hard=[o_mr],
                tag="ohsum")
            yield o_oh
            yield sc.op("vector", lambda e, r=r: e.tensor_mul(
                out=wsct[r][:], in0=oht[r][:], in1=sct[r][:]),
                reads=[f"{ph}_oh{r}", f"{ph}_sc{r}"],
                writes=[f"{ph}_wsc{r}"], tag="wsc")
            yield sc.op("vector", lambda e, r=r: e.reduce_sum(
                out=stat[r][:, 6:7], in_=wsct[r][:], axis=AX.X),
                reads=[f"{ph}_wsc{r}"], writes=[f"{ph}_st{r}"], tag="wsc2")
            yield sc.op("vector", lambda e, r=r: e.tensor_scalar_add(
                out=stat[r][:, 6:7], in0=stat[r][:, 6:7], scalar1=1e-20),
                reads=[f"{ph}_st{r}"], writes=[f"{ph}_st{r}"], tag="rseps")
            yield sc.op("vector", lambda e, r=r: e.reciprocal(
                out=stat[r][:, 7:8], in_=stat[r][:, 6:7]),
                reads=[f"{ph}_st{r}"], writes=[f"{ph}_st{r}"], tag="rinv")
            ppc, ppck = (self.ps_c, "ps_c") if r == 0 else (self.ps_d, "ps_d")
            yield sc.op("tensor", lambda e, r=r, ppc=ppc: e.matmul(
                out=ppc[:, 0:EXP], lhsT=self.c_ltf[:], rhs=oht[r][:],
                start=True, stop=True),
                reads=[f"{ph}_oh{r}", "lt128f"], writes=[ppck], tag="ltmm")
            yield sc.op("tensor", lambda e, r=r, ppc=ppc: e.matmul(
                out=ppc[0:1, 256:256 + EXP], lhsT=self.c_ones128f[:],
                rhs=oht[r][:], start=True, stop=True),
                reads=[f"{ph}_oh{r}", "ones128f"], writes=[ppck], tag="cntmm")
            yield sc.op("vector", lambda e, g=g, ppc=ppc: e.tensor_copy(
                out=loc_all[:, g * EXP:(g + 1) * EXP], in_=ppc[:, 0:EXP]),
                reads=[ppck], writes=[("loc", g)], tag="locev")
            yield sc.op("vector", lambda e, g=g, ppc=ppc: e.tensor_copy(
                out=cnts_row[0:1, g * EXP:(g + 1) * EXP], in_=ppc[0:1, 256:256 + EXP]),
                reads=[ppck], writes=[("cnt", g), ppck], tag="cntev")
            yield sc.op("vector", lambda e, r=r: e.max_index(
                out=idx8[r][:], in_max=m8t[r][:], in_values=sbt[r][:]),
                reads=[f"{ph}_m8{r}", f"{ph}_sbt{r}"], writes=[f"{ph}_idx{r}"],
                hard=[o_oh], tag="maxidx")
            yield sc.op("vector", lambda e, r=r, gk=gk: e.tensor_copy(
                out=self.rec_e[:, gk:gk + TOPK], in_=idx8[r][:, 2:8]),
                reads=[f"{ph}_idx{r}"], writes=[("rec_e", g)], tag="idxcp")
            for k in range(TOPK):
                yield sc.op("vector", lambda e, r=r, k=k, gk=gk: e.scalar_tensor_tensor(
                    out=gsf[r][:], in0=self.c_iotaE[:],
                    scalar=self.rec_e[:, gk + k:gk + k + 1],
                    in1=loc_all[:, (gk // TOPK) * EXP:(gk // TOPK + 1) * EXP],
                    op0=ALU.is_equal, op1=ALU.mult),
                    reads=[("rec_e", g), ("loc", g), "iotaE_bc"],
                    writes=[f"{ph}_gsf{r}"], tag="gpos")
                yield sc.op("vector", lambda e, r=r, k=k, gk=gk: e.reduce_sum(
                    out=self.rec_pos[:, gk + k:gk + k + 1], in_=gsf[r][:],
                    axis=AX.X),
                    reads=[f"{ph}_gsf{r}"], writes=[("rec_pos", g)], tag="gpos2r")
                yield sc.op("vector", lambda e, r=r, k=k: e.scalar_tensor_tensor(
                    out=gsf[r][:], in0=sbt[r][:, 0:EXP],
                    scalar=m8t[r][:, 2 + k:3 + k], in1=sct[r][:],
                    op0=ALU.is_equal, op1=ALU.mult),
                    reads=[f"{ph}_sbt{r}", f"{ph}_m8{r}", f"{ph}_sc{r}"],
                    writes=[f"{ph}_gsf{r}"], tag="gsco")
                yield sc.op("vector", lambda e, r=r, k=k: e.reduce_sum(
                    out=scot[r][:, k:k + 1], in_=gsf[r][:], axis=AX.X),
                    reads=[f"{ph}_gsf{r}"], writes=[f"{ph}_sco{r}"], tag="gsco2r")
            yield sc.op("vector", lambda e, r=r, gk=gk: e.tensor_scalar(
                out=self.rec_w[:, gk:gk + TOPK], in0=scot[r][:, 0:TOPK],
                scalar1=stat[r][:, 7:8], scalar2=None, op0=ALU.mult),
                reads=[f"{ph}_sco{r}", f"{ph}_st{r}"], writes=[("rec_w", g)],
                tag="wraw")

        for g0 in range(0, BL * NT, 2):
            gens = [passa_tile(g0), passa_tile(g0 + 1)]
            alive = [True, True]
            while any(alive):
                for j in range(2):
                    if alive[j]:
                        try:
                            next(gens[j])
                        except StopIteration:
                            alive[j] = False

        if call == 0:
            self.emit_wcast()

        NTT = BL * NT
        self.dma(self.dsc["cnts_d"].rearrange("a b -> (a b)").rearrange("(o c) -> o c", o=1),
                 cnts_row[0:1, :], reads=[("cnt", g) for g in range(NTT)],
                 writes=["cntd"], tag="cntst")
        cm_sb = A("f", 64, ph)
        self.dma(cm_sb[0:64, 0:EXP], self.dsc["cnts_d"],
                 reads=["cntd"], writes=["cntm"], tag="cntld")
        sc.op("tensor", lambda e: e.matmul(
            out=self.ps_e[0:64, 0:EXP], lhsT=self.c_ltf[0:64, 0:64],
            rhs=cm_sb[0:64, 0:EXP], start=True, stop=True),
            reads=["cntm", "lt128f"], writes=["ps_e"], tag="rcallmm")
        rc_sb = A("f", 64, ph)
        sc.op("vector", lambda e: e.tensor_copy(out=rc_sb[0:64, 0:EXP],
                                                in_=self.ps_e[0:64, 0:EXP]),
              reads=["ps_e"], writes=["rcsb", "ps_e"], tag="rcallev")
        if self.debug and call == 0:
            self.dma(self.dbg["d_rc"], rc_sb[0:64, 0:EXP], reads=["rcsb"],
                     writes=[("dbg", "rc")], tag="dbgrc")
        sc.op("tensor", lambda e: e.matmul(
            out=self.ps_f[0:1, 0:EXP], lhsT=self.c_ones128f[0:64, :],
            rhs=cm_sb[0:64, 0:EXP], start=True, stop=True),
            reads=["cntm", "ones128f"], writes=["ps_f"], tag="totmm")
        sc.op("vector", lambda e: e.memset(ccs[0:1, :], 0.0),
              writes=[f"{ph}_cc"], tag="cc0")
        sc.op("vector", lambda e: e.tensor_copy(out=ccs[0:1, 0:EXP],
                                                in_=self.ps_f[0:1, 0:EXP]),
              reads=["ps_f"], writes=[f"{ph}_cc", "ps_f"], tag="cccp")
        self.dma(self.dsc["cc_in"], ccs[0:1, :], reads=[f"{ph}_cc"],
                 writes=[("ccin", call)], tag="ccst")
        ccsem = self.ctx.enter_context(self.nc.semaphore(f"ccsem{call}"))

        if os.environ.get("BK_NOCC", "") == "1":
            for rr_ in range(NCORE):
                self.dma(self.dsc["cc_out"][rr_:rr_ + 1, :], self.dsc["cc_in"],
                         reads=[("ccin", call)], writes=[("ccout", call)],
                         q="gpsimd", tag="fakecc")
        else:
            def cc_fn(e, ccsem=ccsem):
                e.collective_compute(
                    "AllGather", ALU.bypass, replica_groups=[list(range(NCORE))],
                    ins=[self.dsc["cc_in"].opt()],
                    outs=[self.dsc["cc_out"].opt()]).then_inc(ccsem)
                e.wait_ge(ccsem, 1)
                return e.engine_nop()
            sc.op("gpsimd", cc_fn, reads=[("ccin", call)], writes=[("ccout", call)],
                  tag="allgather")
        self.dma(self.row_c[:], self.dsc["cc_out"], reads=[("ccout", call)],
                 writes=["row_c"], tag="ccld")
        sc.op("tensor", lambda e: e.matmul(
            out=self.ps_a[0:1, 0:128], lhsT=self.c_rank[:], rhs=self.row_c[:],
            start=True, stop=True),
            reads=["row_c", "rankmask"], writes=["ps_a"], tag="basemm")
        sc.op("vector", lambda e: e.tensor_copy(out=self.row_b[:],
                                                in_=self.ps_a[0:1, 0:128]),
              reads=["ps_a"], writes=["row_b", "ps_a"], tag="baseev")
        sc.op("tensor", lambda e: e.matmul(
            out=self.ps_a[:, 0:EXP], lhsT=self.c_onescol_f[:],
            rhs=self.row_b[:, 0:EXP], start=True, stop=True),
            reads=["row_b", "ones_col_f"], writes=["ps_a"], tag="basebc")
        fence = sc.op("tensor", lambda e: e.matmul(
            out=self.ps_b[:, 0:EXP], lhsT=self.c_onescol_f[:],
            rhs=self.row_b[:, 0:EXP], start=True, stop=True),
            reads=["row_b", "ones_col_f"], writes=["ps_b"], tag="basebc2")
        sc.op("vector", lambda e: e.tensor_copy(out=self.t_base[:],
                                                in_=self.ps_a[:, 0:EXP]),
              reads=["ps_a"], writes=["t_base", "ps_a", "ps_b"], deps=[fence],
              tag="basebcev")

        sw1_d = self.dsc["bsw1T"].rearrange("(a p) c -> a p c", p=128)
        sw2_d = self.dsc["bsw2T"].rearrange("(a p) c -> a p c", p=128)
        NSF = SF // 128
        for s in range(BL):
            for tc in range(4):
                for kd in range(DG):
                    self.dma(xnfr[:, kd * 512:(kd + 1) * 512],
                             self.dsc["xn_feat"][s * D + kd * 128: s * D + (kd + 1) * 128,
                                                 tc * 512:(tc + 1) * 512],
                             reads=[("xnfd", s, kd, tc * 4 + q) for q in range(4)],
                             writes=[(f"{ph}_xnfr", kd)], tag="shxld")
                xnfr_all = [(f"{ph}_xnfr", kd) for kd in range(DG)]
                for mf in range(NSF):
                    wb = sw1b[mf % 2]
                    wkey = f"{ph}_swb{mf % 2}"
                    for a in range(DG):
                        self.dma(wb[:, a * 128:(a + 1) * 128],
                                 sw1_d[a, :, mf * 128:(mf + 1) * 128],
                                 reads=[("bwd", "bsw1T")], writes=[wkey], tag="sw1ld")
                    pp, ppk = (self.ps_a, "ps_a") if mf % 2 == 0 else (self.ps_b, "ps_b")
                    for kd in range(DG):
                        sc.op("tensor", lambda e, kd=kd, wb=wb, pp=pp: e.matmul(
                            out=pp[:], lhsT=wb[:, kd * 128:(kd + 1) * 128],
                            rhs=xnfr[:, kd * 512:(kd + 1) * 512],
                            start=(kd == 0), stop=(kd == DG - 1)),
                            reads=xnfr_all + [wkey], writes=[ppk], tag="sw1mm")
                    rr = mf % 2
                    self._gelu(pp, ppk, 512, gxe[rr], f"{ph}_gx{rr}", gsc_[rr],
                               f"{ph}_gs{rr}", gub[rr], f"{ph}_gu{rr}",
                               hidS[:, mf * 512:(mf + 1) * 512], (f"{ph}_hidS", mf))
                hidS_all = [(f"{ph}_hidS", mf) for mf in range(NSF)]
                psb = [(self.ps_a, "ps_a"), (self.ps_b, "ps_b"),
                       (self.ps_c, "ps_c"), (self.ps_d, "ps_d")]
                for hhf in range(2):
                    wb = sw1b[hhf]
                    wkey = f"{ph}_swc{hhf}"
                    for grp in range(4):
                        for q in range(4):
                            kf = grp * 4 + q
                            self.dma(wb[:, q * 512:(q + 1) * 512],
                                     sw2_d[kf, :, hhf * 512:(hhf + 1) * 512],
                                     reads=[("bwd", "bsw2T")], writes=[(wkey, q)],
                                     tag="sw2ld")
                            for tt in range(4):
                                pp, ppk = psb[tt]
                                sc.op("tensor", lambda e, kf=kf, q=q, wb=wb, pp=pp,
                                      tt=tt: e.matmul(
                                          out=pp[:],
                                          lhsT=hidS[:, kf * 512 + tt * 128:
                                                    kf * 512 + (tt + 1) * 128],
                                          rhs=wb[:, q * 512:(q + 1) * 512],
                                          start=(kf == 0), stop=(kf == NSF - 1)),
                                      reads=hidS_all + [(wkey, q)], writes=[ppk],
                                      tag="sw2mm")
                    for tt in range(4):
                        g = s * NT + tc * 4 + tt
                        pp, ppk = psb[tt]
                        sc.op("vector", lambda e, g=g, hhf=hhf, pp=pp: e.tensor_copy(
                            out=sht[g % 4][:, hhf * 512:(hhf + 1) * 512], in_=pp[:]),
                            reads=[ppk], writes=[(f"{ph}_sht", g % 4), ppk],
                            tag="shev")
                for tt in range(4):
                    g = s * NT + tc * 4 + tt
                    row0 = g * 128
                    self.dma(self.dsc["sh_tok"][row0:row0 + 128, :], sht[g % 4][:],
                             reads=[(f"{ph}_sht", g % 4)], writes=[("shd", call, g)],
                             tag="shst")

        def passb_tile(g):
            r = g % 2
            gk = g * TOPK
            row0 = g * 128
            yield self.dma(xtk[r][:], xn_tok_d[row0:row0 + 128, :],
                           reads=[("xntd", g)], writes=[f"{ph}_xtk{r}"], tag="ldxtk")
            yield self.dma(rcst[r][0:1, 0:EXP], rc_sb[g:g + 1, 0:EXP],
                           reads=["rcsb"], writes=[(f"{ph}_rcst", r)], tag="rcmv")
            ppb, ppbk = (self.ps_e, "ps_e") if r == 0 else (self.ps_f, "ps_f")
            yield sc.op("tensor", lambda e, r=r, ppb=ppb: e.matmul(
                out=ppb[:, 0:384], lhsT=self.c_onescol_f[:], rhs=rcst[r][0:1, 0:384],
                start=True, stop=True),
                reads=[(f"{ph}_rcst", r), "ones_col_f"], writes=[ppbk], tag="rcbc")
            yield sc.op("vector", lambda e, r=r, ppb=ppb: e.tensor_add(
                out=tb1[r][:], in0=self.c_iota384[:], in1=ppb[:, 0:EXP]),
                reads=[ppbk, "iota384_bc"], writes=[(f"{ph}_tb1", r)], tag="tbl1")
            yield sc.op("vector", lambda e, r=r, ppb=ppb: e.tensor_add(
                out=tb2[r][:], in0=self.t_base[:], in1=ppb[:, 0:EXP]),
                reads=[ppbk, "t_base"], writes=[(f"{ph}_tb2", r), ppbk], tag="tbl2")
            for k in range(TOPK):
                yield sc.op("vector", lambda e, r=r, k=k, gk=gk: e.scalar_tensor_tensor(
                    out=gsf[r][:], in0=self.c_iotaE[:],
                    scalar=self.rec_e[:, gk + k:gk + k + 1], in1=tb1[r][:],
                    op0=ALU.is_equal, op1=ALU.mult),
                    reads=[("rec_e", g), (f"{ph}_tb1", r), "iotaE_bc"],
                    writes=[f"{ph}_gsf{r}"], tag="gslot")
                yield sc.op("vector", lambda e, r=r, k=k: e.reduce_sum(
                    out=rc6[r][:, k:k + 1], in_=gsf[r][:], axis=AX.X),
                    reads=[f"{ph}_gsf{r}"], writes=[(f"{ph}_rc6", r)], tag="gslot2r")
                yield sc.op("vector", lambda e, r=r, k=k, gk=gk: e.scalar_tensor_tensor(
                    out=gsf[r][:], in0=self.c_iotaE[:],
                    scalar=self.rec_e[:, gk + k:gk + k + 1], in1=tb2[r][:],
                    op0=ALU.is_equal, op1=ALU.mult),
                    reads=[("rec_e", g), (f"{ph}_tb2", r), "iotaE_bc"],
                    writes=[f"{ph}_gsf{r}"], tag="gbase")
                yield sc.op("vector", lambda e, r=r, k=k: e.reduce_sum(
                    out=pbt[r][:, k:k + 1], in_=gsf[r][:], axis=AX.X),
                    reads=[f"{ph}_gsf{r}"], writes=[f"{ph}_pb{r}"], tag="gbase2r")
            yield sc.op("vector", lambda e, r=r, gk=gk: e.tensor_add(
                out=pbt[r][:, 0:TOPK], in0=pbt[r][:, 0:TOPK],
                in1=self.rec_pos[:, gk:gk + TOPK]),
                reads=[f"{ph}_pb{r}", ("rec_pos", g)], writes=[f"{ph}_pb{r}"],
                tag="gpos2")
            yield sc.op("vector", lambda e, r=r: e.tensor_scalar(
                out=vmt[r][:, 0:TOPK], in0=pbt[r][:, 0:TOPK],
                scalar1=float(CAP), scalar2=None, op0=ALU.is_lt),
                reads=[f"{ph}_pb{r}"], writes=[f"{ph}_vm{r}"], tag="vmask")
            yield sc.op("vector", lambda e, r=r, gk=gk: e.tensor_mul(
                out=self.rec_w[:, gk:gk + TOPK], in0=vmt[r][:, 0:TOPK],
                in1=self.rec_w[:, gk:gk + TOPK]),
                reads=[f"{ph}_vm{r}", ("rec_w", g)], writes=[("rec_w", g)],
                tag="wfin")
            yield sc.op("vector", lambda e, r=r, gk=gk: e.tensor_add(
                out=slf[r][:, 0:TOPK], in0=rc6[r][:, 0:TOPK],
                in1=self.rec_pos[:, gk:gk + TOPK]),
                reads=[(f"{ph}_rc6", r), ("rec_pos", g)], writes=[f"{ph}_slf{r}"],
                tag="slotp")
            yield sc.op("vector", lambda e, r=r: e.tensor_scalar_add(
                out=slf[r][:, 0:TOPK], in0=slf[r][:, 0:TOPK],
                scalar1=-float(NSLOT)),
                reads=[f"{ph}_slf{r}"], writes=[f"{ph}_slf{r}"], tag="slotsub")
            yield sc.op("vector", lambda e, r=r: e.tensor_mul(
                out=slf[r][:, 0:TOPK], in0=slf[r][:, 0:TOPK],
                in1=vmt[r][:, 0:TOPK]),
                reads=[f"{ph}_slf{r}", f"{ph}_vm{r}"],
                writes=[f"{ph}_slf{r}"], tag="slotmul")
            yield sc.op("vector", lambda e, r=r: e.tensor_scalar_add(
                out=slf[r][:, 0:TOPK], in0=slf[r][:, 0:TOPK],
                scalar1=float(NSLOT)),
                reads=[f"{ph}_slf{r}"], writes=[f"{ph}_slf{r}"], tag="slotadd")
            yield sc.op("vector", lambda e, r=r, gk=gk: e.tensor_copy(
                out=self.rec_slot[:, gk:gk + TOPK], in_=slf[r][:, 0:TOPK]),
                reads=[f"{ph}_slf{r}"], writes=[("rec_slot", g)], tag="sloti")
            for k in range(TOPK):
                yield sc.op("gpsimd", lambda e, r=r, gk=gk, k=k: e.indirect_dma_start(
                    out=self.dsc["xs_loc"],
                    out_offset=IndirectOffsetOnAxis(
                        ap=self.rec_slot[:, gk + k:gk + k + 1], axis=0),
                    in_=xtk[r][:], in_offset=None),
                    reads=[f"{ph}_xtk{r}", ("rec_slot", g)],
                    writes=[("xs", call)], dma=True, tag="scat")

        for g0 in range(0, BL * NT, 2):
            gens = [passb_tile(g0), passb_tile(g0 + 1)]
            alive = [True, True]
            while any(alive):
                for j in range(2):
                    if alive[j]:
                        try:
                            next(gens[j])
                        except StopIteration:
                            alive[j] = False

        sc.op("gpsimd", lambda e: e.engine_nop(),
              reads=[("xs", call)], writes=[("xs_done", call)], tag="xsbar")

        self._emit_experts(call, ph, xstk, xsT, hidE, ytk,
                           gxe, gsc_, gub, w1b, w1o, w2b)

        def combine_tile(g):
            r = g % 2
            gk = g * TOPK
            row0 = g * 128
            for k in range(TOPK):
                rb = (g % 2) * 2 + k % 2
                yield sc.op("gpsimd", lambda e, rb=rb, gk=gk, k=k: e.indirect_dma_start(
                    out=gbuf[rb][:], out_offset=None,
                    in_=self.dsc["y_tab"],
                    in_offset=IndirectOffsetOnAxis(
                        ap=self.rec_slot[:, gk + k:gk + k + 1], axis=0)),
                    reads=[("ytab_done", call), ("rec_slot", g), "ytab_trash"],
                    writes=[(f"{ph}_gb", rb)], dma=True, tag="moegat")
                if k == 0:
                    yield sc.op("vector", lambda e, r=r, rb=rb, gk=gk, k=k:
                                e.tensor_scalar(
                                    out=macc[r][:], in0=gbuf[rb][:],
                                    scalar1=self.rec_w[:, gk + k:gk + k + 1],
                                    scalar2=None, op0=ALU.mult),
                                reads=[(f"{ph}_gb", rb), ("rec_w", g)],
                                writes=[f"{ph}_macc{r}"], tag="mgw")
                else:
                    yield sc.op("vector", lambda e, r=r, rb=rb, gk=gk, k=k:
                                e.scalar_tensor_tensor(
                                    out=macc[r][:], in0=gbuf[rb][:],
                                    scalar=self.rec_w[:, gk + k:gk + k + 1],
                                    in1=macc[r][:], op0=ALU.mult, op1=ALU.add),
                                reads=[(f"{ph}_gb", rb), ("rec_w", g),
                                       f"{ph}_macc{r}"],
                                writes=[f"{ph}_macc{r}"], tag="mgw")
            yield self.dma(xb[r][:], src_x[row0:row0 + 128, :],
                           reads=[src_key], writes=[f"{ph}_xb{r}"], tag="ldx2")
            yield self.dma(sht[g % 4][:], self.dsc["sh_tok"][row0:row0 + 128, :],
                           reads=[("shd", call, g)], writes=[(f"{ph}_sht", g % 4)],
                           tag="shld")
            yield sc.op("vector", lambda e, r=r, g=g: e.tensor_add(
                out=macc[r][:], in0=macc[r][:], in1=sht[g % 4][:]),
                reads=[f"{ph}_macc{r}", (f"{ph}_sht", g % 4)],
                writes=[f"{ph}_macc{r}"], tag="cmb1")
            yield sc.op("vector", lambda e, r=r: e.scalar_tensor_tensor(
                out=xb[r][:], in0=macc[r][:], scalar=0.5, in1=xb[r][:],
                op0=ALU.mult, op1=ALU.add),
                reads=[f"{ph}_macc{r}", f"{ph}_xb{r}"], writes=[f"{ph}_xb{r}"],
                tag="cmb2")
            yield self.dma(dst_x[row0:row0 + 128, :], xb[r][:],
                           reads=[f"{ph}_xb{r}"], writes=[dst_key], tag="stx2")

        for g0 in range(0, BL * NT, 2):
            gens = [combine_tile(g0), combine_tile(g0 + 1)]
            alive = [True, True]
            while any(alive):
                for j in range(2):
                    if alive[j]:
                        try:
                            next(gens[j])
                        except StopIteration:
                            alive[j] = False
        return

    def _shared_sw2_combine(self, call, ph, s, tc, hidS, hidS_all, sw2_d,
                            swb, sht, macc, xb, src_x, dst_x, src_key, dst_key):
        sc = self.sched
        NSF = SF // 128
        psb = [(self.ps_a, "ps_a"), (self.ps_b, "ps_b"),
               (self.ps_c, "ps_c"), (self.ps_d, "ps_d")]
        for hhf in range(2):
            wb = swb[hhf]
            wkey = f"{ph}_swc{hhf}"
            for grp in range(4):
                for q in range(4):
                    kf = grp * 4 + q
                    self.dma(wb[:, q * 512:(q + 1) * 512],
                             sw2_d[kf, :, hhf * 512:(hhf + 1) * 512],
                             reads=[("bwd", "bsw2T")], writes=[(wkey, q)],
                             tag="sw2ld")
                    for tt in range(4):
                        pp, ppk = psb[tt]
                        sc.op("tensor", lambda e, kf=kf, q=q, wb=wb, pp=pp, tt=tt:
                              e.matmul(
                                  out=pp[:],
                                  lhsT=hidS[:, kf * 512 + tt * 128:
                                            kf * 512 + (tt + 1) * 128],
                                  rhs=wb[:, q * 512:(q + 1) * 512],
                                  start=(kf == 0), stop=(kf == NSF - 1)),
                              reads=hidS_all + [(wkey, q)], writes=[ppk],
                              tag="sw2mm")
            for tt in range(4):
                g = s * NT + tc * 4 + tt
                r = g % 2
                pp, ppk = psb[tt]
                sc.op("vector", lambda e, g=g, hhf=hhf, pp=pp: e.tensor_copy(
                    out=sht[g % 4][:, hhf * 512:(hhf + 1) * 512], in_=pp[:]),
                    reads=[ppk], writes=[(f"{ph}_sht", g % 4), ppk], tag="shev")
        for tt in range(4):
            g = s * NT + tc * 4 + tt
            r = g % 2
            gk = g * TOPK
            row0 = g * 128
            om = sc.op("vector", lambda e, r=r: e.memset(macc[r][:], 0.0),
                       writes=[f"{ph}_macc{r}"], tag="macc0")
            for k in range(TOPK):
                sc.op("gpsimd", lambda e, r=r, gk=gk, k=k: e.indirect_dma_start(
                    out=macc[r][:], out_offset=None,
                    in_=self.dsc["y_tab"],
                    in_offset=IndirectOffsetOnAxis(
                        ap=self.rec_slot[:, gk + k:gk + k + 1], axis=0),
                    compute_op=ALU.add),
                    reads=[("ytab_done", call), ("rec_slot", g), "ytab_trash"],
                    writes=[f"{ph}_macc{r}"], deps=[om], dma=True, tag="moegat")
            self.dma(xb[r][:], src_x[row0:row0 + 128, :],
                     reads=[src_key], writes=[f"{ph}_xb{r}"], tag="ldx2")
            sc.op("vector", lambda e, r=r, g=g: e.tensor_add(
                out=macc[r][:], in0=macc[r][:], in1=sht[g % 4][:]),
                reads=[f"{ph}_macc{r}", (f"{ph}_sht", g % 4)],
                writes=[f"{ph}_macc{r}"], tag="cmb1")
            sc.op("vector", lambda e, r=r: e.scalar_tensor_tensor(
                out=xb[r][:], in0=macc[r][:], scalar=0.5, in1=xb[r][:],
                op0=ALU.mult, op1=ALU.add),
                reads=[f"{ph}_macc{r}", f"{ph}_xb{r}"], writes=[f"{ph}_xb{r}"],
                tag="cmb2")
            self.dma(dst_x[row0:row0 + 128, :], xb[r][:],
                     reads=[f"{ph}_xb{r}"], writes=[dst_key], tag="stx2")

    def emit_wcast(self):
        w1v = self.din["w1"].rearrange("e (a p) f -> e a p f", p=128)
        w2v = self.din["w2"].rearrange("e (a p) f -> e a p f", p=128)
        b1v = self.dsc["bw1"].rearrange("e p (a f) -> e a p f", a=DG)
        b2v = self.dsc["bw2"].rearrange("e p (a f) -> e a p f", a=FH // 128)
        for ex in range(EXP):
            gate = [("xntd", ex)] if ex >= 6 else []
            self.dma(b1v[ex], w1v[ex], reads=gate, writes=[("bw1", ex)],
                     q="gpsimd", tag="wcst1")
            self.dma(b2v[ex], w2v[ex], reads=gate, writes=[("bw2", ex)],
                     q="gpsimd", tag="wcst2")

    def _emit_experts(self, call, ph, xstk, xsT, hidE, ytk,
                      gxe, gsc_, gub, w1b, w1o, w2b):
        sc = self.sched
        bw1d = self.dsc["bw1"]
        bw2d = self.dsc["bw2"]
        xs_d = self.dsc["xs_loc"]
        NF = FH // 128
        shared_keys = ([f"{ph}_swb0", f"{ph}_swb1"]
                       + [(f"{ph}_swc{h}", q) for h in range(2) for q in range(4)])
        for ex in range(EXP):
            par = ex % 2
            wb1, wkp = (w1b, "w1b") if par == 0 else (w1o, "w1o")
            extra = []
            if ex == 1:
                extra = shared_keys
            elif ex == 0 and call == 0:
                extra = ["wc_b"]
            for half in range(2):
                self.dma(wb1[:, half * 2048:(half + 1) * 2048],
                         bw1d[ex, :, half * 2048:(half + 1) * 2048],
                         reads=[("bw1", ex)],
                         writes=[f"{ph}_{wkp}{a}" for a in range(half * 4, half * 4 + 4)]
                         + (extra if half == 0 else []), tag="w1lb")
            for half in range(2):
                self.dma(w2b[:, half * 2048:(half + 1) * 2048],
                         bw2d[ex, :, half * 2048:(half + 1) * 2048],
                         reads=[("bw2", ex)],
                         writes=[f"{ph}_w2b{a}" for a in range(half * 2, half * 2 + 2)],
                         tag="w2lb")
            for st in range(3):
                rr = st % 2
                self.dma(xstk[rr][:],
                         xs_d[ex * CAP + st * 128: ex * CAP + (st + 1) * 128, :],
                         reads=[("xs_done", call)], writes=[f"{ph}_xstk{rr}"],
                         tag="xsld")
                for gd in range(DG):
                    sc.op("tensor", lambda e, rr=rr, gd=gd: e.transpose(
                        out=self.ps_t2[:, 0:128],
                        in_=xstk[rr][:, gd * 128:(gd + 1) * 128],
                        identity=self.c_id[:]),
                        reads=[f"{ph}_xstk{rr}", "ident_b"], writes=["pt2"],
                        tag="xsT")
                    sc.op("vector", lambda e, st=st, gd=gd: e.tensor_copy(
                        out=xsT[:, gd * CAP + st * 128: gd * CAP + (st + 1) * 128],
                        in_=self.ps_t2[:, 0:128]),
                        reads=["pt2"], writes=[f"{ph}_xsT", "pt2"], tag="xsTev")
            for mf in range(NF):
                pp, ppk = (self.ps_a, "ps_a") if mf % 2 == 0 else (self.ps_b, "ps_b")
                for kd in range(DG):
                    sc.op("tensor", lambda e, mf=mf, kd=kd, pp=pp, wb1=wb1: e.matmul(
                        out=pp[:, 0:CAP],
                        lhsT=wb1[:, kd * FH + mf * 128: kd * FH + (mf + 1) * 128],
                        rhs=xsT[:, kd * CAP:(kd + 1) * CAP],
                        start=(kd == 0), stop=(kd == DG - 1)),
                        reads=[f"{ph}_xsT", f"{ph}_{wkp}{kd}"], writes=[ppk],
                        tag="w1mm")
                rr = mf % 2
                self._gelu(pp, ppk, CAP, gxe[rr], f"{ph}_gx{rr}", gsc_[rr],
                           f"{ph}_gs{rr}", gub[rr], f"{ph}_gu{rr}",
                           hidE[:, mf * CAP:(mf + 1) * CAP], f"{ph}_hidE")
            for st in range(3):
                rr = st % 2
                for hhf in range(2):
                    pp, ppk = (self.ps_c, "ps_c") if hhf == 0 else (self.ps_d, "ps_d")
                    for kf in range(NF):
                        sc.op("tensor", lambda e, st=st, hhf=hhf, kf=kf, pp=pp: e.matmul(
                            out=pp[:],
                            lhsT=hidE[:, kf * CAP + st * 128: kf * CAP + (st + 1) * 128],
                            rhs=w2b[:, kf * 1024 + hhf * 512: kf * 1024 + (hhf + 1) * 512],
                            start=(kf == 0), stop=(kf == NF - 1)),
                            reads=[f"{ph}_hidE", f"{ph}_w2b{kf}"], writes=[ppk],
                            tag="w2mm")
                    sc.op("vector", lambda e, rr=rr, hhf=hhf, pp=pp: e.tensor_copy(
                        out=ytk[rr][:, hhf * 512:(hhf + 1) * 512], in_=pp[:]),
                        reads=[ppk], writes=[f"{ph}_ytk{rr}", ppk], tag="yev")
                self.dma(self.dsc["y_tab"][ex * CAP + st * 128: ex * CAP + (st + 1) * 128, :],
                         ytk[rr][:], reads=[f"{ph}_ytk{rr}"], writes=[("ytab", call)],
                         tag="yst")

        sc.op("gpsimd", lambda e: e.engine_nop(),
              reads=[("ytab", call)], writes=[("ytab_done", call)], tag="ybar")

    def emit_mamba(self):
        sc = self.sched
        ph = "mam"
        A = self.arena
        resid = self.dsc["resid"]
        xn1_d = self.dsc["xn1_tok"]
        z_d = self.dsc["z_feat"]
        xnf_d = self.dsc["xn_feat"]

        xbm = [A("f", 1024, ph) for _ in range(2)]
        stat = [A("f", 8, ph) for _ in range(2)]
        spz = A("f", 512, ph)
        spa = A("f", 512, ph)
        ytot = A("f", DG * 1024, ph)
        hcar = A("f", 512, ph)
        sqm = A("b", 1024, ph)
        xn1t = [A("b", 1024, ph) for _ in range(2)]
        tri = A("b", 8192, ph)
        xm = A("b", DG * 1027, ph)
        xc = A("b", DG * 1024, ph)
        cacc = A("b", 1024, ph)
        spe = A("b", 512, ph)
        zstg = [A("b", 512, ph) for _ in range(2)]
        bmt = A("b", 1024, ph)
        cmt = A("b", 1024, ph)
        abuf = [A("b", 1024, ph) for _ in range(2)]
        ubuf = A("b", 1024, ph)
        hbuf = [A("b", 1024, ph) for _ in range(3)]
        wbuf = [A("b", 1024, ph) for _ in range(2)]
        xr = A("b", 4096, ph)
        tails = A("b", 24, ph)
        dtt = A("h", DG * 1024, ph)

        xn2t = tri[:, 0:1024]
        xnft = tri[:, 1024:2048]
        bbv = tri[:, 0:4096]
        cbv = tri[:, 4096:8192]
        ztv = [tri[:, 0:1024], tri[:, 1024:2048]]
        ymv = xm

        wst = [self.ar_w[:, 0:1024], self.ar_w[:, 1024:2048]]
        wdt = [self.ar_w[:, 2048:3072], self.ar_w[:, 3072:4096]]
        wbc = self.ar_w[:, 4096:4608]
        wcc = self.ar_w[:, 4608:5120]
        wout = [self.ar_w[:, 5120:6144], self.ar_w[:, 6144:7168]]
        wotk = [self.ar_w[:, 7168:8192], self.ar_w[:, 8192:9216]]

        bwin_d = self.dsc["bw_inT"].rearrange("(a p) c -> a p c", p=128)
        bwdt_d = self.dsc["bw_dtT"].rearrange("(a p) c -> a p c", p=128)
        bwout_d = self.dsc["bw_outT"].rearrange("(a p) c -> a p c", p=128)
        bwb_d = self.dsc["bw_bT"].rearrange("(a p) c -> a p c", p=128)
        bwc_d = self.dsc["bw_cT"].rearrange("(a p) c -> a p c", p=128)

        for a in range(DG):
            self.dma(wbc[:, a * NST:(a + 1) * NST], bwb_d[a, :, :],
                     reads=[("bwd", "bw_bT")], writes=["m_wbc"], tag="wbld")
            self.dma(wcc[:, a * NST:(a + 1) * NST], bwc_d[a, :, :],
                     reads=[("bwd", "bw_cT")], writes=["m_wcc"], tag="wcld2")

        TRI = "mam_tri"
        XMK = "mam_xm"
        SQK = "mam_sqm"

        for s in range(BL):
            for i in range(NT):
                g = s * NT + i
                r = g % 2
                row0 = g * 128
                kx = f"{ph}_xb{r}"
                self.dma(xbm[r][:], resid[row0:row0 + 128, :],
                         reads=["resid1"], writes=[kx], tag="mldx")
                self._ln_tile(xbm[r], kx, stat[r], f"{ph}_st{r}",
                              sqm, SQK, xn1t[r], f"{ph}_xn1{r}")
                self.dma(xn1_d[row0:row0 + 128, :], xn1t[r][:],
                         reads=[f"{ph}_xn1{r}"], writes=[("xn1d", g)], tag="stxn1")
                self._ln_tile(xn1t[r], f"{ph}_xn1{r}", stat[r], f"{ph}_st{r}",
                              sqm, SQK, xn2t, TRI)
                sc.op("vector", lambda e: e.tensor_mul(
                    out=xn2t[:], in0=xn2t[:], in1=self.c_lng[:]),
                    reads=[TRI, "lng_row"], writes=[TRI], tag="ln2g")
                sc.op("vector", lambda e: e.tensor_add(
                    out=xn2t[:], in0=xn2t[:], in1=self.c_lnb[:]),
                    reads=[TRI, "lnb_row"], writes=[TRI], tag="ln2b")
                for gd in range(DG):
                    sc.op("tensor", lambda e, gd=gd: e.transpose(
                        out=self.ps_t1[:, gd * 128:(gd + 1) * 128],
                        in_=xn2t[:, gd * 128:(gd + 1) * 128], identity=self.c_id[:]),
                        reads=[TRI, "ident_b"], writes=[("pt1", gd)], tag="x2T")
                sc.op("vector", lambda e: e.tensor_copy(out=xnft[:], in_=self.ps_t1[:]),
                      reads=[("pt1", gd) for gd in range(DG)],
                      writes=[TRI] + [("pt1", gd) for gd in range(DG)], tag="x2Tev")
                for gd in range(DG):
                    self.dma(xnf_d[s * D + gd * 128: s * D + (gd + 1) * 128,
                                   i * 128:(i + 1) * 128],
                             xnft[:, gd * 128:(gd + 1) * 128],
                             reads=[TRI], writes=[("x2fd", s, gd, i)], tag="stx2f")

            for hh in range(2):
                t0 = hh * TC
                if hh == 0:
                    sc.op("vector", lambda e: e.memset(tails[:], 0.0),
                          writes=["mam_tails"], tag="tail0")
                for nn in range(2):
                    for kd in range(DG):
                        self.dma(xr[:, kd * 512:(kd + 1) * 512],
                                 xnf_d[s * D + kd * 128: s * D + (kd + 1) * 128,
                                       t0 + nn * 512: t0 + (nn + 1) * 512],
                                 reads=[("x2fd", s, kd, q) for q in
                                        range(hh * 8 + nn * 4, hh * 8 + nn * 4 + 4)],
                                 writes=[("mam_xr", kd)], tag="xrld")
                    for me in range(16):
                        wreg = wst[me % 2]
                        wkey = f"mam_wst{me % 2}"
                        for a in range(DG):
                            self.dma(wreg[:, a * 128:(a + 1) * 128],
                                     bwin_d[a, :, me * 128:(me + 1) * 128],
                                     reads=[("bwd", "bw_inT")], writes=[wkey],
                                     tag="winld")
                        pp, ppk = (self.ps_a, "ps_a") if me % 2 == 0 else (self.ps_b, "ps_b")
                        for kd in range(DG):
                            sc.op("tensor", lambda e, kd=kd, wreg=wreg, pp=pp: e.matmul(
                                out=pp[:], lhsT=wreg[:, kd * 128:(kd + 1) * 128],
                                rhs=xr[:, kd * 512:(kd + 1) * 512],
                                start=(kd == 0), stop=(kd == DG - 1)),
                                reads=[("mam_xr", kd), wkey], writes=[ppk],
                                tag="winmm")
                        if me < 8:
                            sc.op("vector", lambda e, me=me, nn=nn, pp=pp: e.tensor_copy(
                                out=xm[:, me * 1027 + 3 + nn * 512:
                                       me * 1027 + 3 + (nn + 1) * 512],
                                in_=pp[:]),
                                reads=[ppk], writes=[XMK, ppk], tag="xmev")
                        else:
                            zr = me % 2
                            sc.op("vector", lambda e, zr=zr, pp=pp: e.tensor_copy(
                                out=zstg[zr][:], in_=pp[:]),
                                reads=[ppk], writes=[f"mam_zs{zr}", ppk], tag="zev")
                            self.dma(z_d[(me - 8) * 128:(me - 7) * 128,
                                         s * S + t0 + nn * 512: s * S + t0 + (nn + 1) * 512],
                                     zstg[zr][:],
                                     reads=[f"mam_zs{zr}"], writes=[("zd", s, hh)],
                                     tag="zst")
                if hh == 0:
                    sc.op("vector", lambda e: e.memset(
                        xm.rearrange("p (g c) -> p g c", g=DG)[:, :, 0:3], 0.0),
                        reads=[XMK], writes=[XMK], tag="convpad")
                else:
                    for gd in range(DG):
                        sc.op("vector", lambda e, gd=gd: e.tensor_copy(
                            out=xm[:, gd * 1027: gd * 1027 + 3],
                            in_=tails[:, gd * 3:(gd + 1) * 3]),
                            reads=["mam_tails"], writes=[XMK], tag="convcarry")
                for gd in range(DG):
                    base = gd * 1027
                    sc.op("vector", lambda e, base=base, gd=gd: e.tensor_scalar(
                        out=cacc[:], in0=xm[:, base:base + 1024],
                        scalar1=self.p_convw[:, 4 * gd:4 * gd + 1], scalar2=None,
                        op0=ALU.mult),
                        reads=[XMK, "p_convw"], writes=["mam_cacc"], tag="cv0")
                    for j in range(1, 4):
                        sc.op("vector", lambda e, base=base, gd=gd, j=j:
                              e.scalar_tensor_tensor(
                                  out=cacc[:], in0=xm[:, base + j:base + j + 1024],
                                  scalar=self.p_convw[:, 4 * gd + j:4 * gd + j + 1],
                                  in1=cacc[:], op0=ALU.mult, op1=ALU.add),
                              reads=[XMK, "p_convw", "mam_cacc"],
                              writes=["mam_cacc"], tag="cvj")
                    sc.op("vector", lambda e, base=base, gd=gd: e.tensor_copy(
                        out=tails[:, gd * 3:(gd + 1) * 3],
                        in_=xm[:, base + 1024: base + 1027]),
                        reads=[XMK], writes=["mam_tails"], tag="tailsv")
                    sc.op("scalar", lambda e, gd=gd: e.activation(
                        out=sqm[:], in_=cacc[:], func=AF.Sigmoid,
                        bias=self.p_convb[:, gd:gd + 1]),
                        reads=["mam_cacc", "p_convb"], writes=[SQK], tag="cvsig")
                    sc.op("vector", lambda e, gd=gd: e.scalar_tensor_tensor(
                        out=xc[:, gd * 1024:(gd + 1) * 1024], in0=cacc[:],
                        scalar=self.p_convb[:, gd:gd + 1], in1=sqm[:],
                        op0=ALU.add, op1=ALU.mult),
                        reads=["mam_cacc", "p_convb", SQK], writes=["mam_xc"],
                        tag="cvmul")
                for md in range(DG):
                    wreg = wdt[md % 2]
                    wkey = f"mam_wdt{md % 2}"
                    for a in range(DG):
                        self.dma(wreg[:, a * 128:(a + 1) * 128],
                                 bwdt_d[a, :, md * 128:(md + 1) * 128],
                                 reads=[("bwd", "bw_dtT")], writes=[wkey], tag="wdtld")
                    for nn in range(2):
                        pp, ppk = (self.ps_c, "ps_c") if nn == 0 else (self.ps_d, "ps_d")
                        for kd in range(DG):
                            sc.op("tensor", lambda e, kd=kd, wreg=wreg, pp=pp, nn=nn:
                                  e.matmul(out=pp[:],
                                           lhsT=wreg[:, kd * 128:(kd + 1) * 128],
                                           rhs=xc[:, kd * 1024 + nn * 512:
                                                  kd * 1024 + (nn + 1) * 512],
                                           start=(kd == 0), stop=(kd == DG - 1)),
                                  reads=["mam_xc", wkey], writes=[ppk], tag="wdtmm")
                        sc.op("vector", lambda e, md=md, pp=pp: e.tensor_scalar(
                            out=spz[:], in0=pp[:],
                            scalar1=self.p_bdt[:, md:md + 1], scalar2=None,
                            op0=ALU.add),
                            reads=[ppk, "p_bdt"], writes=["mam_spz", ppk], tag="spz")
                        sc.op("vector", lambda e: e.tensor_scalar_mul(
                            out=spa[:], in0=spz[:], scalar1=-1.0),
                            reads=["mam_spz"], writes=["mam_spa"], tag="spneg")
                        sc.op("vector", lambda e: e.tensor_max(
                            out=spa[:], in0=spa[:], in1=spz[:]),
                            reads=["mam_spa", "mam_spz"], writes=["mam_spa"],
                            tag="spabs")
                        sc.op("scalar", lambda e: e.activation(
                            out=spe[:], in_=spa[:], func=AF.Exp, scale=-1.0),
                            reads=["mam_spa"], writes=["mam_spe"], tag="spexp")
                        sc.op("scalar", lambda e: e.activation(
                            out=spa[:], in_=spe[:], func=AF.Ln,
                            bias=self.c_kcol[:, 1:2]),
                            reads=["mam_spe", "kcol"], writes=["mam_spa"],
                            tag="spln")
                        sc.op("vector", lambda e: e.tensor_relu(out=spz[:], in_=spz[:]),
                              reads=["mam_spz"], writes=["mam_spz"], tag="sprelu")
                        sc.op("vector", lambda e, md=md, nn=nn: e.tensor_add(
                            out=dtt[:, md * 1024 + nn * 512: md * 1024 + (nn + 1) * 512],
                            in0=spz[:], in1=spa[:]),
                            reads=["mam_spz", "mam_spa"], writes=["mam_dt"],
                            tag="spadd")
                for nn in range(2):
                    for kd in range(DG):
                        sc.op("tensor", lambda e, kd=kd, nn=nn: e.matmul(
                            out=self.ps_e[0:NST, :],
                            lhsT=wbc[:, kd * NST:(kd + 1) * NST],
                            rhs=xc[:, kd * 1024 + nn * 512: kd * 1024 + (nn + 1) * 512],
                            start=(kd == 0), stop=(kd == DG - 1)),
                            reads=["mam_xc", "m_wbc"], writes=["ps_e"], tag="wbmm")
                    sc.op("vector", lambda e, nn=nn: e.tensor_copy(
                        out=bmt[0:NST, nn * 512:(nn + 1) * 512],
                        in_=self.ps_e[0:NST, :]),
                        reads=["ps_e"], writes=["mam_bm", "ps_e"], tag="bmev")
                    for kd in range(DG):
                        sc.op("tensor", lambda e, kd=kd, nn=nn: e.matmul(
                            out=self.ps_f[0:NST, :],
                            lhsT=wcc[:, kd * NST:(kd + 1) * NST],
                            rhs=xc[:, kd * 1024 + nn * 512: kd * 1024 + (nn + 1) * 512],
                            start=(kd == 0), stop=(kd == DG - 1)),
                            reads=["mam_xc", "m_wcc"], writes=["ps_f"], tag="wcmm")
                    sc.op("vector", lambda e, nn=nn: e.tensor_copy(
                        out=cmt[0:NST, nn * 512:(nn + 1) * 512],
                        in_=self.ps_f[0:NST, :]),
                        reads=["ps_f"], writes=["mam_cm", "ps_f"], tag="cmev")

                if self.debug and s == 0 and hh == 0:
                    self.dma(self.dbg["d_xc"].rearrange("(g p) c -> p g c", p=128),
                             xc.rearrange("p (g c) -> p g c", g=DG),
                             reads=["mam_xc"], writes=[("dbg", "xc")])
                    self.dma(self.dbg["d_dt"].rearrange("(g p) c -> p g c", p=128),
                             dtt.rearrange("p (g c) -> p g c", g=DG),
                             reads=["mam_dt"], writes=[("dbg", "dt")])
                    self.dbg_dump("d_bm", bmt[0:NST, :], ["mam_bm"])
                for blk in range(NST // NBLK):
                    for q in range(NBLK):
                        n = blk * NBLK + q
                        for nn in range(2):
                            pp, ppk = (self.ps_e, "ps_e") if nn == 0 else (self.ps_f, "ps_f")
                            sc.op("tensor", lambda e, n=n, nn=nn, pp=pp: e.matmul(
                                out=pp[:],
                                lhsT=self.c_oneh[:, n * 128:(n + 1) * 128],
                                rhs=bmt[0:NST, nn * 512:(nn + 1) * 512],
                                start=True, stop=True),
                                reads=["mam_bm", "onehot_b"], writes=[ppk],
                                tag="bbmm")
                            sc.op("scalar", lambda e, q=q, nn=nn, pp=pp: e.activation(
                                out=bbv[:, q * 1024 + nn * 512: q * 1024 + (nn + 1) * 512],
                                in_=pp[:], func=AF.Copy),
                                reads=[ppk], writes=[TRI, ppk], tag="bbev")
                            sc.op("tensor", lambda e, n=n, nn=nn, pp=pp: e.matmul(
                                out=pp[:],
                                lhsT=self.c_oneh[:, n * 128:(n + 1) * 128],
                                rhs=cmt[0:NST, nn * 512:(nn + 1) * 512],
                                start=True, stop=True),
                                reads=["mam_cm", "onehot_b"], writes=[ppk],
                                tag="cbmm")
                            sc.op("scalar", lambda e, q=q, nn=nn, pp=pp: e.activation(
                                out=cbv[:, q * 1024 + nn * 512: q * 1024 + (nn + 1) * 512],
                                in_=pp[:], func=AF.Copy),
                                reads=[ppk], writes=[TRI, ppk], tag="cbev")
                    for gd in range(DG):
                        for q in range(NBLK):
                            n = blk * NBLK + q
                            ra = (gd * NBLK + q) % 2
                            rh = (gd * NBLK + q) % 3
                            sc.op("scalar", lambda e, gd=gd, n=n, ra=ra: e.activation(
                                out=abuf[ra][:], in_=dtt[:, gd * 1024:(gd + 1) * 1024],
                                func=AF.Exp, scale=self.p_A[:, n:n + 1]),
                                reads=["mam_dt", "p_A"], writes=[f"mam_ab{ra}"],
                                tag="aexp")
                            sc.op("vector", lambda e, gd=gd, q=q: e.tensor_mul(
                                out=ubuf[:], in0=xc[:, gd * 1024:(gd + 1) * 1024],
                                in1=bbv[:, q * 1024:(q + 1) * 1024]),
                                reads=["mam_xc", TRI], writes=["mam_ub"], tag="umul")
                            if hh == 0:
                                sc.op("vector", lambda e, ra=ra, rh=rh:
                                      e.tensor_tensor_scan(
                                          out=hbuf[rh][:], data0=abuf[ra][:],
                                          data1=ubuf[:], initial=0.0,
                                          op0=ALU.mult, op1=ALU.add),
                                      reads=[f"mam_ab{ra}", "mam_ub"],
                                      writes=[f"mam_hb{rh}"], tag="tts")
                                sc.op("vector", lambda e, gd=gd, n=n, rh=rh:
                                      e.tensor_copy(
                                          out=hcar[:, gd * NST + n: gd * NST + n + 1],
                                          in_=hbuf[rh][:, TC - 1:TC]),
                                      reads=[f"mam_hb{rh}"], writes=["mam_hc"],
                                      tag="hcar")
                            else:
                                ini = hcar[:, gd * NST + n: gd * NST + n + 1]
                                sc.op("vector", lambda e, ra=ra, rh=rh, ini=ini:
                                      e.tensor_tensor_scan(
                                          out=hbuf[rh][:], data0=abuf[ra][:],
                                          data1=ubuf[:], initial=ini,
                                          op0=ALU.mult, op1=ALU.add),
                                      reads=[f"mam_ab{ra}", "mam_ub", "mam_hc"],
                                      writes=[f"mam_hb{rh}"], tag="tts")
                            rw = rh % 2
                            sc.op("gpsimd", lambda e, rw=rw, rh=rh, q=q: e.tensor_mul(
                                out=wbuf[rw][:], in0=hbuf[rh][:],
                                in1=cbv[:, q * 1024:(q + 1) * 1024]),
                                reads=[f"mam_hb{rh}", TRI], writes=[f"mam_wb{rw}"],
                                tag="wmul")
                            if q == 0:
                                self.dma(ymv[:, gd * 1024:(gd + 1) * 1024],
                                         wbuf[rw][:],
                                         reads=[f"mam_wb{rw}"],
                                         writes=[("mam_yg", gd), XMK],
                                         q="gpsimd", tag="yg0")
                            else:
                                sc.op("gpsimd", lambda e, gd=gd, rh=rh: e.dma_start(
                                    out=ymv[:, gd * 1024:(gd + 1) * 1024],
                                    in_=wbuf[rw][:], accum_op=ALU.add),
                                    reads=[f"mam_wb{rw}", ("mam_yg", gd)],
                                    writes=[("mam_yg", gd), XMK], dma=True, tag="ygacc")
                    for gd in range(DG):
                        if blk == 0:
                            sc.op("vector", lambda e, gd=gd: e.tensor_copy(
                                out=ytot[:, gd * 1024:(gd + 1) * 1024],
                                in_=ymv[:, gd * 1024:(gd + 1) * 1024]),
                                reads=[("mam_yg", gd)], writes=[("mam_yt", gd)],
                                tag="ytcp")
                        else:
                            sc.op("vector", lambda e, gd=gd: e.tensor_add(
                                out=ytot[:, gd * 1024:(gd + 1) * 1024],
                                in0=ytot[:, gd * 1024:(gd + 1) * 1024],
                                in1=ymv[:, gd * 1024:(gd + 1) * 1024]),
                                reads=[("mam_yg", gd), ("mam_yt", gd)],
                                writes=[("mam_yt", gd)], tag="ytadd")

                if self.debug and s == 0 and hh == 0:
                    self.dma(self.dbg["d_ytot"].rearrange("(g p) c -> p g c", p=128),
                             ytot.rearrange("p (g c) -> p g c", g=DG),
                             reads=[("mam_yt", gd) for gd in range(DG)],
                             writes=[("dbg", "ytot")])
                for gd in range(DG):
                    sc.op("vector", lambda e, gd=gd: e.scalar_tensor_tensor(
                        out=ytot[:, gd * 1024:(gd + 1) * 1024],
                        in0=xc[:, gd * 1024:(gd + 1) * 1024],
                        scalar=self.p_dp[:, gd:gd + 1],
                        in1=ytot[:, gd * 1024:(gd + 1) * 1024],
                        op0=ALU.mult, op1=ALU.add),
                        reads=["mam_xc", "p_dp", ("mam_yt", gd)],
                        writes=[("mam_yt", gd)], tag="dpterm")
                for gd in range(DG):
                    rz = gd % 2
                    self.dma(ztv[rz][:],
                             z_d[gd * 128:(gd + 1) * 128, s * S + t0: s * S + t0 + TC],
                             reads=[("zd", s, hh)], writes=[f"mam_zt{rz}", TRI],
                             tag="zld")
                    sc.op("scalar", lambda e, rz=rz: e.activation(
                        out=sqm[:], in_=ztv[rz][:], func=AF.Sigmoid),
                        reads=[f"mam_zt{rz}"], writes=[SQK], tag="zsig")
                    sc.op("vector", lambda e, rz=rz: e.tensor_mul(
                        out=ztv[rz][:], in0=ztv[rz][:], in1=sqm[:]),
                        reads=[f"mam_zt{rz}", SQK], writes=[f"mam_zt{rz}"],
                        tag="zmul")
                    sc.op("vector", lambda e, gd=gd, rz=rz: e.tensor_mul(
                        out=ymv[:, gd * 1024:(gd + 1) * 1024],
                        in0=ytot[:, gd * 1024:(gd + 1) * 1024], in1=ztv[rz][:]),
                        reads=[("mam_yt", gd), f"mam_zt{rz}"],
                        writes=[("mam_ym", gd), ("mam_yg", gd), "mam_xm"],
                        tag="ygate")
                self._wout_combine(s, hh, ymv, wout, bwout_d, xbm, tri, resid,
                                   xn1_d)

    def _wout_combine(self, s, hh, ymv, wout, bwout_d, xbm, tri, resid, xn1_d):
        sc = self.sched
        xn1rv = [tri[:, 2048 + i * 1024: 2048 + (i + 1) * 1024] for i in range(2)]
        psb0 = [(self.ps_a, "ps_a"), (self.ps_b, "ps_b")]
        psb1 = [(self.ps_c, "ps_c"), (self.ps_d, "ps_d")]
        for ttg in range(4):
            tts = [ttg * 2, ttg * 2 + 1]
            for tt in tts:
                g = s * NT + hh * 8 + tt
                row0 = g * 128
                self.dma(xbm[tt % 2][:], resid[row0:row0 + 128, :],
                         reads=["resid1"], writes=[("mam_xbt", tt % 2)], tag="mldx2")
                self.dma(xn1rv[tt % 2][:], xn1_d[row0:row0 + 128, :],
                         reads=[("xn1d", g)], writes=[("mam_x1r", tt % 2), "mam_tri"],
                         tag="ldxn1")
            for hhf in range(2):
                wreg = wout[hhf]
                wkey = f"mam_wo{hhf}"
                psb = psb0 if hhf == 0 else psb1
                for grp in range(4):
                    for qq in range(2):
                        kd = grp * 2 + qq
                        self.dma(wreg[:, qq * 512:(qq + 1) * 512],
                                 bwout_d[kd, :, hhf * 512:(hhf + 1) * 512],
                                 reads=[("bwd", "bw_outT")], writes=[(wkey, qq)],
                                 tag="wold")
                        for ii, tt in enumerate(tts):
                            pp, ppk = psb[ii]
                            sc.op("tensor", lambda e, kd=kd, qq=qq, wreg=wreg,
                                  pp=pp, tt=tt: e.matmul(
                                      out=pp[:],
                                      lhsT=ymv[:, kd * 1024 + tt * 128:
                                               kd * 1024 + (tt + 1) * 128],
                                      rhs=wreg[:, qq * 512:(qq + 1) * 512],
                                      start=(kd == 0), stop=(kd == DG - 1)),
                                  reads=[("mam_ym", kd), "mam_xm", (wkey, qq)],
                                  writes=[ppk], tag="womm")
                for ii, tt in enumerate(tts):
                    pp, ppk = psb[ii]
                    sc.op("vector", lambda e, tt=tt, hhf=hhf, pp=pp:
                          e.tensor_add(
                              out=xbm[tt % 2][:, hhf * 512:(hhf + 1) * 512],
                              in0=xbm[tt % 2][:, hhf * 512:(hhf + 1) * 512],
                              in1=pp[:]),
                          reads=[("mam_xbt", tt % 2), ppk],
                          writes=[("mam_xbt", tt % 2), ppk], tag="mcmb1")
            for tt in tts:
                g = s * NT + hh * 8 + tt
                row0 = g * 128
                sc.op("vector", lambda e, tt=tt: e.tensor_add(
                    out=xbm[tt % 2][:], in0=xbm[tt % 2][:], in1=xn1rv[tt % 2][:]),
                    reads=[("mam_xbt", tt % 2), ("mam_x1r", tt % 2)],
                    writes=[("mam_xbt", tt % 2)], tag="mcmb2")
                self.dma(resid[row0:row0 + 128, :], xbm[tt % 2][:],
                         reads=[("mam_xbt", tt % 2)], writes=["resid1"], tag="mstx")

    def emit_finish(self):
        sc = self.sched
        for eng in ("vector", "gpsimd"):
            sc.op(eng, lambda e: e.engine_nop(), reads=["xout"], tag="fin")


if __name__ == "__main__":
    nc = build_program()
    print("built ok")


# version 17
# speedup vs baseline: 1.0796x; 1.0796x over previous
import os
import sys
import numpy as np
from contextlib import ExitStack

sys.path.insert(0, "/opt/trn_rl_repo")

import concourse.bass as bass
import concourse.mybir as mybir
from concourse.bass_utils import run_bass_kernel_spmd
from concourse.bass_core_rust import IndirectOffsetOnAxis

F32 = mybir.dt.float32
BF16 = mybir.dt.bfloat16
F16 = mybir.dt.float16
I32 = mybir.dt.int32
U32 = mybir.dt.uint32
AF = mybir.ActivationFunctionType
ALU = mybir.AluOpType
AX = mybir.AxisListType

NCORE = 8
B, S, D = 32, 2048, 1024
BL = B // NCORE
DG = D // 128
E2 = 2048
NST = 64
EXP, TOPK, CAP = 64, 6, 384
FH, SF = 512, 2048
TC = 1024
NT = S // 128
TLOC = BL * S
NSLOT = EXP * CAP
NBLK = 4
BIGF = 1.0e30
OORF = float(NSLOT)

GELU_A = 1.5957691216057308
GELU_B = 0.07135481627803054


class Res:
    __slots__ = ("ws", "readers")

    def __init__(self):
        self.ws = []
        self.readers = []


SMALL_TAGS = {
    "ln_sum", "ln_sq2", "ln_mu", "ln_musq", "ln_ex2", "ln_var", "ln_ln",
    "ln_rstd", "ln_apply",
    "sbadd", "max8", "matchrep", "ohsum", "wsc", "wsc2", "rseps", "rinv",
    "locev", "locpre", "rcupd", "idxcp", "maxidx", "gpos", "gpos2r", "gsco",
    "gsco2r", "wraw", "cc0", "cccp", "baseev", "basebcev",
    "gbase", "gbase2r", "gpos2", "vmask", "wfin", "slote", "slotp", "mgw",
    "slotsub", "slotmul", "slotadd", "sloti",
    "hcar", "tailsv", "convcarry", "tail0", "gslot", "gslot2r", "tbl1", "tbl2",
    "cntev", "rcallev",
    "spz", "spneg", "spabs", "spexp", "spln", "sprelu", "spadd",
    "ge_u", "ge_arg", "ge_sig", "ge_cp", "ge_sq",
    "Aexp", "Aneg", "Aevac", "zrow", "cvsig", "cvmul", "zsig", "zmul",
}


class Op:
    __slots__ = ("eng", "fn", "deps", "inc", "need_inc", "seq", "tag", "hard",
                 "sm", "pool")

    def __init__(self, eng, fn, deps, inc, tag="", hard=()):
        self.eng = eng
        self.fn = fn
        self.deps = deps
        self.inc = inc
        self.need_inc = False
        self.seq = 0
        self.tag = tag
        self.hard = list(hard)
        self.sm = tag in SMALL_TAGS
        self.pool = None


class Sched:
    ENGS = ("sync", "vector", "scalar", "tensor", "gpsimd")

    GP_RING = 6
    NPOOL = 16

    def __init__(self, nc, ctx):
        self.nc = nc
        self.ops = []
        self.per_eng = {e: [] for e in self.ENGS}
        self.res = {}
        self.sems = {e: ctx.enter_context(nc.semaphore(f"sem_{e}")) for e in self.ENGS}
        self.gp_dmas = []
        self.pools = {}
        self.pool_rr = {}
        for e in ("sync", "scalar", "gpsimd"):
            self.pools[e] = [ctx.enter_context(nc.semaphore(f"dp_{e}_{j}"))
                             for j in range(self.NPOOL)]
            self.pool_rr[e] = 0

    def R(self, key):
        r = self.res.get(key)
        if r is None:
            r = self.res[key] = Res()
        return r

    def op(self, eng, fn, reads=(), writes=(), deps=(), dma=False, hard=(), tag=""):
        dl = []
        for k in reads:
            dl.extend(self.R(k).ws)
        for k in writes:
            r = self.R(k)
            for w in r.ws:
                if not (dma and w.inc == 16):
                    dl.append(w)
            dl.extend(r.readers)
        dl.extend(deps)
        if dma and eng == "gpsimd" and len(self.gp_dmas) >= self.GP_RING:
            dl.append(self.gp_dmas[-self.GP_RING])
        o = Op(eng, fn, dl, 16 if dma else 1, tag, hard)
        if dma:
            j = self.pool_rr[eng]
            self.pool_rr[eng] = (j + 1) % self.NPOOL
            o.pool = (eng, j)
        if dma and eng == "gpsimd":
            self.gp_dmas.append(o)
        self.per_eng[eng].append(o)
        self.ops.append(o)
        for k in reads:
            self.R(k).readers.append(o)
        for k in writes:
            r = self.R(k)
            if r.readers:
                r.ws = [o]
                r.readers = []
            else:
                r.ws.append(o)
        return o

    def finalize(self):
        for o in self.ops:
            if o.inc == 16:
                o.need_inc = True
            for d in o.deps:
                if d.eng != o.eng or d.inc == 16 or d.sm or o.sm:
                    d.need_inc = True
            for d in o.hard:
                d.need_inc = True
        for e in self.ENGS:
            c = 0
            for o in self.per_eng[e]:
                if o.pool is None:
                    if o.need_inc:
                        c += o.inc
                    o.seq = c
        pc = {}
        for o in self.ops:
            if o.pool is not None:
                pc[o.pool] = pc.get(o.pool, 0) + 16
                o.seq = pc[o.pool]
        self.waits = {}
        for o in self.ops:
            w = {}
            for d in o.deps:
                if d.pool is not None:
                    key = d.pool
                elif d.eng == o.eng and not (d.sm or o.sm):
                    continue
                else:
                    key = d.eng
                w[key] = max(w.get(key, 0), d.seq)
            for d in o.hard:
                key = d.pool if d.pool is not None else d.eng
                w[key] = max(w.get(key, 0), d.seq)
            self.waits[id(o)] = w

    def replay(self, eng_name, eng_obj):
        sems = self.sems
        last = {}
        for o in self.per_eng[eng_name]:
            for key, val in self.waits[id(o)].items():
                if last.get(key, -1) < val:
                    h = self.pools[key[0]][key[1]] if isinstance(key, tuple) \
                        else sems[key]
                    eng_obj.wait_ge(h, val)
                    last[key] = val
            inst = o.fn(eng_obj)
            if o.need_inc:
                if inst is None:
                    raise RuntimeError(f"op {o.tag} needs inc but returned None")
                if o.pool is not None:
                    inst.then_inc(self.pools[o.pool[0]][o.pool[1]], 16)
                else:
                    inst.then_inc(sems[eng_name], o.inc)


def build_host_inputs(inp):
    import ml_dtypes
    bf = ml_dtypes.bfloat16
    f32 = np.float32
    x = np.ascontiguousarray(inp["L0_attn_x"], dtype=f32)
    shards = [np.ascontiguousarray(x[c * BL:(c + 1) * BL].reshape(TLOC, D))
              for c in range(NCORE)]

    w = {}
    w["w_inT"] = np.ascontiguousarray(inp["L0_attn_W_in"].T.astype(f32))
    w["w_dtT"] = np.ascontiguousarray(inp["L0_attn_W_dt"].T.astype(f32))
    w["w_outT"] = np.ascontiguousarray(inp["L0_attn_W_out"].T.astype(f32))
    w["w_bT"] = np.ascontiguousarray(inp["L0_attn_W_B"].T.astype(f32))
    w["w_cT"] = np.ascontiguousarray(inp["L0_attn_W_C"].T.astype(f32))
    w["routT"] = np.ascontiguousarray(inp["L0_ffn_router_weight"].T.astype(f32))
    w["sw1T"] = np.ascontiguousarray(inp["L0_ffn_sw1"].T.astype(f32))
    w["sw2T"] = np.ascontiguousarray(inp["L0_ffn_sw2"].T.astype(f32))
    w["w1"] = np.ascontiguousarray(inp["L0_ffn_w1"].astype(f32))
    w["w2"] = np.ascontiguousarray(inp["L0_ffn_w2"].astype(f32))
    w["conv_w"] = np.ascontiguousarray(inp["L0_attn_conv_w"].reshape(D, 4).astype(f32))
    w["conv_b"] = np.ascontiguousarray(inp["L0_attn_conv_b"].reshape(D, 1).astype(f32))
    w["b_dt"] = np.ascontiguousarray(inp["L0_attn_b_dt"].reshape(D, 1).astype(f32))
    w["dp"] = np.ascontiguousarray(inp["L0_attn_Dp"].reshape(D, 1).astype(f32))
    w["a_log"] = np.ascontiguousarray(inp["L0_attn_A_log"].reshape(1, NST).astype(f32))
    w["bias_bc"] = np.ascontiguousarray(np.broadcast_to(
        inp["L0_ffn_expert_bias"].astype(f32).reshape(1, EXP), (128, EXP)))
    w["lng_row"] = np.ascontiguousarray(np.broadcast_to(
        inp["L0_attn_ln_g"].reshape(1, D), (128, D)).astype(bf))
    w["lnb_row"] = np.ascontiguousarray(np.broadcast_to(
        inp["L0_attn_ln_b"].reshape(1, D), (128, D)).astype(bf))

    c = {}
    c["ones_col_b"] = np.ones((1, 128), bf)
    c["ones_col_f"] = np.ones((1, 128), f32)
    c["onessq_b"] = np.ones((128, 128), bf)
    c["ltstrict_b"] = np.ascontiguousarray(
        np.tril(np.ones((128, 128), f32), k=-1).T.astype(bf))
    c["ident_b"] = np.eye(128, dtype=bf)
    oh = np.zeros((EXP, NST, 128), f32)
    for n in range(NST):
        oh[n, n, :] = 1.0
    c["onehot_b"] = np.ascontiguousarray(oh.reshape(EXP, NST * 128)).astype(bf)
    c["iota384_bc"] = np.ascontiguousarray(np.broadcast_to(
        (np.arange(EXP, dtype=f32) * CAP).reshape(1, EXP), (128, EXP)))
    c["iotaE_bc"] = np.ascontiguousarray(np.broadcast_to(
        np.arange(EXP, dtype=f32).reshape(1, EXP), (128, EXP)))
    c["oor_bc"] = np.full((128, EXP), OORF, f32)
    c["lt128f"] = np.ascontiguousarray(
        np.tril(np.ones((128, 128), f32), k=-1).T)
    c["ones128f"] = np.ones((128, 1), f32)
    kc = np.zeros((128, 2), f32); kc[:, 0] = 1e-5; kc[:, 1] = 1.0
    c["kcol"] = kc
    rm = np.zeros((NCORE, NCORE, 1), f32)
    for cid in range(NCORE):
        rm[cid, :cid, 0] = 1.0

    maps = []
    for cid in range(NCORE):
        m = {"x_in": shards[cid], "rankmask": np.ascontiguousarray(rm[cid])}
        m.update(w)
        m.update(c)
        maps.append(m)
    return maps


def kernel(**inputs):
    maps = build_host_inputs(inputs)
    nc = build_program()
    res = run_bass_kernel_spmd(nc, maps, core_ids=list(range(NCORE)))
    out = np.concatenate([np.asarray(r["x_out"]).reshape(BL, S, D)
                          for r in res.results], axis=0)
    return out.astype(np.float32)


def build_program():
    nc = bass.Bass("TRN2", target_bir_lowering=False, debug=False)
    with ExitStack() as ctx:
        P = Prog(nc, ctx)
        P.declare()
        P.emit_setup()
        P.emit_ffn(0)
        P.dbg_dump("d_xffn1", P.dsc["resid"], ["resid1"])
        P.dbg_dump("d_xntok", P.dsc["xn_tok"], [("xntd", g) for g in range(BL * NT)])
        P.dbg_dump("d_ccout", P.dsc["cc_out"], [("ccout", 0)])
        P.dbg_dump("d_wtab", P.dsc["w_tab"], [("wtab", 0)])
        P.dbg_dump("d_xs", P.dsc["xs_loc"][0:512, :], [("xs", 0)])
        P.dbg_dump("d_ytab", P.dsc["y_tab"][0:512, :], [("ytab", 0)])
        if P.debug:
            P.dma(P.dbg["d_rece"], P.rec_e[:], reads=[("rec_e", g) for g in range(BL * NT)], writes=[("dbg", "rece")])
            P.dma(P.dbg["d_recp"], P.rec_pos[:], reads=[("rec_pos", g) for g in range(BL * NT)], writes=[("dbg", "recp")])
            P.dma(P.dbg["d_recw"], P.rec_w[:], reads=[("rec_w", g) for g in range(BL * NT)], writes=[("dbg", "recw")])
            P.dma(P.dbg["d_ytf"], P.dsc["y_tab"], reads=[("ytab", 0), "ytab_trash"], writes=[("dbg", "ytf")])
            P.dma(P.dbg["d_cnts"], P.dsc["cnts_d"], reads=["cntd"], writes=[("dbg", "cnts")])
            P.dma(P.dbg["d_slot"], P.rec_slot[:], reads=[("rec_slot", g) for g in range(BL * NT)], writes=[("dbg", "slot")])
        P.emit_mamba()
        P.dbg_dump("d_xmam", P.dsc["resid"], ["resid1"])
        P.dbg_dump("d_xn1", P.dsc["xn1_tok"], [("xn1d", g) for g in range(BL * NT)])
        P.dbg_dump("d_zf", P.dsc["z_feat"][:, 0:TC], [("zd", 0, 0)])
        P.emit_ffn(1)
        P.emit_finish()
        P.sched.finalize()
        block = ctx.enter_context(nc.Block())

        @block.sync
        def _(e):
            P.sched.replay("sync", e)

        @block.vector
        def _(e):
            P.sched.replay("vector", e)

        @block.scalar
        def _(e):
            P.sched.replay("scalar", e)

        @block.tensor
        def _(e):
            P.sched.replay("tensor", e)

        @block.gpsimd
        def _(e):
            P.sched.replay("gpsimd", e)
    return nc


class Prog:
    def __init__(self, nc, ctx):
        self.nc = nc
        self.ctx = ctx
        self.sched = Sched(nc, ctx)

    def declare(self):
        nc = self.nc
        ap = lambda t: t.ap()
        self.x_in = ap(nc.dram_tensor("x_in", [TLOC, D], F32, kind="ExternalInput"))
        self.x_out = ap(nc.dram_tensor("x_out", [TLOC, D], F32, kind="ExternalOutput"))

        self.din = {}
        for name, shape, dt in [
            ("w_inT", [D, E2], F32), ("w_dtT", [D, D], F32), ("w_outT", [D, D], F32),
            ("w_bT", [D, NST], F32), ("w_cT", [D, NST], F32), ("routT", [D, EXP], F32),
            ("sw1T", [D, SF], F32), ("sw2T", [SF, D], F32),
            ("w1", [EXP, D, FH], F32), ("w2", [EXP, FH, D], F32),
            ("conv_w", [D, 4], F32), ("conv_b", [D, 1], F32), ("b_dt", [D, 1], F32),
            ("dp", [D, 1], F32), ("a_log", [1, NST], F32), ("bias_bc", [128, EXP], F32),
            ("ones_col_f", [1, 128], F32), ("iota384_bc", [128, EXP], F32),
            ("iotaE_bc", [128, EXP], F32), ("oor_bc", [128, EXP], F32),
            ("rankmask", [NCORE, 1], F32), ("kcol", [128, 2], F32),
            ("lt128f", [128, 128], F32), ("ones128f", [128, 1], F32),
            ("ones_col_b", [1, 128], BF16), ("onessq_b", [128, 128], BF16),
            ("ltstrict_b", [128, 128], BF16), ("ident_b", [128, 128], BF16),
            ("onehot_b", [EXP, NST * 128], BF16),
            ("lng_row", [128, D], BF16), ("lnb_row", [128, D], BF16),
        ]:
            self.din[name] = ap(nc.dram_tensor(name, shape, dt, kind="ExternalInput"))

        dsc = {}
        for name, shape, dt in [
            ("resid", [TLOC, D], F32), ("xn_tok", [TLOC, D], BF16),
            ("xn1_tok", [TLOC, D], BF16), ("xn_feat", [BL * D, S], BF16),
            ("z_feat", [D, BL * S], BF16),
            ("xs_loc", [NSLOT + 1, D], BF16), ("w_tab", [NSLOT + 1, 1], F32),
            ("y_tab", [NSLOT + 1, D], BF16),
            ("bw_inT", [D, E2], BF16), ("bw_dtT", [D, D], BF16),
            ("bw_outT", [D, D], BF16), ("bw_bT", [D, NST], BF16),
            ("bw_cT", [D, NST], BF16), ("broutT", [D, EXP], BF16),
            ("bsw1T", [D, SF], BF16), ("bsw2T", [SF, D], BF16),
            ("bw1", [EXP, 128, DG * FH], BF16), ("bw2", [EXP, 128, (FH // 128) * D], BF16),
            ("cc_in", [1, 128], F32), ("cc_out", [NCORE, 128], F32),
            ("sh_tok", [TLOC, D], BF16), ("cnts_d", [BL * NT, EXP], F32),
        ]:
            dsc[name] = ap(nc.dram_tensor(name, shape, dt))
        self.dsc = dsc
        self.debug = os.environ.get("BK_DEBUG", "") == "1"
        self.dbg = {}
        if self.debug:
            for name, shape, dt in [
                ("d_xffn1", [TLOC, D], F32), ("d_xmam", [TLOC, D], F32),
                ("d_xntok", [TLOC, D], BF16), ("d_ccout", [NCORE, 128], F32),
                ("d_wtab", [NSLOT + 1, 1], F32), ("d_xs", [512, D], BF16),
                ("d_ytab", [512, D], BF16), ("d_rece", [128, BL * NT * TOPK], F32),
                ("d_recp", [128, BL * NT * TOPK], F32),
                ("d_recw", [128, BL * NT * TOPK], F32),
                ("d_xc", [D, TC], BF16), ("d_dt", [D, TC], F16),
                ("d_bm", [NST, TC], BF16), ("d_ytot", [D, TC], F32),
                ("d_xn1", [TLOC, D], BF16), ("d_zf", [D, TC], BF16),
                ("d_ytf", [NSLOT + 1, D], BF16),
                ("d_slot", [128, BL * NT * TOPK], I32),
                ("d_cnts", [BL * NT, EXP], F32), ("d_rc", [64, EXP], F32),
            ]:
                self.dbg[name] = ap(nc.dram_tensor(name, shape, dt, kind="ExternalOutput"))

        sb = lambda n, s, d: self.ctx.enter_context(nc.sbuf_tensor(n, s, d))
        self.c_onescol_b = sb("c_onescol_b", [1, 128], BF16)
        self.c_onescol_f = sb("c_onescol_f", [1, 128], F32)
        self.c_onessq = sb("c_onessq", [128, 128], BF16)
        self.c_lt = sb("c_lt", [128, 128], BF16)
        self.c_id = sb("c_id", [128, 128], BF16)
        self.c_iota384 = sb("c_iota384", [128, EXP], F32)
        self.c_iotaE = sb("c_iotaE", [128, EXP], F32)
        self.c_oor = sb("c_oor", [128, EXP], F32)
        self.c_bias = sb("c_bias", [128, EXP], F32)
        self.c_rank = sb("c_rank", [NCORE, 1], F32)
        self.c_kcol = sb("c_kcol", [128, 2], F32)
        self.c_ltf = sb("c_ltf", [128, 128], F32)
        self.c_ones128f = sb("c_ones128f", [128, 1], F32)
        self.c_lng = sb("c_lng", [128, D], BF16)
        self.c_lnb = sb("c_lnb", [128, D], BF16)
        self.c_oneh = sb("c_oneh", [EXP, NST * 128], BF16)
        self.p_convw = sb("p_convw", [128, 4 * DG], F32)
        self.p_convb = sb("p_convb", [128, DG], F32)
        self.p_bdt = sb("p_bdt", [128, DG], F32)
        self.p_dp = sb("p_dp", [128, DG], F32)
        self.p_A = sb("p_A", [128, NST], F32)
        self.wr_rt = sb("wr_rt", [128, DG * EXP], BF16)
        self.t_base = sb("t_base", [128, EXP], F32)
        self.row_a = sb("row_a", [1, 128], F32)
        self.row_b = sb("row_b", [1, 128], F32)
        self.row_c = sb("row_c", [NCORE, 128], F32)
        self.rec_e = sb("rec_e", [128, BL * NT * TOPK], F32)
        self.rec_pos = sb("rec_pos", [128, BL * NT * TOPK], F32)
        self.rec_w = sb("rec_w", [128, BL * NT * TOPK], F32)
        self.rec_slot = sb("rec_slot", [128, BL * NT * TOPK], I32)
        self.idx8u = [sb("idx8u0", [128, 8], U32), sb("idx8u1", [128, 8], U32)]
        self.ar_b = sb("ar_b", [128, 45216], BF16)
        self.ar_f = sb("ar_f", [128, 11792], F32)
        self.ar_h = sb("ar_h", [128, 8192], F16)
        self.ar_w = sb("ar_w", [128, 12288], BF16)
        ps = lambda n, s, d=F32: self.ctx.enter_context(nc.psum_tensor(n, s, d))
        self.ps_a = ps("ps_a", [128, 512])
        self.ps_b = ps("ps_b", [128, 512])
        self.ps_c = ps("ps_c", [128, 512])
        self.ps_d = ps("ps_d", [128, 512])
        self.ps_e = ps("ps_e", [128, 512])
        self.ps_f = ps("ps_f", [128, 512])
        self.ps_t1 = ps("ps_t1", [128, 1024], BF16)
        self.ps_t2 = ps("ps_t2", [128, 1024], BF16)
        self._cur = {}

    def arena(self, which, cols, phase):
        key = (which, phase)
        cur = self._cur.get(key, 0)
        t = {"b": self.ar_b, "f": self.ar_f, "h": self.ar_h, "w": self.ar_w}[which]
        if cur + cols > t.shape[1]:
            raise RuntimeError(f"arena {which} overflow in {phase}: {cur}+{cols}>{t.shape[1]}")
        self._cur[key] = cur + cols
        return t[:, cur:cur + cols]

    def dma(self, out, in_, reads=(), writes=(), deps=(), q="sync", tag=""):
        return self.sched.op(q, lambda e: e.dma_start(out=out, in_=in_),
                             reads=reads, writes=writes, deps=deps, dma=True, tag=tag)

    def dbg_dump(self, name, src_ap, reads, tag="dbg"):
        if not self.debug:
            return
        self.dma(self.dbg[name], src_ap, reads=reads, writes=[("dbg", name)],
                 tag=tag)

    def emit_setup(self):
        sc = self.sched
        din = self.din
        for t, key in [
            (self.c_onescol_b, "ones_col_b"), (self.c_onescol_f, "ones_col_f"),
            (self.c_onessq, "onessq_b"), (self.c_lt, "ltstrict_b"),
            (self.c_id, "ident_b"), (self.c_iota384, "iota384_bc"),
            (self.c_iotaE, "iotaE_bc"), (self.c_oor, "oor_bc"),
            (self.c_bias, "bias_bc"), (self.c_rank, "rankmask"),
            (self.c_lng, "lng_row"), (self.c_lnb, "lnb_row"),
            (self.c_oneh, "onehot_b"), (self.c_kcol, "kcol"),
            (self.c_ltf, "lt128f"), (self.c_ones128f, "ones128f"),
        ]:
            self.dma(t[:], din[key], writes=[key], tag="cld")
        for g in range(DG):
            self.dma(self.p_convw[:, 4 * g:4 * g + 4],
                     din["conv_w"][g * 128:(g + 1) * 128, :], writes=["p_convw"])
            self.dma(self.p_convb[:, g:g + 1],
                     din["conv_b"][g * 128:(g + 1) * 128, :], writes=["p_convb"])
            self.dma(self.p_bdt[:, g:g + 1],
                     din["b_dt"][g * 128:(g + 1) * 128, :], writes=["p_bdt"])
            self.dma(self.p_dp[:, g:g + 1],
                     din["dp"][g * 128:(g + 1) * 128, :], writes=["p_dp"])
        self.dma(self.row_a[:, 0:NST], din["a_log"], writes=["row_a"])
        sc.op("scalar", lambda e: e.activation(out=self.row_b[:, 0:NST],
                                               in_=self.row_a[:, 0:NST], func=AF.Exp),
              reads=["row_a"], writes=["row_b"], tag="Aexp")
        sc.op("vector", lambda e: e.tensor_scalar_mul(out=self.row_b[:, 0:NST],
                                                      in0=self.row_b[:, 0:NST],
                                                      scalar1=-1.0),
              reads=["row_b"], writes=["row_b"], tag="Aneg")
        sc.op("tensor", lambda e: e.matmul(out=self.ps_a[:, 0:NST],
                                           lhsT=self.c_onescol_f[:],
                                           rhs=self.row_b[:, 0:NST],
                                           start=True, stop=True),
              reads=["row_b", "ones_col_f"], writes=["ps_a"], tag="Abc")
        fA = sc.op("tensor", lambda e: e.matmul(out=self.ps_b[:, 0:NST],
                                                lhsT=self.c_onescol_f[:],
                                                rhs=self.row_b[:, 0:NST],
                                                start=True, stop=True),
                   reads=["row_b", "ones_col_f"], writes=["ps_b"], tag="Abc2")
        sc.op("vector", lambda e: e.tensor_copy(out=self.p_A[:], in_=self.ps_a[:, 0:NST]),
              reads=["ps_a"], writes=["p_A", "ps_a", "ps_b"], deps=[fA], tag="Aevac")

        stage_f = self.arena("f", 4096, "setup")
        stage_b = self.arena("w", 4096, "setup")
        for (src, dst, R, C) in [
            ("w_inT", "bw_inT", D, E2), ("w_dtT", "bw_dtT", D, D),
            ("w_outT", "bw_outT", D, D), ("w_bT", "bw_bT", D, NST),
            ("w_cT", "bw_cT", D, NST), ("routT", "broutT", D, EXP),
            ("sw1T", "bsw1T", D, SF), ("sw2T", "bsw2T", SF, D),
        ]:
            sa = self.din[src].rearrange("(a p) c -> a p c", p=128)
            da = self.dsc[dst].rearrange("(a p) c -> a p c", p=128)
            for a in range(R // 128):
                step = min(4096, C)
                for c0 in range(0, C, step):
                    n = min(step, C - c0)
                    self.dma(stage_f[:, 0:n], sa[a, :, c0:c0 + n],
                             writes=["wc_f"], tag="wcld")
                    sc.op("vector", lambda e, n=n: e.tensor_copy(
                        out=stage_b[:, 0:n], in_=stage_f[:, 0:n]),
                        reads=["wc_f"], writes=["wc_b"], tag="wccast")
                    self.dma(da[a, :, c0:c0 + n], stage_b[:, 0:n],
                             reads=["wc_b"], writes=[("bwd", dst)], tag="wcst")
        zrow = self.arena("b", 1024, "setup")
        sc.op("vector", lambda e: e.memset(zrow[0:1, :], 0.0),
              writes=["zrow"], tag="zrow")
        self.dma(self.dsc["y_tab"][NSLOT:NSLOT + 1, :], zrow[0:1, :],
                 reads=["zrow"], writes=["ytab_trash"], tag="ztrash")
        rt8 = self.dsc["broutT"].rearrange("(a p) c -> a p c", p=128)
        for a in range(DG):
            self.dma(self.wr_rt[:, a * EXP:(a + 1) * EXP], rt8[a, :, :],
                     reads=[("bwd", "broutT")], writes=["wr_rt"], tag="rtld")

    def _ln_tile(self, xin, kx, st, kst, sq, ksq, xout, kxo):
        for _ in self._ln_tile_gen(xin, kx, st, kst, sq, ksq, xout, kxo):
            pass

    def _ln_tile_gen(self, xin, kx, st, kst, sq, ksq, xout, kxo):
        sc = self.sched
        yield sc.op("vector", lambda e: e.reduce_sum(out=st[:, 0:1], in_=xin[:], axis=AX.X),
              reads=[kx], writes=[kst], tag="ln_sum")
        yield sc.op("vector", lambda e: e.tensor_mul(out=sq[:], in0=xin[:], in1=xin[:]),
            reads=[kx], writes=[ksq], tag="ln_sq")
        yield sc.op("vector", lambda e: e.reduce_sum(out=st[:, 1:2], in_=sq[:], axis=AX.X),
            reads=[ksq], writes=[kst], tag="ln_sq2")
        yield sc.op("vector", lambda e: e.tensor_scalar_mul(
            out=st[:, 2:3], in0=st[:, 0:1], scalar1=-1.0 / D),
            reads=[kst], writes=[kst], tag="ln_mu")
        yield sc.op("vector", lambda e: e.scalar_tensor_tensor(
            out=st[:, 3:4], in0=st[:, 2:3], scalar=1.0, in1=st[:, 2:3],
            op0=ALU.mult, op1=ALU.mult),
            reads=[kst], writes=[kst], tag="ln_musq")
        yield sc.op("vector", lambda e: e.tensor_scalar(
            out=st[:, 1:2], in0=st[:, 1:2], scalar1=1.0 / D, scalar2=None,
            op0=ALU.mult),
            reads=[kst], writes=[kst], tag="ln_ex2")
        yield sc.op("vector", lambda e: e.tensor_sub(
            out=st[:, 1:2], in0=st[:, 1:2], in1=st[:, 3:4]),
            reads=[kst], writes=[kst], tag="ln_var")
        yield sc.op("scalar", lambda e: e.activation(
            out=st[:, 4:5], in_=st[:, 1:2], func=AF.Ln,
            bias=self.c_kcol[:, 0:1]),
            reads=[kst, "kcol"], writes=[kst], tag="ln_ln")
        yield sc.op("scalar", lambda e: e.activation(
            out=st[:, 5:6], in_=st[:, 4:5], func=AF.Exp, scale=-0.5),
            reads=[kst], writes=[kst], tag="ln_rstd")
        yield sc.op("vector", lambda e: e.tensor_scalar(
            out=xout[:], in0=xin[:], scalar1=st[:, 2:3], scalar2=st[:, 5:6],
            op0=ALU.add, op1=ALU.mult),
            reads=[kx, kst], writes=[kxo], tag="ln_apply")

    def _gelu(self, pp, ppk, cols, gxe, kgx, gsc_, kgs, gu, kgu, outap, kout):
        sc = self.sched
        sc.op("scalar", lambda e: e.activation(out=gxe[:, 0:cols], in_=pp[:, 0:cols],
                                               func=AF.Copy),
              reads=[ppk], writes=[kgx], tag="ge_cp")
        sc.op("scalar", lambda e: e.activation(out=gsc_[:, 0:cols], in_=pp[:, 0:cols],
                                               func=AF.Square),
              reads=[ppk], writes=[kgs, ppk], tag="ge_sq")
        sc.op("vector", lambda e: e.tensor_scalar(
            out=gu[:, 0:cols], in0=gsc_[:, 0:cols], scalar1=GELU_B, scalar2=GELU_A,
            op0=ALU.mult, op1=ALU.add),
            reads=[kgs], writes=[kgu], tag="ge_u")
        sc.op("vector", lambda e: e.tensor_mul(
            out=gu[:, 0:cols], in0=gu[:, 0:cols], in1=gxe[:, 0:cols]),
            reads=[kgu, kgx], writes=[kgu], tag="ge_arg")
        sc.op("scalar", lambda e: e.activation(out=gu[:, 0:cols], in_=gu[:, 0:cols],
                                               func=AF.Sigmoid),
              reads=[kgu], writes=[kgu], tag="ge_sig")
        sc.op("vector", lambda e: e.tensor_mul(
            out=outap, in0=gu[:, 0:cols], in1=gxe[:, 0:cols]),
            reads=[kgu, kgx], writes=[kout], tag="ge_mul")

    def emit_ffn(self, call):
        sc = self.sched
        ph = f"ffn{call}"
        A = self.arena
        src_x = self.x_in if call == 0 else self.dsc["resid"]
        dst_x = self.dsc["resid"] if call == 0 else self.x_out
        src_key = "x_in" if call == 0 else "resid1"
        dst_key = "resid1" if call == 0 else "xout"

        xb = [A("f", 1024, ph) for _ in range(2)]
        stat = [A("f", 8, ph) for _ in range(2)]
        sbt = [A("f", 66, ph) for _ in range(2)]
        sct = [A("f", 64, ph) for _ in range(2)]
        sbm = [A("f", 66, ph) for _ in range(2)]
        m8t = [A("f", 8, ph) for _ in range(2)]
        scot = [A("f", 8, ph) for _ in range(2)]
        idx8 = self.idx8u
        pbt = [A("f", 8, ph) for _ in range(2)]
        vmt = [A("f", 8, ph) for _ in range(2)]
        slf = [A("f", 8, ph) for _ in range(2)]
        wkc = [A("f", 8, ph) for _ in range(2)]
        ccs = A("f", 128, ph)
        gsc_ = [A("f", 512, ph) for _ in range(2)]
        gsf = [A("f", 64, ph) for _ in range(2)]

        sqs = [A("b", 1024, ph) for _ in range(2)]
        xnt = [A("b", 1024, ph) for _ in range(2)]
        xnf = [A("b", 1024, ph) for _ in range(2)]
        oht = [A("f", 64, ph) for _ in range(2)]
        wsct = [A("f", 64, ph) for _ in range(2)]
        xtk = [A("b", 1024, ph) for _ in range(2)]
        xstk = [A("b", 1024, ph) for _ in range(2)]
        xsT = A("b", 3072, ph)
        hidE = A("b", 1536, ph)
        ytk = [A("b", 1024, ph) for _ in range(2)]
        gxe = [A("b", 512, ph) for _ in range(2)]
        gub = [A("b", 512, ph) for _ in range(2)]
        xnfr = A("b", 4096, ph)
        hidS = A("b", 8192, ph)
        sht = [A("b", 1024, ph) for _ in range(4)]
        macc = [A("f", 1024, ph) for _ in range(2)]
        gbuf = [A("b", 1024, ph) for _ in range(4)]
        w1b = self.ar_w[:, 0:4096]
        w2b = self.ar_w[:, 4096:8192]
        sw1b = [self.ar_w[:, 8192:10240], self.ar_w[:, 10240:12288]]
        w1o = self.ar_w[:, 8192:12288]

        xn_tok_d = self.dsc["xn_tok"]
        xn_feat_d = self.dsc["xn_feat"]

        sc.op("vector", lambda e: e.memset(sbt[0][:, 64:66], BIGF),
              writes=[f"{ph}_sbp0"], tag="sbpad")
        sc.op("vector", lambda e: e.memset(sbt[1][:, 64:66], BIGF),
              writes=[f"{ph}_sbp1"], tag="sbpad")

        loc_all = A("h", BL * NT * EXP, ph)
        cnts_row = A("f", BL * NT * EXP, ph)
        rcst = [A("f", 384, ph) for _ in range(2)]
        tb1 = [A("f", 64, ph) for _ in range(2)]
        tb2 = [A("f", 64, ph) for _ in range(2)]
        rc6 = [A("f", 8, ph) for _ in range(2)]

        def passa_tile(g):
            s, i = g // NT, g % NT
            r = g % 2
            gk = g * TOPK
            row0 = g * 128
            kx = f"{ph}_xb{r}"
            yield self.dma(xb[r][:], src_x[row0:row0 + 128, :],
                           reads=[src_key], writes=[kx], tag="ldx")
            for _o in self._ln_tile_gen(xb[r], kx, stat[r], f"{ph}_st{r}",
                                        sqs[r], f"{ph}_sq{r}", xnt[r], f"{ph}_xnt{r}"):
                yield _o
            yield self.dma(xn_tok_d[row0:row0 + 128, :], xnt[r][:],
                           reads=[f"{ph}_xnt{r}"], writes=[("xntd", g)], tag="stxn")
            for gd in range(DG):
                yield sc.op("tensor", lambda e, r=r, gd=gd: e.transpose(
                    out=self.ps_t1[:, gd * 128:(gd + 1) * 128] if r == 0 else
                        self.ps_t2[:, gd * 128:(gd + 1) * 128],
                    in_=xnt[r][:, gd * 128:(gd + 1) * 128], identity=self.c_id[:]),
                    reads=[f"{ph}_xnt{r}", "ident_b"], writes=[("pt", r, gd)],
                    tag="xT")
            yield sc.op("vector", lambda e, r=r: e.tensor_copy(
                out=xnf[r][:], in_=self.ps_t1[:] if r == 0 else self.ps_t2[:]),
                reads=[("pt", r, gd) for gd in range(DG)],
                writes=[f"{ph}_xnf{r}"] + [("pt", r, gd) for gd in range(DG)],
                tag="xTev")
            for gd in range(DG):
                yield self.dma(xn_feat_d[s * D + gd * 128: s * D + (gd + 1) * 128,
                                         i * 128:(i + 1) * 128],
                               xnf[r][:, gd * 128:(gd + 1) * 128],
                               reads=[f"{ph}_xnf{r}"], writes=[("xnfd", s, gd, i)],
                               tag="stxf")
            ppl, pplk = (self.ps_a, "ps_a") if r == 0 else (self.ps_b, "ps_b")
            for kd in range(DG):
                yield sc.op("tensor", lambda e, r=r, kd=kd, ppl=ppl: e.matmul(
                    out=ppl[:, 0:EXP],
                    lhsT=xnf[r][:, kd * 128:(kd + 1) * 128],
                    rhs=self.wr_rt[:, kd * EXP:(kd + 1) * EXP],
                    start=(kd == 0), stop=(kd == DG - 1)),
                    reads=[f"{ph}_xnf{r}", "wr_rt"], writes=[pplk], tag="rtmm")
            yield sc.op("scalar", lambda e, r=r, ppl=ppl: e.activation(
                out=sct[r][:], in_=ppl[:, 0:EXP], func=AF.Sigmoid),
                reads=[pplk], writes=[f"{ph}_sc{r}", pplk], tag="sigm")
            o_sb = sc.op("vector", lambda e, r=r: e.tensor_add(
                out=sbt[r][:, 0:EXP], in0=sct[r][:], in1=self.c_bias[:]),
                reads=[f"{ph}_sc{r}", "bias_bc", f"{ph}_sbp{r}"],
                writes=[f"{ph}_sbt{r}"], tag="sbadd")
            yield o_sb
            o_m8 = sc.op("vector", lambda e, r=r: e.max(
                out=m8t[r][:], in_=sbt[r][:]),
                reads=[f"{ph}_sbt{r}"], writes=[f"{ph}_m8{r}"], hard=[o_sb],
                tag="max8")
            yield o_m8
            o_mr = sc.op("vector", lambda e, r=r: e.match_replace(
                out=sbm[r][:], in_to_replace=m8t[r][:], in_values=sbt[r][:],
                imm_value=-BIGF),
                reads=[f"{ph}_m8{r}", f"{ph}_sbt{r}"], writes=[f"{ph}_sbm{r}"],
                hard=[o_m8], tag="matchrep")
            yield o_mr
            o_oh = sc.op("vector", lambda e, r=r: e.tensor_scalar(
                out=oht[r][:], in0=sbm[r][:, 0:EXP], scalar1=-BIGF, scalar2=None,
                op0=ALU.is_equal),
                reads=[f"{ph}_sbm{r}"], writes=[f"{ph}_oh{r}"], hard=[o_mr],
                tag="ohsum")
            yield o_oh
            yield sc.op("vector", lambda e, r=r: e.tensor_mul(
                out=wsct[r][:], in0=oht[r][:], in1=sct[r][:]),
                reads=[f"{ph}_oh{r}", f"{ph}_sc{r}"],
                writes=[f"{ph}_wsc{r}"], tag="wsc")
            yield sc.op("vector", lambda e, r=r: e.reduce_sum(
                out=stat[r][:, 6:7], in_=wsct[r][:], axis=AX.X),
                reads=[f"{ph}_wsc{r}"], writes=[f"{ph}_st{r}"], tag="wsc2")
            yield sc.op("vector", lambda e, r=r: e.tensor_scalar_add(
                out=stat[r][:, 6:7], in0=stat[r][:, 6:7], scalar1=1e-20),
                reads=[f"{ph}_st{r}"], writes=[f"{ph}_st{r}"], tag="rseps")
            yield sc.op("vector", lambda e, r=r: e.reciprocal(
                out=stat[r][:, 7:8], in_=stat[r][:, 6:7]),
                reads=[f"{ph}_st{r}"], writes=[f"{ph}_st{r}"], tag="rinv")
            ppc, ppck = (self.ps_c, "ps_c") if r == 0 else (self.ps_d, "ps_d")
            yield sc.op("tensor", lambda e, r=r, ppc=ppc: e.matmul(
                out=ppc[:, 0:EXP], lhsT=self.c_ltf[:], rhs=oht[r][:],
                start=True, stop=True),
                reads=[f"{ph}_oh{r}", "lt128f"], writes=[ppck], tag="ltmm")
            yield sc.op("tensor", lambda e, r=r, ppc=ppc: e.matmul(
                out=ppc[0:1, 256:256 + EXP], lhsT=self.c_ones128f[:],
                rhs=oht[r][:], start=True, stop=True),
                reads=[f"{ph}_oh{r}", "ones128f"], writes=[ppck], tag="cntmm")
            yield sc.op("vector", lambda e, g=g, ppc=ppc: e.tensor_copy(
                out=loc_all[:, g * EXP:(g + 1) * EXP], in_=ppc[:, 0:EXP]),
                reads=[ppck], writes=[("loc", g)], tag="locev")
            yield sc.op("vector", lambda e, g=g, ppc=ppc: e.tensor_copy(
                out=cnts_row[0:1, g * EXP:(g + 1) * EXP], in_=ppc[0:1, 256:256 + EXP]),
                reads=[ppck], writes=[("cnt", g), ppck], tag="cntev")
            yield sc.op("vector", lambda e, r=r: e.max_index(
                out=idx8[r][:], in_max=m8t[r][:], in_values=sbt[r][:]),
                reads=[f"{ph}_m8{r}", f"{ph}_sbt{r}"], writes=[f"{ph}_idx{r}"],
                hard=[o_oh], tag="maxidx")
            yield sc.op("vector", lambda e, r=r, gk=gk: e.tensor_copy(
                out=self.rec_e[:, gk:gk + TOPK], in_=idx8[r][:, 2:8]),
                reads=[f"{ph}_idx{r}"], writes=[("rec_e", g)], tag="idxcp")
            for k in range(TOPK):
                yield sc.op("vector", lambda e, r=r, k=k, gk=gk: e.scalar_tensor_tensor(
                    out=gsf[r][:], in0=self.c_iotaE[:],
                    scalar=self.rec_e[:, gk + k:gk + k + 1],
                    in1=loc_all[:, (gk // TOPK) * EXP:(gk // TOPK + 1) * EXP],
                    op0=ALU.is_equal, op1=ALU.mult),
                    reads=[("rec_e", g), ("loc", g), "iotaE_bc"],
                    writes=[f"{ph}_gsf{r}"], tag="gpos")
                yield sc.op("vector", lambda e, r=r, k=k, gk=gk: e.reduce_sum(
                    out=self.rec_pos[:, gk + k:gk + k + 1], in_=gsf[r][:],
                    axis=AX.X),
                    reads=[f"{ph}_gsf{r}"], writes=[("rec_pos", g)], tag="gpos2r")
                yield sc.op("vector", lambda e, r=r, k=k: e.scalar_tensor_tensor(
                    out=gsf[r][:], in0=sbt[r][:, 0:EXP],
                    scalar=m8t[r][:, 2 + k:3 + k], in1=sct[r][:],
                    op0=ALU.is_equal, op1=ALU.mult),
                    reads=[f"{ph}_sbt{r}", f"{ph}_m8{r}", f"{ph}_sc{r}"],
                    writes=[f"{ph}_gsf{r}"], tag="gsco")
                yield sc.op("vector", lambda e, r=r, k=k: e.reduce_sum(
                    out=scot[r][:, k:k + 1], in_=gsf[r][:], axis=AX.X),
                    reads=[f"{ph}_gsf{r}"], writes=[f"{ph}_sco{r}"], tag="gsco2r")
            yield sc.op("vector", lambda e, r=r, gk=gk: e.tensor_scalar(
                out=self.rec_w[:, gk:gk + TOPK], in0=scot[r][:, 0:TOPK],
                scalar1=stat[r][:, 7:8], scalar2=None, op0=ALU.mult),
                reads=[f"{ph}_sco{r}", f"{ph}_st{r}"], writes=[("rec_w", g)],
                tag="wraw")

        for g0 in range(0, BL * NT, 2):
            gens = [passa_tile(g0), passa_tile(g0 + 1)]
            alive = [True, True]
            while any(alive):
                for j in range(2):
                    if alive[j]:
                        try:
                            next(gens[j])
                        except StopIteration:
                            alive[j] = False

        if call == 0:
            self.emit_wcast()

        NTT = BL * NT
        self.dma(self.dsc["cnts_d"].rearrange("a b -> (a b)").rearrange("(o c) -> o c", o=1),
                 cnts_row[0:1, :], reads=[("cnt", g) for g in range(NTT)],
                 writes=["cntd"], tag="cntst")
        cm_sb = A("f", 64, ph)
        self.dma(cm_sb[0:64, 0:EXP], self.dsc["cnts_d"],
                 reads=["cntd"], writes=["cntm"], tag="cntld")
        sc.op("tensor", lambda e: e.matmul(
            out=self.ps_e[0:64, 0:EXP], lhsT=self.c_ltf[0:64, 0:64],
            rhs=cm_sb[0:64, 0:EXP], start=True, stop=True),
            reads=["cntm", "lt128f"], writes=["ps_e"], tag="rcallmm")
        rc_sb = A("f", 64, ph)
        sc.op("vector", lambda e: e.tensor_copy(out=rc_sb[0:64, 0:EXP],
                                                in_=self.ps_e[0:64, 0:EXP]),
              reads=["ps_e"], writes=["rcsb", "ps_e"], tag="rcallev")
        if self.debug and call == 0:
            self.dma(self.dbg["d_rc"], rc_sb[0:64, 0:EXP], reads=["rcsb"],
                     writes=[("dbg", "rc")], tag="dbgrc")
        sc.op("tensor", lambda e: e.matmul(
            out=self.ps_f[0:1, 0:EXP], lhsT=self.c_ones128f[0:64, :],
            rhs=cm_sb[0:64, 0:EXP], start=True, stop=True),
            reads=["cntm", "ones128f"], writes=["ps_f"], tag="totmm")
        sc.op("vector", lambda e: e.memset(ccs[0:1, :], 0.0),
              writes=[f"{ph}_cc"], tag="cc0")
        sc.op("vector", lambda e: e.tensor_copy(out=ccs[0:1, 0:EXP],
                                                in_=self.ps_f[0:1, 0:EXP]),
              reads=["ps_f"], writes=[f"{ph}_cc", "ps_f"], tag="cccp")
        self.dma(self.dsc["cc_in"], ccs[0:1, :], reads=[f"{ph}_cc"],
                 writes=[("ccin", call)], tag="ccst")
        ccsem = self.ctx.enter_context(self.nc.semaphore(f"ccsem{call}"))

        if os.environ.get("BK_NOCC", "") == "1":
            for rr_ in range(NCORE):
                self.dma(self.dsc["cc_out"][rr_:rr_ + 1, :], self.dsc["cc_in"],
                         reads=[("ccin", call)], writes=[("ccout", call)],
                         q="gpsimd", tag="fakecc")
        else:
            def cc_fn(e, ccsem=ccsem):
                e.collective_compute(
                    "AllGather", ALU.bypass, replica_groups=[list(range(NCORE))],
                    ins=[self.dsc["cc_in"].opt()],
                    outs=[self.dsc["cc_out"].opt()]).then_inc(ccsem)
                e.wait_ge(ccsem, 1)
                return e.engine_nop()
            sc.op("gpsimd", cc_fn, reads=[("ccin", call)], writes=[("ccout", call)],
                  tag="allgather")
        self.dma(self.row_c[:], self.dsc["cc_out"], reads=[("ccout", call)],
                 writes=["row_c"], tag="ccld")
        sc.op("tensor", lambda e: e.matmul(
            out=self.ps_a[0:1, 0:128], lhsT=self.c_rank[:], rhs=self.row_c[:],
            start=True, stop=True),
            reads=["row_c", "rankmask"], writes=["ps_a"], tag="basemm")
        sc.op("vector", lambda e: e.tensor_copy(out=self.row_b[:],
                                                in_=self.ps_a[0:1, 0:128]),
              reads=["ps_a"], writes=["row_b", "ps_a"], tag="baseev")
        sc.op("tensor", lambda e: e.matmul(
            out=self.ps_a[:, 0:EXP], lhsT=self.c_onescol_f[:],
            rhs=self.row_b[:, 0:EXP], start=True, stop=True),
            reads=["row_b", "ones_col_f"], writes=["ps_a"], tag="basebc")
        fence = sc.op("tensor", lambda e: e.matmul(
            out=self.ps_b[:, 0:EXP], lhsT=self.c_onescol_f[:],
            rhs=self.row_b[:, 0:EXP], start=True, stop=True),
            reads=["row_b", "ones_col_f"], writes=["ps_b"], tag="basebc2")
        sc.op("vector", lambda e: e.tensor_copy(out=self.t_base[:],
                                                in_=self.ps_a[:, 0:EXP]),
              reads=["ps_a"], writes=["t_base", "ps_a", "ps_b"], deps=[fence],
              tag="basebcev")

        sw1_d = self.dsc["bsw1T"].rearrange("(a p) c -> a p c", p=128)
        sw2_d = self.dsc["bsw2T"].rearrange("(a p) c -> a p c", p=128)
        NSF = SF // 128
        for s in range(BL):
            for tc in range(4):
                for kd in range(DG):
                    self.dma(xnfr[:, kd * 512:(kd + 1) * 512],
                             self.dsc["xn_feat"][s * D + kd * 128: s * D + (kd + 1) * 128,
                                                 tc * 512:(tc + 1) * 512],
                             reads=[("xnfd", s, kd, tc * 4 + q) for q in range(4)],
                             writes=[(f"{ph}_xnfr", kd)], tag="shxld")
                xnfr_all = [(f"{ph}_xnfr", kd) for kd in range(DG)]
                for mf in range(NSF):
                    wb = sw1b[mf % 2]
                    wkey = f"{ph}_swb{mf % 2}"
                    for a in range(DG):
                        self.dma(wb[:, a * 128:(a + 1) * 128],
                                 sw1_d[a, :, mf * 128:(mf + 1) * 128],
                                 reads=[("bwd", "bsw1T")], writes=[wkey], tag="sw1ld")
                    pp, ppk = (self.ps_a, "ps_a") if mf % 2 == 0 else (self.ps_b, "ps_b")
                    for kd in range(DG):
                        sc.op("tensor", lambda e, kd=kd, wb=wb, pp=pp: e.matmul(
                            out=pp[:], lhsT=wb[:, kd * 128:(kd + 1) * 128],
                            rhs=xnfr[:, kd * 512:(kd + 1) * 512],
                            start=(kd == 0), stop=(kd == DG - 1)),
                            reads=xnfr_all + [wkey], writes=[ppk], tag="sw1mm")
                    rr = mf % 2
                    self._gelu(pp, ppk, 512, gxe[rr], f"{ph}_gx{rr}", gsc_[rr],
                               f"{ph}_gs{rr}", gub[rr], f"{ph}_gu{rr}",
                               hidS[:, mf * 512:(mf + 1) * 512], (f"{ph}_hidS", mf))
                hidS_all = [(f"{ph}_hidS", mf) for mf in range(NSF)]
                psb = [(self.ps_a, "ps_a"), (self.ps_b, "ps_b"),
                       (self.ps_c, "ps_c"), (self.ps_d, "ps_d")]
                for hhf in range(2):
                    wb = sw1b[hhf]
                    wkey = f"{ph}_swc{hhf}"
                    for grp in range(4):
                        for q in range(4):
                            kf = grp * 4 + q
                            self.dma(wb[:, q * 512:(q + 1) * 512],
                                     sw2_d[kf, :, hhf * 512:(hhf + 1) * 512],
                                     reads=[("bwd", "bsw2T")], writes=[(wkey, q)],
                                     tag="sw2ld")
                            for tt in range(4):
                                pp, ppk = psb[tt]
                                sc.op("tensor", lambda e, kf=kf, q=q, wb=wb, pp=pp,
                                      tt=tt: e.matmul(
                                          out=pp[:],
                                          lhsT=hidS[:, kf * 512 + tt * 128:
                                                    kf * 512 + (tt + 1) * 128],
                                          rhs=wb[:, q * 512:(q + 1) * 512],
                                          start=(kf == 0), stop=(kf == NSF - 1)),
                                      reads=hidS_all + [(wkey, q)], writes=[ppk],
                                      tag="sw2mm")
                    for tt in range(4):
                        g = s * NT + tc * 4 + tt
                        pp, ppk = psb[tt]
                        sc.op("vector", lambda e, g=g, hhf=hhf, pp=pp: e.tensor_copy(
                            out=sht[g % 4][:, hhf * 512:(hhf + 1) * 512], in_=pp[:]),
                            reads=[ppk], writes=[(f"{ph}_sht", g % 4), ppk],
                            tag="shev")
                for tt in range(4):
                    g = s * NT + tc * 4 + tt
                    row0 = g * 128
                    self.dma(self.dsc["sh_tok"][row0:row0 + 128, :], sht[g % 4][:],
                             reads=[(f"{ph}_sht", g % 4)], writes=[("shd", call, g)],
                             tag="shst")

        def passb_tile(g):
            r = g % 2
            gk = g * TOPK
            row0 = g * 128
            yield self.dma(xtk[r][:], xn_tok_d[row0:row0 + 128, :],
                           reads=[("xntd", g)], writes=[f"{ph}_xtk{r}"], tag="ldxtk")
            yield self.dma(rcst[r][0:1, 0:EXP], rc_sb[g:g + 1, 0:EXP],
                           reads=["rcsb"], writes=[(f"{ph}_rcst", r)], tag="rcmv")
            ppb, ppbk = (self.ps_e, "ps_e") if r == 0 else (self.ps_f, "ps_f")
            yield sc.op("tensor", lambda e, r=r, ppb=ppb: e.matmul(
                out=ppb[:, 0:384], lhsT=self.c_onescol_f[:], rhs=rcst[r][0:1, 0:384],
                start=True, stop=True),
                reads=[(f"{ph}_rcst", r), "ones_col_f"], writes=[ppbk], tag="rcbc")
            yield sc.op("vector", lambda e, r=r, ppb=ppb: e.tensor_add(
                out=tb1[r][:], in0=self.c_iota384[:], in1=ppb[:, 0:EXP]),
                reads=[ppbk, "iota384_bc"], writes=[(f"{ph}_tb1", r)], tag="tbl1")
            yield sc.op("vector", lambda e, r=r, ppb=ppb: e.tensor_add(
                out=tb2[r][:], in0=self.t_base[:], in1=ppb[:, 0:EXP]),
                reads=[ppbk, "t_base"], writes=[(f"{ph}_tb2", r), ppbk], tag="tbl2")
            for k in range(TOPK):
                yield sc.op("vector", lambda e, r=r, k=k, gk=gk: e.scalar_tensor_tensor(
                    out=gsf[r][:], in0=self.c_iotaE[:],
                    scalar=self.rec_e[:, gk + k:gk + k + 1], in1=tb1[r][:],
                    op0=ALU.is_equal, op1=ALU.mult),
                    reads=[("rec_e", g), (f"{ph}_tb1", r), "iotaE_bc"],
                    writes=[f"{ph}_gsf{r}"], tag="gslot")
                yield sc.op("vector", lambda e, r=r, k=k: e.reduce_sum(
                    out=rc6[r][:, k:k + 1], in_=gsf[r][:], axis=AX.X),
                    reads=[f"{ph}_gsf{r}"], writes=[(f"{ph}_rc6", r)], tag="gslot2r")
                yield sc.op("vector", lambda e, r=r, k=k, gk=gk: e.scalar_tensor_tensor(
                    out=gsf[r][:], in0=self.c_iotaE[:],
                    scalar=self.rec_e[:, gk + k:gk + k + 1], in1=tb2[r][:],
                    op0=ALU.is_equal, op1=ALU.mult),
                    reads=[("rec_e", g), (f"{ph}_tb2", r), "iotaE_bc"],
                    writes=[f"{ph}_gsf{r}"], tag="gbase")
                yield sc.op("vector", lambda e, r=r, k=k: e.reduce_sum(
                    out=pbt[r][:, k:k + 1], in_=gsf[r][:], axis=AX.X),
                    reads=[f"{ph}_gsf{r}"], writes=[f"{ph}_pb{r}"], tag="gbase2r")
            yield sc.op("vector", lambda e, r=r, gk=gk: e.tensor_add(
                out=pbt[r][:, 0:TOPK], in0=pbt[r][:, 0:TOPK],
                in1=self.rec_pos[:, gk:gk + TOPK]),
                reads=[f"{ph}_pb{r}", ("rec_pos", g)], writes=[f"{ph}_pb{r}"],
                tag="gpos2")
            yield sc.op("vector", lambda e, r=r: e.tensor_scalar(
                out=vmt[r][:, 0:TOPK], in0=pbt[r][:, 0:TOPK],
                scalar1=float(CAP), scalar2=None, op0=ALU.is_lt),
                reads=[f"{ph}_pb{r}"], writes=[f"{ph}_vm{r}"], tag="vmask")
            yield sc.op("vector", lambda e, r=r, gk=gk: e.tensor_mul(
                out=self.rec_w[:, gk:gk + TOPK], in0=vmt[r][:, 0:TOPK],
                in1=self.rec_w[:, gk:gk + TOPK]),
                reads=[f"{ph}_vm{r}", ("rec_w", g)], writes=[("rec_w", g)],
                tag="wfin")
            yield sc.op("vector", lambda e, r=r, gk=gk: e.tensor_add(
                out=slf[r][:, 0:TOPK], in0=rc6[r][:, 0:TOPK],
                in1=self.rec_pos[:, gk:gk + TOPK]),
                reads=[(f"{ph}_rc6", r), ("rec_pos", g)], writes=[f"{ph}_slf{r}"],
                tag="slotp")
            yield sc.op("vector", lambda e, r=r: e.tensor_scalar_add(
                out=slf[r][:, 0:TOPK], in0=slf[r][:, 0:TOPK],
                scalar1=-float(NSLOT)),
                reads=[f"{ph}_slf{r}"], writes=[f"{ph}_slf{r}"], tag="slotsub")
            yield sc.op("vector", lambda e, r=r: e.tensor_mul(
                out=slf[r][:, 0:TOPK], in0=slf[r][:, 0:TOPK],
                in1=vmt[r][:, 0:TOPK]),
                reads=[f"{ph}_slf{r}", f"{ph}_vm{r}"],
                writes=[f"{ph}_slf{r}"], tag="slotmul")
            yield sc.op("vector", lambda e, r=r: e.tensor_scalar_add(
                out=slf[r][:, 0:TOPK], in0=slf[r][:, 0:TOPK],
                scalar1=float(NSLOT)),
                reads=[f"{ph}_slf{r}"], writes=[f"{ph}_slf{r}"], tag="slotadd")
            yield sc.op("vector", lambda e, r=r, gk=gk: e.tensor_copy(
                out=self.rec_slot[:, gk:gk + TOPK], in_=slf[r][:, 0:TOPK]),
                reads=[f"{ph}_slf{r}"], writes=[("rec_slot", g)], tag="sloti")
            for k in range(TOPK):
                yield sc.op("gpsimd", lambda e, r=r, gk=gk, k=k: e.indirect_dma_start(
                    out=self.dsc["xs_loc"],
                    out_offset=IndirectOffsetOnAxis(
                        ap=self.rec_slot[:, gk + k:gk + k + 1], axis=0),
                    in_=xtk[r][:], in_offset=None),
                    reads=[f"{ph}_xtk{r}", ("rec_slot", g)],
                    writes=[("xs", call)], dma=True, tag="scat")

        for g0 in range(0, BL * NT, 2):
            gens = [passb_tile(g0), passb_tile(g0 + 1)]
            alive = [True, True]
            while any(alive):
                for j in range(2):
                    if alive[j]:
                        try:
                            next(gens[j])
                        except StopIteration:
                            alive[j] = False

        sc.op("gpsimd", lambda e: e.engine_nop(),
              reads=[("xs", call)], writes=[("xs_done", call)], tag="xsbar")

        self._emit_experts(call, ph, xstk, xsT, hidE, ytk,
                           gxe, gsc_, gub, w1b, w1o, w2b)

        def combine_tile(g):
            r = g % 2
            gk = g * TOPK
            row0 = g * 128
            for k in range(TOPK):
                rb = (g % 2) * 2 + k % 2
                yield sc.op("gpsimd", lambda e, rb=rb, gk=gk, k=k: e.indirect_dma_start(
                    out=gbuf[rb][:], out_offset=None,
                    in_=self.dsc["y_tab"],
                    in_offset=IndirectOffsetOnAxis(
                        ap=self.rec_slot[:, gk + k:gk + k + 1], axis=0)),
                    reads=[("ytab_done", call), ("rec_slot", g), "ytab_trash"],
                    writes=[(f"{ph}_gb", rb)], dma=True, tag="moegat")
                if k == 0:
                    yield sc.op("vector", lambda e, r=r, rb=rb, gk=gk, k=k:
                                e.tensor_scalar(
                                    out=macc[r][:], in0=gbuf[rb][:],
                                    scalar1=self.rec_w[:, gk + k:gk + k + 1],
                                    scalar2=None, op0=ALU.mult),
                                reads=[(f"{ph}_gb", rb), ("rec_w", g)],
                                writes=[f"{ph}_macc{r}"], tag="mgw")
                else:
                    yield sc.op("vector", lambda e, r=r, rb=rb, gk=gk, k=k:
                                e.scalar_tensor_tensor(
                                    out=macc[r][:], in0=gbuf[rb][:],
                                    scalar=self.rec_w[:, gk + k:gk + k + 1],
                                    in1=macc[r][:], op0=ALU.mult, op1=ALU.add),
                                reads=[(f"{ph}_gb", rb), ("rec_w", g),
                                       f"{ph}_macc{r}"],
                                writes=[f"{ph}_macc{r}"], tag="mgw")
            yield self.dma(xb[r][:], src_x[row0:row0 + 128, :],
                           reads=[src_key], writes=[f"{ph}_xb{r}"], tag="ldx2")
            yield self.dma(sht[g % 4][:], self.dsc["sh_tok"][row0:row0 + 128, :],
                           reads=[("shd", call, g)], writes=[(f"{ph}_sht", g % 4)],
                           tag="shld")
            yield sc.op("vector", lambda e, r=r, g=g: e.tensor_add(
                out=macc[r][:], in0=macc[r][:], in1=sht[g % 4][:]),
                reads=[f"{ph}_macc{r}", (f"{ph}_sht", g % 4)],
                writes=[f"{ph}_macc{r}"], tag="cmb1")
            yield sc.op("vector", lambda e, r=r: e.scalar_tensor_tensor(
                out=xb[r][:], in0=macc[r][:], scalar=0.5, in1=xb[r][:],
                op0=ALU.mult, op1=ALU.add),
                reads=[f"{ph}_macc{r}", f"{ph}_xb{r}"], writes=[f"{ph}_xb{r}"],
                tag="cmb2")
            yield self.dma(dst_x[row0:row0 + 128, :], xb[r][:],
                           reads=[f"{ph}_xb{r}"], writes=[dst_key], tag="stx2")

        for g0 in range(0, BL * NT, 2):
            gens = [combine_tile(g0), combine_tile(g0 + 1)]
            alive = [True, True]
            while any(alive):
                for j in range(2):
                    if alive[j]:
                        try:
                            next(gens[j])
                        except StopIteration:
                            alive[j] = False
        return

    def _shared_sw2_combine(self, call, ph, s, tc, hidS, hidS_all, sw2_d,
                            swb, sht, macc, xb, src_x, dst_x, src_key, dst_key):
        sc = self.sched
        NSF = SF // 128
        psb = [(self.ps_a, "ps_a"), (self.ps_b, "ps_b"),
               (self.ps_c, "ps_c"), (self.ps_d, "ps_d")]
        for hhf in range(2):
            wb = swb[hhf]
            wkey = f"{ph}_swc{hhf}"
            for grp in range(4):
                for q in range(4):
                    kf = grp * 4 + q
                    self.dma(wb[:, q * 512:(q + 1) * 512],
                             sw2_d[kf, :, hhf * 512:(hhf + 1) * 512],
                             reads=[("bwd", "bsw2T")], writes=[(wkey, q)],
                             tag="sw2ld")
                    for tt in range(4):
                        pp, ppk = psb[tt]
                        sc.op("tensor", lambda e, kf=kf, q=q, wb=wb, pp=pp, tt=tt:
                              e.matmul(
                                  out=pp[:],
                                  lhsT=hidS[:, kf * 512 + tt * 128:
                                            kf * 512 + (tt + 1) * 128],
                                  rhs=wb[:, q * 512:(q + 1) * 512],
                                  start=(kf == 0), stop=(kf == NSF - 1)),
                              reads=hidS_all + [(wkey, q)], writes=[ppk],
                              tag="sw2mm")
            for tt in range(4):
                g = s * NT + tc * 4 + tt
                r = g % 2
                pp, ppk = psb[tt]
                sc.op("vector", lambda e, g=g, hhf=hhf, pp=pp: e.tensor_copy(
                    out=sht[g % 4][:, hhf * 512:(hhf + 1) * 512], in_=pp[:]),
                    reads=[ppk], writes=[(f"{ph}_sht", g % 4), ppk], tag="shev")
        for tt in range(4):
            g = s * NT + tc * 4 + tt
            r = g % 2
            gk = g * TOPK
            row0 = g * 128
            om = sc.op("vector", lambda e, r=r: e.memset(macc[r][:], 0.0),
                       writes=[f"{ph}_macc{r}"], tag="macc0")
            for k in range(TOPK):
                sc.op("gpsimd", lambda e, r=r, gk=gk, k=k: e.indirect_dma_start(
                    out=macc[r][:], out_offset=None,
                    in_=self.dsc["y_tab"],
                    in_offset=IndirectOffsetOnAxis(
                        ap=self.rec_slot[:, gk + k:gk + k + 1], axis=0),
                    compute_op=ALU.add),
                    reads=[("ytab_done", call), ("rec_slot", g), "ytab_trash"],
                    writes=[f"{ph}_macc{r}"], deps=[om], dma=True, tag="moegat")
            self.dma(xb[r][:], src_x[row0:row0 + 128, :],
                     reads=[src_key], writes=[f"{ph}_xb{r}"], tag="ldx2")
            sc.op("vector", lambda e, r=r, g=g: e.tensor_add(
                out=macc[r][:], in0=macc[r][:], in1=sht[g % 4][:]),
                reads=[f"{ph}_macc{r}", (f"{ph}_sht", g % 4)],
                writes=[f"{ph}_macc{r}"], tag="cmb1")
            sc.op("vector", lambda e, r=r: e.scalar_tensor_tensor(
                out=xb[r][:], in0=macc[r][:], scalar=0.5, in1=xb[r][:],
                op0=ALU.mult, op1=ALU.add),
                reads=[f"{ph}_macc{r}", f"{ph}_xb{r}"], writes=[f"{ph}_xb{r}"],
                tag="cmb2")
            self.dma(dst_x[row0:row0 + 128, :], xb[r][:],
                     reads=[f"{ph}_xb{r}"], writes=[dst_key], tag="stx2")

    def emit_wcast(self):
        w1v = self.din["w1"].rearrange("e (a p) f -> e a p f", p=128)
        w2v = self.din["w2"].rearrange("e (a p) f -> e a p f", p=128)
        b1v = self.dsc["bw1"].rearrange("e p (a f) -> e a p f", a=DG)
        b2v = self.dsc["bw2"].rearrange("e p (a f) -> e a p f", a=FH // 128)
        for ex in range(EXP):
            gate = [("xntd", ex)] if ex >= 6 else []
            self.dma(b1v[ex], w1v[ex], reads=gate, writes=[("bw1", ex)],
                     q="gpsimd", tag="wcst1")
            self.dma(b2v[ex], w2v[ex], reads=gate, writes=[("bw2", ex)],
                     q="gpsimd", tag="wcst2")

    def _emit_experts(self, call, ph, xstk, xsT, hidE, ytk,
                      gxe, gsc_, gub, w1b, w1o, w2b):
        sc = self.sched
        bw1d = self.dsc["bw1"]
        bw2d = self.dsc["bw2"]
        xs_d = self.dsc["xs_loc"]
        NF = FH // 128
        shared_keys = ([f"{ph}_swb0", f"{ph}_swb1"]
                       + [(f"{ph}_swc{h}", q) for h in range(2) for q in range(4)])
        for ex in range(EXP):
            par = ex % 2
            wb1, wkp = (w1b, "w1b") if par == 0 else (w1o, "w1o")
            extra = []
            if ex == 1:
                extra = shared_keys
            elif ex == 0 and call == 0:
                extra = ["wc_b"]
            for half in range(2):
                self.dma(wb1[:, half * 2048:(half + 1) * 2048],
                         bw1d[ex, :, half * 2048:(half + 1) * 2048],
                         reads=[("bw1", ex)],
                         writes=[f"{ph}_{wkp}{a}" for a in range(half * 4, half * 4 + 4)]
                         + (extra if half == 0 else []), tag="w1lb")
            for half in range(2):
                self.dma(w2b[:, half * 2048:(half + 1) * 2048],
                         bw2d[ex, :, half * 2048:(half + 1) * 2048],
                         reads=[("bw2", ex)],
                         writes=[f"{ph}_w2b{a}" for a in range(half * 2, half * 2 + 2)],
                         tag="w2lb")
            for st in range(3):
                rr = st % 2
                self.dma(xstk[rr][:],
                         xs_d[ex * CAP + st * 128: ex * CAP + (st + 1) * 128, :],
                         reads=[("xs_done", call)], writes=[f"{ph}_xstk{rr}"],
                         tag="xsld")
                for gd in range(DG):
                    sc.op("tensor", lambda e, rr=rr, gd=gd: e.transpose(
                        out=self.ps_t2[:, 0:128],
                        in_=xstk[rr][:, gd * 128:(gd + 1) * 128],
                        identity=self.c_id[:]),
                        reads=[f"{ph}_xstk{rr}", "ident_b"], writes=["pt2"],
                        tag="xsT")
                    sc.op("vector", lambda e, st=st, gd=gd: e.tensor_copy(
                        out=xsT[:, gd * CAP + st * 128: gd * CAP + (st + 1) * 128],
                        in_=self.ps_t2[:, 0:128]),
                        reads=["pt2"], writes=[f"{ph}_xsT", "pt2"], tag="xsTev")
            for mf in range(NF):
                pp, ppk = (self.ps_a, "ps_a") if mf % 2 == 0 else (self.ps_b, "ps_b")
                for kd in range(DG):
                    sc.op("tensor", lambda e, mf=mf, kd=kd, pp=pp, wb1=wb1: e.matmul(
                        out=pp[:, 0:CAP],
                        lhsT=wb1[:, kd * FH + mf * 128: kd * FH + (mf + 1) * 128],
                        rhs=xsT[:, kd * CAP:(kd + 1) * CAP],
                        start=(kd == 0), stop=(kd == DG - 1)),
                        reads=[f"{ph}_xsT", f"{ph}_{wkp}{kd}"], writes=[ppk],
                        tag="w1mm")
                rr = mf % 2
                self._gelu(pp, ppk, CAP, gxe[rr], f"{ph}_gx{rr}", gsc_[rr],
                           f"{ph}_gs{rr}", gub[rr], f"{ph}_gu{rr}",
                           hidE[:, mf * CAP:(mf + 1) * CAP], f"{ph}_hidE")
            for st in range(3):
                rr = st % 2
                for hhf in range(2):
                    pp, ppk = (self.ps_c, "ps_c") if hhf == 0 else (self.ps_d, "ps_d")
                    for kf in range(NF):
                        sc.op("tensor", lambda e, st=st, hhf=hhf, kf=kf, pp=pp: e.matmul(
                            out=pp[:],
                            lhsT=hidE[:, kf * CAP + st * 128: kf * CAP + (st + 1) * 128],
                            rhs=w2b[:, kf * 1024 + hhf * 512: kf * 1024 + (hhf + 1) * 512],
                            start=(kf == 0), stop=(kf == NF - 1)),
                            reads=[f"{ph}_hidE", f"{ph}_w2b{kf}"], writes=[ppk],
                            tag="w2mm")
                    sc.op("vector", lambda e, rr=rr, hhf=hhf, pp=pp: e.tensor_copy(
                        out=ytk[rr][:, hhf * 512:(hhf + 1) * 512], in_=pp[:]),
                        reads=[ppk], writes=[f"{ph}_ytk{rr}", ppk], tag="yev")
                self.dma(self.dsc["y_tab"][ex * CAP + st * 128: ex * CAP + (st + 1) * 128, :],
                         ytk[rr][:], reads=[f"{ph}_ytk{rr}"], writes=[("ytab", call)],
                         tag="yst")

        sc.op("gpsimd", lambda e: e.engine_nop(),
              reads=[("ytab", call)], writes=[("ytab_done", call)], tag="ybar")

    def emit_mamba(self):
        sc = self.sched
        ph = "mam"
        A = self.arena
        resid = self.dsc["resid"]
        xn1_d = self.dsc["xn1_tok"]
        z_d = self.dsc["z_feat"]
        xnf_d = self.dsc["xn_feat"]

        xbm = [A("f", 1024, ph) for _ in range(2)]
        stat = [A("f", 8, ph) for _ in range(2)]
        spz = A("f", 512, ph)
        spa = A("f", 512, ph)
        ytot = A("f", DG * 1024, ph)
        hcar = A("f", 512, ph)
        sqm = A("b", 1024, ph)
        xn1t = [A("b", 1024, ph) for _ in range(2)]
        tri = A("b", 8192, ph)
        xm = A("b", DG * 1027, ph)
        xc = A("b", DG * 1024, ph)
        cacc = A("b", 1024, ph)
        spe = A("b", 512, ph)
        zstg = [A("b", 512, ph) for _ in range(2)]
        bmt = A("b", 1024, ph)
        cmt = A("b", 1024, ph)
        abuf = [A("b", 1024, ph) for _ in range(2)]
        ubuf = A("b", 1024, ph)
        hbuf = [A("b", 1024, ph) for _ in range(3)]
        wbuf = [A("b", 1024, ph) for _ in range(2)]
        xr = A("b", 4096, ph)
        tails = A("b", 24, ph)
        dtt = A("h", DG * 1024, ph)

        xn2t = tri[:, 0:1024]
        xnft = tri[:, 1024:2048]
        bbv = tri[:, 0:4096]
        cbv = tri[:, 4096:8192]
        ztv = [tri[:, 0:1024], tri[:, 1024:2048]]
        ymv = xm

        wst = [self.ar_w[:, 0:1024], self.ar_w[:, 1024:2048]]
        wdt = [self.ar_w[:, 2048:3072], self.ar_w[:, 3072:4096]]
        wbc = self.ar_w[:, 4096:4608]
        wcc = self.ar_w[:, 4608:5120]
        wout = [self.ar_w[:, 5120:6144], self.ar_w[:, 6144:7168]]
        wotk = [self.ar_w[:, 7168:8192], self.ar_w[:, 8192:9216]]

        bwin_d = self.dsc["bw_inT"].rearrange("(a p) c -> a p c", p=128)
        bwdt_d = self.dsc["bw_dtT"].rearrange("(a p) c -> a p c", p=128)
        bwout_d = self.dsc["bw_outT"].rearrange("(a p) c -> a p c", p=128)
        bwb_d = self.dsc["bw_bT"].rearrange("(a p) c -> a p c", p=128)
        bwc_d = self.dsc["bw_cT"].rearrange("(a p) c -> a p c", p=128)

        for a in range(DG):
            self.dma(wbc[:, a * NST:(a + 1) * NST], bwb_d[a, :, :],
                     reads=[("bwd", "bw_bT")], writes=["m_wbc"], tag="wbld")
            self.dma(wcc[:, a * NST:(a + 1) * NST], bwc_d[a, :, :],
                     reads=[("bwd", "bw_cT")], writes=["m_wcc"], tag="wcld2")

        TRI = "mam_tri"
        XMK = "mam_xm"
        SQK = "mam_sqm"

        for s in range(BL):
            for i in range(NT):
                g = s * NT + i
                r = g % 2
                row0 = g * 128
                kx = f"{ph}_xb{r}"
                self.dma(xbm[r][:], resid[row0:row0 + 128, :],
                         reads=["resid1"], writes=[kx], tag="mldx")
                self._ln_tile(xbm[r], kx, stat[r], f"{ph}_st{r}",
                              sqm, SQK, xn1t[r], f"{ph}_xn1{r}")
                self.dma(xn1_d[row0:row0 + 128, :], xn1t[r][:],
                         reads=[f"{ph}_xn1{r}"], writes=[("xn1d", g)], tag="stxn1")
                self._ln_tile(xn1t[r], f"{ph}_xn1{r}", stat[r], f"{ph}_st{r}",
                              sqm, SQK, xn2t, TRI)
                sc.op("vector", lambda e: e.tensor_mul(
                    out=xn2t[:], in0=xn2t[:], in1=self.c_lng[:]),
                    reads=[TRI, "lng_row"], writes=[TRI], tag="ln2g")
                sc.op("vector", lambda e: e.tensor_add(
                    out=xn2t[:], in0=xn2t[:], in1=self.c_lnb[:]),
                    reads=[TRI, "lnb_row"], writes=[TRI], tag="ln2b")
                for gd in range(DG):
                    sc.op("tensor", lambda e, gd=gd: e.transpose(
                        out=self.ps_t1[:, gd * 128:(gd + 1) * 128],
                        in_=xn2t[:, gd * 128:(gd + 1) * 128], identity=self.c_id[:]),
                        reads=[TRI, "ident_b"], writes=[("pt1", gd)], tag="x2T")
                sc.op("vector", lambda e: e.tensor_copy(out=xnft[:], in_=self.ps_t1[:]),
                      reads=[("pt1", gd) for gd in range(DG)],
                      writes=[TRI] + [("pt1", gd) for gd in range(DG)], tag="x2Tev")
                for gd in range(DG):
                    self.dma(xnf_d[s * D + gd * 128: s * D + (gd + 1) * 128,
                                   i * 128:(i + 1) * 128],
                             xnft[:, gd * 128:(gd + 1) * 128],
                             reads=[TRI], writes=[("x2fd", s, gd, i)], tag="stx2f")

            for hh in range(2):
                t0 = hh * TC
                if hh == 0:
                    sc.op("vector", lambda e: e.memset(tails[:], 0.0),
                          writes=["mam_tails"], tag="tail0")
                for nn in range(2):
                    for kd in range(DG):
                        self.dma(xr[:, kd * 512:(kd + 1) * 512],
                                 xnf_d[s * D + kd * 128: s * D + (kd + 1) * 128,
                                       t0 + nn * 512: t0 + (nn + 1) * 512],
                                 reads=[("x2fd", s, kd, q) for q in
                                        range(hh * 8 + nn * 4, hh * 8 + nn * 4 + 4)],
                                 writes=[("mam_xr", kd)], tag="xrld")
                    for me in range(16):
                        wreg = wst[me % 2]
                        wkey = f"mam_wst{me % 2}"
                        for a in range(DG):
                            self.dma(wreg[:, a * 128:(a + 1) * 128],
                                     bwin_d[a, :, me * 128:(me + 1) * 128],
                                     reads=[("bwd", "bw_inT")], writes=[wkey],
                                     tag="winld")
                        pp, ppk = (self.ps_a, "ps_a") if me % 2 == 0 else (self.ps_b, "ps_b")
                        for kd in range(DG):
                            sc.op("tensor", lambda e, kd=kd, wreg=wreg, pp=pp: e.matmul(
                                out=pp[:], lhsT=wreg[:, kd * 128:(kd + 1) * 128],
                                rhs=xr[:, kd * 512:(kd + 1) * 512],
                                start=(kd == 0), stop=(kd == DG - 1)),
                                reads=[("mam_xr", kd), wkey], writes=[ppk],
                                tag="winmm")
                        if me < 8:
                            sc.op("vector", lambda e, me=me, nn=nn, pp=pp: e.tensor_copy(
                                out=xm[:, me * 1027 + 3 + nn * 512:
                                       me * 1027 + 3 + (nn + 1) * 512],
                                in_=pp[:]),
                                reads=[ppk], writes=[XMK, ppk], tag="xmev")
                        else:
                            zr = me % 2
                            sc.op("vector", lambda e, zr=zr, pp=pp: e.tensor_copy(
                                out=zstg[zr][:], in_=pp[:]),
                                reads=[ppk], writes=[f"mam_zs{zr}", ppk], tag="zev")
                            self.dma(z_d[(me - 8) * 128:(me - 7) * 128,
                                         s * S + t0 + nn * 512: s * S + t0 + (nn + 1) * 512],
                                     zstg[zr][:],
                                     reads=[f"mam_zs{zr}"], writes=[("zd", s, hh)],
                                     tag="zst")
                if hh == 0:
                    sc.op("vector", lambda e: e.memset(
                        xm.rearrange("p (g c) -> p g c", g=DG)[:, :, 0:3], 0.0),
                        reads=[XMK], writes=[XMK], tag="convpad")
                else:
                    for gd in range(DG):
                        sc.op("vector", lambda e, gd=gd: e.tensor_copy(
                            out=xm[:, gd * 1027: gd * 1027 + 3],
                            in_=tails[:, gd * 3:(gd + 1) * 3]),
                            reads=["mam_tails"], writes=[XMK], tag="convcarry")
                for gd in range(DG):
                    base = gd * 1027
                    sc.op("vector", lambda e, base=base, gd=gd: e.tensor_scalar(
                        out=cacc[:], in0=xm[:, base:base + 1024],
                        scalar1=self.p_convw[:, 4 * gd:4 * gd + 1], scalar2=None,
                        op0=ALU.mult),
                        reads=[XMK, "p_convw"], writes=["mam_cacc"], tag="cv0")
                    for j in range(1, 4):
                        sc.op("vector", lambda e, base=base, gd=gd, j=j:
                              e.scalar_tensor_tensor(
                                  out=cacc[:], in0=xm[:, base + j:base + j + 1024],
                                  scalar=self.p_convw[:, 4 * gd + j:4 * gd + j + 1],
                                  in1=cacc[:], op0=ALU.mult, op1=ALU.add),
                              reads=[XMK, "p_convw", "mam_cacc"],
                              writes=["mam_cacc"], tag="cvj")
                    sc.op("vector", lambda e, base=base, gd=gd: e.tensor_copy(
                        out=tails[:, gd * 3:(gd + 1) * 3],
                        in_=xm[:, base + 1024: base + 1027]),
                        reads=[XMK], writes=["mam_tails"], tag="tailsv")
                    sc.op("scalar", lambda e, gd=gd: e.activation(
                        out=sqm[:], in_=cacc[:], func=AF.Sigmoid,
                        bias=self.p_convb[:, gd:gd + 1]),
                        reads=["mam_cacc", "p_convb"], writes=[SQK], tag="cvsig")
                    sc.op("vector", lambda e, gd=gd: e.scalar_tensor_tensor(
                        out=xc[:, gd * 1024:(gd + 1) * 1024], in0=cacc[:],
                        scalar=self.p_convb[:, gd:gd + 1], in1=sqm[:],
                        op0=ALU.add, op1=ALU.mult),
                        reads=["mam_cacc", "p_convb", SQK], writes=["mam_xc"],
                        tag="cvmul")
                for md in range(DG):
                    wreg = wdt[md % 2]
                    wkey = f"mam_wdt{md % 2}"
                    for a in range(DG):
                        self.dma(wreg[:, a * 128:(a + 1) * 128],
                                 bwdt_d[a, :, md * 128:(md + 1) * 128],
                                 reads=[("bwd", "bw_dtT")], writes=[wkey], tag="wdtld")
                    for nn in range(2):
                        pp, ppk = (self.ps_c, "ps_c") if nn == 0 else (self.ps_d, "ps_d")
                        for kd in range(DG):
                            sc.op("tensor", lambda e, kd=kd, wreg=wreg, pp=pp, nn=nn:
                                  e.matmul(out=pp[:],
                                           lhsT=wreg[:, kd * 128:(kd + 1) * 128],
                                           rhs=xc[:, kd * 1024 + nn * 512:
                                                  kd * 1024 + (nn + 1) * 512],
                                           start=(kd == 0), stop=(kd == DG - 1)),
                                  reads=["mam_xc", wkey], writes=[ppk], tag="wdtmm")
                        sc.op("vector", lambda e, md=md, pp=pp: e.tensor_scalar(
                            out=spz[:], in0=pp[:],
                            scalar1=self.p_bdt[:, md:md + 1], scalar2=None,
                            op0=ALU.add),
                            reads=[ppk, "p_bdt"], writes=["mam_spz", ppk], tag="spz")
                        sc.op("vector", lambda e: e.tensor_scalar_mul(
                            out=spa[:], in0=spz[:], scalar1=-1.0),
                            reads=["mam_spz"], writes=["mam_spa"], tag="spneg")
                        sc.op("vector", lambda e: e.tensor_max(
                            out=spa[:], in0=spa[:], in1=spz[:]),
                            reads=["mam_spa", "mam_spz"], writes=["mam_spa"],
                            tag="spabs")
                        sc.op("scalar", lambda e: e.activation(
                            out=spe[:], in_=spa[:], func=AF.Exp, scale=-1.0),
                            reads=["mam_spa"], writes=["mam_spe"], tag="spexp")
                        sc.op("scalar", lambda e: e.activation(
                            out=spa[:], in_=spe[:], func=AF.Ln,
                            bias=self.c_kcol[:, 1:2]),
                            reads=["mam_spe", "kcol"], writes=["mam_spa"],
                            tag="spln")
                        sc.op("vector", lambda e: e.tensor_relu(out=spz[:], in_=spz[:]),
                              reads=["mam_spz"], writes=["mam_spz"], tag="sprelu")
                        sc.op("vector", lambda e, md=md, nn=nn: e.tensor_add(
                            out=dtt[:, md * 1024 + nn * 512: md * 1024 + (nn + 1) * 512],
                            in0=spz[:], in1=spa[:]),
                            reads=["mam_spz", "mam_spa"], writes=["mam_dt"],
                            tag="spadd")
                for nn in range(2):
                    for kd in range(DG):
                        sc.op("tensor", lambda e, kd=kd, nn=nn: e.matmul(
                            out=self.ps_e[0:NST, :],
                            lhsT=wbc[:, kd * NST:(kd + 1) * NST],
                            rhs=xc[:, kd * 1024 + nn * 512: kd * 1024 + (nn + 1) * 512],
                            start=(kd == 0), stop=(kd == DG - 1)),
                            reads=["mam_xc", "m_wbc"], writes=["ps_e"], tag="wbmm")
                    sc.op("vector", lambda e, nn=nn: e.tensor_copy(
                        out=bmt[0:NST, nn * 512:(nn + 1) * 512],
                        in_=self.ps_e[0:NST, :]),
                        reads=["ps_e"], writes=["mam_bm", "ps_e"], tag="bmev")
                    for kd in range(DG):
                        sc.op("tensor", lambda e, kd=kd, nn=nn: e.matmul(
                            out=self.ps_f[0:NST, :],
                            lhsT=wcc[:, kd * NST:(kd + 1) * NST],
                            rhs=xc[:, kd * 1024 + nn * 512: kd * 1024 + (nn + 1) * 512],
                            start=(kd == 0), stop=(kd == DG - 1)),
                            reads=["mam_xc", "m_wcc"], writes=["ps_f"], tag="wcmm")
                    sc.op("vector", lambda e, nn=nn: e.tensor_copy(
                        out=cmt[0:NST, nn * 512:(nn + 1) * 512],
                        in_=self.ps_f[0:NST, :]),
                        reads=["ps_f"], writes=["mam_cm", "ps_f"], tag="cmev")

                if self.debug and s == 0 and hh == 0:
                    self.dma(self.dbg["d_xc"].rearrange("(g p) c -> p g c", p=128),
                             xc.rearrange("p (g c) -> p g c", g=DG),
                             reads=["mam_xc"], writes=[("dbg", "xc")])
                    self.dma(self.dbg["d_dt"].rearrange("(g p) c -> p g c", p=128),
                             dtt.rearrange("p (g c) -> p g c", g=DG),
                             reads=["mam_dt"], writes=[("dbg", "dt")])
                    self.dbg_dump("d_bm", bmt[0:NST, :], ["mam_bm"])
                for blk in range(NST // NBLK):
                    for q in range(NBLK):
                        n = blk * NBLK + q
                        for nn in range(2):
                            pp, ppk = (self.ps_e, "ps_e") if nn == 0 else (self.ps_f, "ps_f")
                            sc.op("tensor", lambda e, n=n, nn=nn, pp=pp: e.matmul(
                                out=pp[:],
                                lhsT=self.c_oneh[:, n * 128:(n + 1) * 128],
                                rhs=bmt[0:NST, nn * 512:(nn + 1) * 512],
                                start=True, stop=True),
                                reads=["mam_bm", "onehot_b"], writes=[ppk],
                                tag="bbmm")
                            sc.op("scalar", lambda e, q=q, nn=nn, pp=pp: e.activation(
                                out=bbv[:, q * 1024 + nn * 512: q * 1024 + (nn + 1) * 512],
                                in_=pp[:], func=AF.Copy),
                                reads=[ppk], writes=[TRI, ppk], tag="bbev")
                            sc.op("tensor", lambda e, n=n, nn=nn, pp=pp: e.matmul(
                                out=pp[:],
                                lhsT=self.c_oneh[:, n * 128:(n + 1) * 128],
                                rhs=cmt[0:NST, nn * 512:(nn + 1) * 512],
                                start=True, stop=True),
                                reads=["mam_cm", "onehot_b"], writes=[ppk],
                                tag="cbmm")
                            sc.op("scalar", lambda e, q=q, nn=nn, pp=pp: e.activation(
                                out=cbv[:, q * 1024 + nn * 512: q * 1024 + (nn + 1) * 512],
                                in_=pp[:], func=AF.Copy),
                                reads=[ppk], writes=[TRI, ppk], tag="cbev")
                    for gd in range(DG):
                        for q in range(NBLK):
                            n = blk * NBLK + q
                            ra = (gd * NBLK + q) % 2
                            rh = (gd * NBLK + q) % 3
                            sc.op("scalar", lambda e, gd=gd, n=n, ra=ra: e.activation(
                                out=abuf[ra][:], in_=dtt[:, gd * 1024:(gd + 1) * 1024],
                                func=AF.Exp, scale=self.p_A[:, n:n + 1]),
                                reads=["mam_dt", "p_A"], writes=[f"mam_ab{ra}"],
                                tag="aexp")
                            sc.op("vector", lambda e, gd=gd, q=q: e.tensor_mul(
                                out=ubuf[:], in0=xc[:, gd * 1024:(gd + 1) * 1024],
                                in1=bbv[:, q * 1024:(q + 1) * 1024]),
                                reads=["mam_xc", TRI], writes=["mam_ub"], tag="umul")
                            if hh == 0:
                                sc.op("vector", lambda e, ra=ra, rh=rh:
                                      e.tensor_tensor_scan(
                                          out=hbuf[rh][:], data0=abuf[ra][:],
                                          data1=ubuf[:], initial=0.0,
                                          op0=ALU.mult, op1=ALU.add),
                                      reads=[f"mam_ab{ra}", "mam_ub"],
                                      writes=[f"mam_hb{rh}"], tag="tts")
                                sc.op("vector", lambda e, gd=gd, n=n, rh=rh:
                                      e.tensor_copy(
                                          out=hcar[:, gd * NST + n: gd * NST + n + 1],
                                          in_=hbuf[rh][:, TC - 1:TC]),
                                      reads=[f"mam_hb{rh}"], writes=["mam_hc"],
                                      tag="hcar")
                            else:
                                ini = hcar[:, gd * NST + n: gd * NST + n + 1]
                                sc.op("vector", lambda e, ra=ra, rh=rh, ini=ini:
                                      e.tensor_tensor_scan(
                                          out=hbuf[rh][:], data0=abuf[ra][:],
                                          data1=ubuf[:], initial=ini,
                                          op0=ALU.mult, op1=ALU.add),
                                      reads=[f"mam_ab{ra}", "mam_ub", "mam_hc"],
                                      writes=[f"mam_hb{rh}"], tag="tts")
                            rw = rh % 2
                            sc.op("vector", lambda e, rw=rw, rh=rh, q=q: e.tensor_mul(
                                out=wbuf[rw][:], in0=hbuf[rh][:],
                                in1=cbv[:, q * 1024:(q + 1) * 1024]),
                                reads=[f"mam_hb{rh}", TRI], writes=[f"mam_wb{rw}"],
                                tag="wmul")
                            if q == 0:
                                self.dma(ymv[:, gd * 1024:(gd + 1) * 1024],
                                         wbuf[rw][:],
                                         reads=[f"mam_wb{rw}"],
                                         writes=[("mam_yg", gd), XMK],
                                         q="gpsimd", tag="yg0")
                            else:
                                sc.op("gpsimd", lambda e, gd=gd, rh=rh: e.dma_start(
                                    out=ymv[:, gd * 1024:(gd + 1) * 1024],
                                    in_=wbuf[rw][:], accum_op=ALU.add),
                                    reads=[f"mam_wb{rw}", ("mam_yg", gd)],
                                    writes=[("mam_yg", gd), XMK], dma=True, tag="ygacc")
                    for gd in range(DG):
                        if blk == 0:
                            sc.op("vector", lambda e, gd=gd: e.tensor_copy(
                                out=ytot[:, gd * 1024:(gd + 1) * 1024],
                                in_=ymv[:, gd * 1024:(gd + 1) * 1024]),
                                reads=[("mam_yg", gd)], writes=[("mam_yt", gd)],
                                tag="ytcp")
                        else:
                            sc.op("vector", lambda e, gd=gd: e.tensor_add(
                                out=ytot[:, gd * 1024:(gd + 1) * 1024],
                                in0=ytot[:, gd * 1024:(gd + 1) * 1024],
                                in1=ymv[:, gd * 1024:(gd + 1) * 1024]),
                                reads=[("mam_yg", gd), ("mam_yt", gd)],
                                writes=[("mam_yt", gd)], tag="ytadd")

                if self.debug and s == 0 and hh == 0:
                    self.dma(self.dbg["d_ytot"].rearrange("(g p) c -> p g c", p=128),
                             ytot.rearrange("p (g c) -> p g c", g=DG),
                             reads=[("mam_yt", gd) for gd in range(DG)],
                             writes=[("dbg", "ytot")])
                for gd in range(DG):
                    sc.op("vector", lambda e, gd=gd: e.scalar_tensor_tensor(
                        out=ytot[:, gd * 1024:(gd + 1) * 1024],
                        in0=xc[:, gd * 1024:(gd + 1) * 1024],
                        scalar=self.p_dp[:, gd:gd + 1],
                        in1=ytot[:, gd * 1024:(gd + 1) * 1024],
                        op0=ALU.mult, op1=ALU.add),
                        reads=["mam_xc", "p_dp", ("mam_yt", gd)],
                        writes=[("mam_yt", gd)], tag="dpterm")
                for gd in range(DG):
                    rz = gd % 2
                    self.dma(ztv[rz][:],
                             z_d[gd * 128:(gd + 1) * 128, s * S + t0: s * S + t0 + TC],
                             reads=[("zd", s, hh)], writes=[f"mam_zt{rz}", TRI],
                             tag="zld")
                    sc.op("scalar", lambda e, rz=rz: e.activation(
                        out=sqm[:], in_=ztv[rz][:], func=AF.Sigmoid),
                        reads=[f"mam_zt{rz}"], writes=[SQK], tag="zsig")
                    sc.op("vector", lambda e, rz=rz: e.tensor_mul(
                        out=ztv[rz][:], in0=ztv[rz][:], in1=sqm[:]),
                        reads=[f"mam_zt{rz}", SQK], writes=[f"mam_zt{rz}"],
                        tag="zmul")
                    sc.op("vector", lambda e, gd=gd, rz=rz: e.tensor_mul(
                        out=ymv[:, gd * 1024:(gd + 1) * 1024],
                        in0=ytot[:, gd * 1024:(gd + 1) * 1024], in1=ztv[rz][:]),
                        reads=[("mam_yt", gd), f"mam_zt{rz}"],
                        writes=[("mam_ym", gd), ("mam_yg", gd), "mam_xm"],
                        tag="ygate")
                self._wout_combine(s, hh, ymv, wout, bwout_d, xbm, tri, resid,
                                   xn1_d)

    def _wout_combine(self, s, hh, ymv, wout, bwout_d, xbm, tri, resid, xn1_d):
        sc = self.sched
        xn1rv = [tri[:, 2048 + i * 1024: 2048 + (i + 1) * 1024] for i in range(2)]
        psb0 = [(self.ps_a, "ps_a"), (self.ps_b, "ps_b")]
        psb1 = [(self.ps_c, "ps_c"), (self.ps_d, "ps_d")]
        for ttg in range(4):
            tts = [ttg * 2, ttg * 2 + 1]
            for tt in tts:
                g = s * NT + hh * 8 + tt
                row0 = g * 128
                self.dma(xbm[tt % 2][:], resid[row0:row0 + 128, :],
                         reads=["resid1"], writes=[("mam_xbt", tt % 2)], tag="mldx2")
                self.dma(xn1rv[tt % 2][:], xn1_d[row0:row0 + 128, :],
                         reads=[("xn1d", g)], writes=[("mam_x1r", tt % 2), "mam_tri"],
                         tag="ldxn1")
            for hhf in range(2):
                wreg = wout[hhf]
                wkey = f"mam_wo{hhf}"
                psb = psb0 if hhf == 0 else psb1
                for grp in range(4):
                    for qq in range(2):
                        kd = grp * 2 + qq
                        self.dma(wreg[:, qq * 512:(qq + 1) * 512],
                                 bwout_d[kd, :, hhf * 512:(hhf + 1) * 512],
                                 reads=[("bwd", "bw_outT")], writes=[(wkey, qq)],
                                 tag="wold")
                        for ii, tt in enumerate(tts):
                            pp, ppk = psb[ii]
                            sc.op("tensor", lambda e, kd=kd, qq=qq, wreg=wreg,
                                  pp=pp, tt=tt: e.matmul(
                                      out=pp[:],
                                      lhsT=ymv[:, kd * 1024 + tt * 128:
                                               kd * 1024 + (tt + 1) * 128],
                                      rhs=wreg[:, qq * 512:(qq + 1) * 512],
                                      start=(kd == 0), stop=(kd == DG - 1)),
                                  reads=[("mam_ym", kd), "mam_xm", (wkey, qq)],
                                  writes=[ppk], tag="womm")
                for ii, tt in enumerate(tts):
                    pp, ppk = psb[ii]
                    sc.op("vector", lambda e, tt=tt, hhf=hhf, pp=pp:
                          e.tensor_add(
                              out=xbm[tt % 2][:, hhf * 512:(hhf + 1) * 512],
                              in0=xbm[tt % 2][:, hhf * 512:(hhf + 1) * 512],
                              in1=pp[:]),
                          reads=[("mam_xbt", tt % 2), ppk],
                          writes=[("mam_xbt", tt % 2), ppk], tag="mcmb1")
            for tt in tts:
                g = s * NT + hh * 8 + tt
                row0 = g * 128
                sc.op("vector", lambda e, tt=tt: e.tensor_add(
                    out=xbm[tt % 2][:], in0=xbm[tt % 2][:], in1=xn1rv[tt % 2][:]),
                    reads=[("mam_xbt", tt % 2), ("mam_x1r", tt % 2)],
                    writes=[("mam_xbt", tt % 2)], tag="mcmb2")
                self.dma(resid[row0:row0 + 128, :], xbm[tt % 2][:],
                         reads=[("mam_xbt", tt % 2)], writes=["resid1"], tag="mstx")

    def emit_finish(self):
        sc = self.sched
        for eng in ("vector", "gpsimd"):
            sc.op(eng, lambda e: e.engine_nop(), reads=["xout"], tag="fin")


if __name__ == "__main__":
    nc = build_program()
    print("built ok")


# version 18
# speedup vs baseline: 1.1203x; 1.0377x over previous
import os
import sys
import numpy as np
from contextlib import ExitStack

sys.path.insert(0, "/opt/trn_rl_repo")

import concourse.bass as bass
import concourse.mybir as mybir
from concourse.bass_utils import run_bass_kernel_spmd
from concourse.bass_core_rust import IndirectOffsetOnAxis

F32 = mybir.dt.float32
BF16 = mybir.dt.bfloat16
F16 = mybir.dt.float16
I32 = mybir.dt.int32
U32 = mybir.dt.uint32
AF = mybir.ActivationFunctionType
ALU = mybir.AluOpType
AX = mybir.AxisListType

NCORE = 8
B, S, D = 32, 2048, 1024
BL = B // NCORE
DG = D // 128
E2 = 2048
NST = 64
EXP, TOPK, CAP = 64, 6, 384
FH, SF = 512, 2048
TC = 1024
NT = S // 128
TLOC = BL * S
NSLOT = EXP * CAP
NBLK = 4
BIGF = 1.0e30
OORF = float(NSLOT)

GELU_A = 1.5957691216057308
GELU_B = 0.07135481627803054


class Res:
    __slots__ = ("ws", "readers")

    def __init__(self):
        self.ws = []
        self.readers = []


SMALL_TAGS = {
    "ln_sum", "ln_sq2", "ln_mu", "ln_musq", "ln_ex2", "ln_var", "ln_ln",
    "ln_rstd", "ln_apply",
    "sbadd", "max8", "matchrep", "ohsum", "wsc", "wsc2", "rseps", "rinv",
    "locev", "locpre", "rcupd", "idxcp", "maxidx", "gpos", "gpos2r", "gsco",
    "gsco2r", "wraw", "cc0", "cccp", "baseev", "basebcev",
    "gbase", "gbase2r", "gpos2", "vmask", "wfin", "slote", "slotp", "mgw",
    "slotsub", "slotmul", "slotadd", "sloti",
    "hcar", "tailsv", "convcarry", "tail0", "gslot", "gslot2r", "tbl1", "tbl2",
    "cntev", "rcallev",
    "spz", "spneg", "spabs", "spexp", "spln", "sprelu", "spadd",
    "ge_u", "ge_arg", "ge_sig", "ge_cp", "ge_sq",
    "Aexp", "Aneg", "Aevac", "zrow", "cvsig", "cvmul", "zsig", "zmul",
}


class Op:
    __slots__ = ("eng", "fn", "deps", "inc", "need_inc", "seq", "tag", "hard",
                 "sm", "pool")

    def __init__(self, eng, fn, deps, inc, tag="", hard=()):
        self.eng = eng
        self.fn = fn
        self.deps = deps
        self.inc = inc
        self.need_inc = False
        self.seq = 0
        self.tag = tag
        self.hard = list(hard)
        self.sm = tag in SMALL_TAGS
        self.pool = None


class Sched:
    ENGS = ("sync", "vector", "scalar", "tensor", "gpsimd")

    GP_RING = 6
    NPOOL = 16

    def __init__(self, nc, ctx):
        self.nc = nc
        self.ops = []
        self.per_eng = {e: [] for e in self.ENGS}
        self.res = {}
        self.sems = {e: ctx.enter_context(nc.semaphore(f"sem_{e}")) for e in self.ENGS}
        self.gp_dmas = []
        self.pools = {}
        self.pool_rr = {}
        for e in ("sync", "scalar", "gpsimd"):
            self.pools[e] = [ctx.enter_context(nc.semaphore(f"dp_{e}_{j}"))
                             for j in range(self.NPOOL)]
            self.pool_rr[e] = 0

    def R(self, key):
        r = self.res.get(key)
        if r is None:
            r = self.res[key] = Res()
        return r

    def op(self, eng, fn, reads=(), writes=(), deps=(), dma=False, hard=(), tag=""):
        dl = []
        for k in reads:
            dl.extend(self.R(k).ws)
        for k in writes:
            r = self.R(k)
            for w in r.ws:
                if not (dma and w.inc == 16):
                    dl.append(w)
            dl.extend(r.readers)
        dl.extend(deps)
        if dma and eng == "gpsimd" and len(self.gp_dmas) >= self.GP_RING:
            dl.append(self.gp_dmas[-self.GP_RING])
        o = Op(eng, fn, dl, 16 if dma else 1, tag, hard)
        if dma:
            j = self.pool_rr[eng]
            self.pool_rr[eng] = (j + 1) % self.NPOOL
            o.pool = (eng, j)
        if dma and eng == "gpsimd":
            self.gp_dmas.append(o)
        self.per_eng[eng].append(o)
        self.ops.append(o)
        for k in reads:
            self.R(k).readers.append(o)
        for k in writes:
            r = self.R(k)
            if r.readers:
                r.ws = [o]
                r.readers = []
            else:
                r.ws.append(o)
        return o

    def finalize(self):
        for o in self.ops:
            if o.inc == 16:
                o.need_inc = True
            for d in o.deps:
                if d.eng != o.eng or d.inc == 16 or d.sm or o.sm:
                    d.need_inc = True
            for d in o.hard:
                d.need_inc = True
        for e in self.ENGS:
            c = 0
            for o in self.per_eng[e]:
                if o.pool is None:
                    if o.need_inc:
                        c += o.inc
                    o.seq = c
        pc = {}
        for o in self.ops:
            if o.pool is not None:
                pc[o.pool] = pc.get(o.pool, 0) + 16
                o.seq = pc[o.pool]
        self.waits = {}
        for o in self.ops:
            w = {}
            for d in o.deps:
                if d.pool is not None:
                    key = d.pool
                elif d.eng == o.eng and not (d.sm or o.sm):
                    continue
                else:
                    key = d.eng
                w[key] = max(w.get(key, 0), d.seq)
            for d in o.hard:
                key = d.pool if d.pool is not None else d.eng
                w[key] = max(w.get(key, 0), d.seq)
            self.waits[id(o)] = w

    def replay(self, eng_name, eng_obj):
        sems = self.sems
        last = {}
        for o in self.per_eng[eng_name]:
            for key, val in self.waits[id(o)].items():
                if last.get(key, -1) < val:
                    h = self.pools[key[0]][key[1]] if isinstance(key, tuple) \
                        else sems[key]
                    eng_obj.wait_ge(h, val)
                    last[key] = val
            inst = o.fn(eng_obj)
            if o.need_inc:
                if inst is None:
                    raise RuntimeError(f"op {o.tag} needs inc but returned None")
                if o.pool is not None:
                    inst.then_inc(self.pools[o.pool[0]][o.pool[1]], 16)
                else:
                    inst.then_inc(sems[eng_name], o.inc)


def build_host_inputs(inp):
    import ml_dtypes
    bf = ml_dtypes.bfloat16
    f32 = np.float32
    x = np.ascontiguousarray(inp["L0_attn_x"], dtype=f32)
    shards = [np.ascontiguousarray(x[c * BL:(c + 1) * BL].reshape(TLOC, D))
              for c in range(NCORE)]

    w = {}
    w["w_inT"] = np.ascontiguousarray(inp["L0_attn_W_in"].T.astype(f32))
    w["w_dtT"] = np.ascontiguousarray(inp["L0_attn_W_dt"].T.astype(f32))
    w["w_outT"] = np.ascontiguousarray(inp["L0_attn_W_out"].T.astype(f32))
    w["w_bT"] = np.ascontiguousarray(inp["L0_attn_W_B"].T.astype(f32))
    w["w_cT"] = np.ascontiguousarray(inp["L0_attn_W_C"].T.astype(f32))
    w["routT"] = np.ascontiguousarray(inp["L0_ffn_router_weight"].T.astype(f32))
    w["sw1T"] = np.ascontiguousarray(inp["L0_ffn_sw1"].T.astype(f32))
    w["sw2T"] = np.ascontiguousarray(inp["L0_ffn_sw2"].T.astype(f32))
    w["w1"] = np.ascontiguousarray(inp["L0_ffn_w1"].astype(f32))
    w["w2"] = np.ascontiguousarray(inp["L0_ffn_w2"].astype(f32))
    w["conv_w"] = np.ascontiguousarray(inp["L0_attn_conv_w"].reshape(D, 4).astype(f32))
    w["conv_b"] = np.ascontiguousarray(inp["L0_attn_conv_b"].reshape(D, 1).astype(f32))
    w["b_dt"] = np.ascontiguousarray(inp["L0_attn_b_dt"].reshape(D, 1).astype(f32))
    w["dp"] = np.ascontiguousarray(inp["L0_attn_Dp"].reshape(D, 1).astype(f32))
    w["a_log"] = np.ascontiguousarray(inp["L0_attn_A_log"].reshape(1, NST).astype(f32))
    w["bias_bc"] = np.ascontiguousarray(np.broadcast_to(
        inp["L0_ffn_expert_bias"].astype(f32).reshape(1, EXP), (128, EXP)))
    w["lng_row"] = np.ascontiguousarray(np.broadcast_to(
        inp["L0_attn_ln_g"].reshape(1, D), (128, D)).astype(bf))
    w["lnb_row"] = np.ascontiguousarray(np.broadcast_to(
        inp["L0_attn_ln_b"].reshape(1, D), (128, D)).astype(bf))

    c = {}
    c["ones_col_b"] = np.ones((1, 128), bf)
    c["ones_col_f"] = np.ones((1, 128), f32)
    c["onessq_b"] = np.ones((128, 128), bf)
    c["ltstrict_b"] = np.ascontiguousarray(
        np.tril(np.ones((128, 128), f32), k=-1).T.astype(bf))
    c["ident_b"] = np.eye(128, dtype=bf)
    oh = np.zeros((EXP, NST, 128), f32)
    for n in range(NST):
        oh[n, n, :] = 1.0
    c["onehot_b"] = np.ascontiguousarray(oh.reshape(EXP, NST * 128)).astype(bf)
    c["iota384_bc"] = np.ascontiguousarray(np.broadcast_to(
        (np.arange(EXP, dtype=f32) * CAP).reshape(1, EXP), (128, EXP)))
    c["iotaE_bc"] = np.ascontiguousarray(np.broadcast_to(
        np.arange(EXP, dtype=f32).reshape(1, EXP), (128, EXP)))
    c["oor_bc"] = np.full((128, EXP), OORF, f32)
    c["lt128f"] = np.ascontiguousarray(
        np.tril(np.ones((128, 128), f32), k=-1).T)
    c["ones128f"] = np.ones((128, 1), f32)
    kc = np.zeros((128, 2), f32); kc[:, 0] = 1e-5; kc[:, 1] = 1.0
    c["kcol"] = kc
    rm = np.zeros((NCORE, NCORE, 1), f32)
    for cid in range(NCORE):
        rm[cid, :cid, 0] = 1.0

    maps = []
    for cid in range(NCORE):
        m = {"x_in": shards[cid], "rankmask": np.ascontiguousarray(rm[cid])}
        m.update(w)
        m.update(c)
        maps.append(m)
    return maps


def kernel(**inputs):
    maps = build_host_inputs(inputs)
    nc = build_program()
    res = run_bass_kernel_spmd(nc, maps, core_ids=list(range(NCORE)))
    out = np.concatenate([np.asarray(r["x_out"]).reshape(BL, S, D)
                          for r in res.results], axis=0)
    return out.astype(np.float32)


def build_program():
    nc = bass.Bass("TRN2", target_bir_lowering=False, debug=False)
    with ExitStack() as ctx:
        P = Prog(nc, ctx)
        P.declare()
        P.emit_setup()
        P.emit_ffn(0)
        P.dbg_dump("d_xffn1", P.dsc["resid"], ["resid1"])
        P.dbg_dump("d_xntok", P.dsc["xn_tok"], [("xntd", g) for g in range(BL * NT)])
        P.dbg_dump("d_ccout", P.dsc["cc_out"], [("ccout", 0)])
        P.dbg_dump("d_wtab", P.dsc["w_tab"], [("wtab", 0)])
        P.dbg_dump("d_xs", P.dsc["xs_loc"][0:512, :], [("xs", 0)])
        P.dbg_dump("d_ytab", P.dsc["y_tab"][0:512, :], [("ytab", 0)])
        if P.debug:
            P.dma(P.dbg["d_rece"], P.rec_e[:], reads=[("rec_e", g) for g in range(BL * NT)], writes=[("dbg", "rece")])
            P.dma(P.dbg["d_recp"], P.rec_pos[:], reads=[("rec_pos", g) for g in range(BL * NT)], writes=[("dbg", "recp")])
            P.dma(P.dbg["d_recw"], P.rec_w[:], reads=[("rec_w", g) for g in range(BL * NT)], writes=[("dbg", "recw")])
            P.dma(P.dbg["d_ytf"], P.dsc["y_tab"], reads=[("ytab", 0), "ytab_trash"], writes=[("dbg", "ytf")])
            P.dma(P.dbg["d_cnts"], P.dsc["cnts_d"], reads=["cntd"], writes=[("dbg", "cnts")])
            P.dma(P.dbg["d_slot"], P.rec_slot[:], reads=[("rec_slot", g) for g in range(BL * NT)], writes=[("dbg", "slot")])
        P.emit_mamba()
        P.dbg_dump("d_xmam", P.dsc["resid"], ["resid1"])
        P.dbg_dump("d_xn1", P.dsc["xn1_tok"], [("xn1d", g) for g in range(BL * NT)])
        P.dbg_dump("d_zf", P.dsc["z_feat"][:, 0:TC], [("zd", 0, 0)])
        P.emit_ffn(1)
        P.emit_finish()
        P.sched.finalize()
        block = ctx.enter_context(nc.Block())

        @block.sync
        def _(e):
            P.sched.replay("sync", e)

        @block.vector
        def _(e):
            P.sched.replay("vector", e)

        @block.scalar
        def _(e):
            P.sched.replay("scalar", e)

        @block.tensor
        def _(e):
            P.sched.replay("tensor", e)

        @block.gpsimd
        def _(e):
            P.sched.replay("gpsimd", e)
    return nc


class Prog:
    def __init__(self, nc, ctx):
        self.nc = nc
        self.ctx = ctx
        self.sched = Sched(nc, ctx)

    def declare(self):
        nc = self.nc
        ap = lambda t: t.ap()
        self.x_in = ap(nc.dram_tensor("x_in", [TLOC, D], F32, kind="ExternalInput"))
        self.x_out = ap(nc.dram_tensor("x_out", [TLOC, D], F32, kind="ExternalOutput"))

        self.din = {}
        for name, shape, dt in [
            ("w_inT", [D, E2], F32), ("w_dtT", [D, D], F32), ("w_outT", [D, D], F32),
            ("w_bT", [D, NST], F32), ("w_cT", [D, NST], F32), ("routT", [D, EXP], F32),
            ("sw1T", [D, SF], F32), ("sw2T", [SF, D], F32),
            ("w1", [EXP, D, FH], F32), ("w2", [EXP, FH, D], F32),
            ("conv_w", [D, 4], F32), ("conv_b", [D, 1], F32), ("b_dt", [D, 1], F32),
            ("dp", [D, 1], F32), ("a_log", [1, NST], F32), ("bias_bc", [128, EXP], F32),
            ("ones_col_f", [1, 128], F32), ("iota384_bc", [128, EXP], F32),
            ("iotaE_bc", [128, EXP], F32), ("oor_bc", [128, EXP], F32),
            ("rankmask", [NCORE, 1], F32), ("kcol", [128, 2], F32),
            ("lt128f", [128, 128], F32), ("ones128f", [128, 1], F32),
            ("ones_col_b", [1, 128], BF16), ("onessq_b", [128, 128], BF16),
            ("ltstrict_b", [128, 128], BF16), ("ident_b", [128, 128], BF16),
            ("onehot_b", [EXP, NST * 128], BF16),
            ("lng_row", [128, D], BF16), ("lnb_row", [128, D], BF16),
        ]:
            self.din[name] = ap(nc.dram_tensor(name, shape, dt, kind="ExternalInput"))

        dsc = {}
        for name, shape, dt in [
            ("resid", [TLOC, D], F32), ("xn_tok", [TLOC, D], BF16),
            ("xn1_tok", [TLOC, D], BF16), ("xn_feat", [BL * D, S], BF16),
            ("z_feat", [D, BL * S], BF16),
            ("xs_loc", [NSLOT + 1, D], BF16), ("w_tab", [NSLOT + 1, 1], F32),
            ("y_tab", [NSLOT + 1, D], BF16),
            ("bw_inT", [D, E2], BF16), ("bw_dtT", [D, D], BF16),
            ("bw_outT", [D, D], BF16), ("bw_bT", [D, NST], BF16),
            ("bw_cT", [D, NST], BF16), ("broutT", [D, EXP], BF16),
            ("bsw1T", [D, SF], BF16), ("bsw2T", [SF, D], BF16),
            ("bw1", [EXP, 128, DG * FH], BF16), ("bw2", [EXP, 128, (FH // 128) * D], BF16),
            ("cc_in", [1, 128], F32), ("cc_out", [NCORE, 128], F32),
            ("sh_tok", [TLOC, D], BF16), ("cnts_d", [BL * NT, EXP], F32),
        ]:
            dsc[name] = ap(nc.dram_tensor(name, shape, dt))
        self.dsc = dsc
        self.debug = os.environ.get("BK_DEBUG", "") == "1"
        self.dbg = {}
        if self.debug:
            for name, shape, dt in [
                ("d_xffn1", [TLOC, D], F32), ("d_xmam", [TLOC, D], F32),
                ("d_xntok", [TLOC, D], BF16), ("d_ccout", [NCORE, 128], F32),
                ("d_wtab", [NSLOT + 1, 1], F32), ("d_xs", [512, D], BF16),
                ("d_ytab", [512, D], BF16), ("d_rece", [128, BL * NT * TOPK], F32),
                ("d_recp", [128, BL * NT * TOPK], F32),
                ("d_recw", [128, BL * NT * TOPK], F32),
                ("d_xc", [D, TC], BF16), ("d_dt", [D, TC], F16),
                ("d_bm", [NST, TC], BF16), ("d_ytot", [D, TC], F32),
                ("d_xn1", [TLOC, D], BF16), ("d_zf", [D, TC], BF16),
                ("d_ytf", [NSLOT + 1, D], BF16),
                ("d_slot", [128, BL * NT * TOPK], I32),
                ("d_cnts", [BL * NT, EXP], F32), ("d_rc", [64, EXP], F32),
            ]:
                self.dbg[name] = ap(nc.dram_tensor(name, shape, dt, kind="ExternalOutput"))

        sb = lambda n, s, d: self.ctx.enter_context(nc.sbuf_tensor(n, s, d))
        self.c_onescol_b = sb("c_onescol_b", [1, 128], BF16)
        self.c_onescol_f = sb("c_onescol_f", [1, 128], F32)
        self.c_onessq = sb("c_onessq", [128, 128], BF16)
        self.c_lt = sb("c_lt", [128, 128], BF16)
        self.c_id = sb("c_id", [128, 128], BF16)
        self.c_iota384 = sb("c_iota384", [128, EXP], F32)
        self.c_iotaE = sb("c_iotaE", [128, EXP], F32)
        self.c_oor = sb("c_oor", [128, EXP], F32)
        self.c_bias = sb("c_bias", [128, EXP], F32)
        self.c_rank = sb("c_rank", [NCORE, 1], F32)
        self.c_kcol = sb("c_kcol", [128, 2], F32)
        self.c_ltf = sb("c_ltf", [128, 128], F32)
        self.c_ones128f = sb("c_ones128f", [128, 1], F32)
        self.c_lng = sb("c_lng", [128, D], BF16)
        self.c_lnb = sb("c_lnb", [128, D], BF16)
        self.c_oneh = sb("c_oneh", [EXP, NST * 128], BF16)
        self.p_convw = sb("p_convw", [128, 4 * DG], F32)
        self.p_convb = sb("p_convb", [128, DG], F32)
        self.p_bdt = sb("p_bdt", [128, DG], F32)
        self.p_dp = sb("p_dp", [128, DG], F32)
        self.p_A = sb("p_A", [128, NST], F32)
        self.wr_rt = sb("wr_rt", [128, DG * EXP], BF16)
        self.t_base = sb("t_base", [128, EXP], F32)
        self.row_a = sb("row_a", [1, 128], F32)
        self.row_b = sb("row_b", [1, 128], F32)
        self.row_c = sb("row_c", [NCORE, 128], F32)
        self.rec_e = sb("rec_e", [128, BL * NT * TOPK], F32)
        self.rec_pos = sb("rec_pos", [128, BL * NT * TOPK], F32)
        self.rec_w = sb("rec_w", [128, BL * NT * TOPK], F32)
        self.rec_slot = sb("rec_slot", [128, BL * NT * TOPK], I32)
        self.idx8u = [sb("idx8u0", [128, 8], U32), sb("idx8u1", [128, 8], U32)]
        self.ar_b = sb("ar_b", [128, 43584], BF16)
        self.ar_f = sb("ar_f", [128, 12608], F32)
        self.ar_h = sb("ar_h", [128, 8192], F16)
        self.ar_w = sb("ar_w", [128, 12288], BF16)
        ps = lambda n, s, d=F32: self.ctx.enter_context(nc.psum_tensor(n, s, d))
        self.ps_a = ps("ps_a", [128, 512])
        self.ps_b = ps("ps_b", [128, 512])
        self.ps_c = ps("ps_c", [128, 512])
        self.ps_d = ps("ps_d", [128, 512])
        self.ps_e = ps("ps_e", [128, 512])
        self.ps_f = ps("ps_f", [128, 512])
        self.ps_t1 = ps("ps_t1", [128, 1024], BF16)
        self.ps_t2 = ps("ps_t2", [128, 1024], BF16)
        self._cur = {}

    def arena(self, which, cols, phase):
        key = (which, phase)
        cur = self._cur.get(key, 0)
        t = {"b": self.ar_b, "f": self.ar_f, "h": self.ar_h, "w": self.ar_w}[which]
        if cur + cols > t.shape[1]:
            raise RuntimeError(f"arena {which} overflow in {phase}: {cur}+{cols}>{t.shape[1]}")
        self._cur[key] = cur + cols
        return t[:, cur:cur + cols]

    def dma(self, out, in_, reads=(), writes=(), deps=(), q="sync", tag=""):
        return self.sched.op(q, lambda e: e.dma_start(out=out, in_=in_),
                             reads=reads, writes=writes, deps=deps, dma=True, tag=tag)

    def dbg_dump(self, name, src_ap, reads, tag="dbg"):
        if not self.debug:
            return
        self.dma(self.dbg[name], src_ap, reads=reads, writes=[("dbg", name)],
                 tag=tag)

    def emit_setup(self):
        sc = self.sched
        din = self.din
        for t, key in [
            (self.c_onescol_b, "ones_col_b"), (self.c_onescol_f, "ones_col_f"),
            (self.c_onessq, "onessq_b"), (self.c_lt, "ltstrict_b"),
            (self.c_id, "ident_b"), (self.c_iota384, "iota384_bc"),
            (self.c_iotaE, "iotaE_bc"), (self.c_oor, "oor_bc"),
            (self.c_bias, "bias_bc"), (self.c_rank, "rankmask"),
            (self.c_lng, "lng_row"), (self.c_lnb, "lnb_row"),
            (self.c_oneh, "onehot_b"), (self.c_kcol, "kcol"),
            (self.c_ltf, "lt128f"), (self.c_ones128f, "ones128f"),
        ]:
            self.dma(t[:], din[key], writes=[key], tag="cld")
        for g in range(DG):
            self.dma(self.p_convw[:, 4 * g:4 * g + 4],
                     din["conv_w"][g * 128:(g + 1) * 128, :], writes=["p_convw"])
            self.dma(self.p_convb[:, g:g + 1],
                     din["conv_b"][g * 128:(g + 1) * 128, :], writes=["p_convb"])
            self.dma(self.p_bdt[:, g:g + 1],
                     din["b_dt"][g * 128:(g + 1) * 128, :], writes=["p_bdt"])
            self.dma(self.p_dp[:, g:g + 1],
                     din["dp"][g * 128:(g + 1) * 128, :], writes=["p_dp"])
        self.dma(self.row_a[:, 0:NST], din["a_log"], writes=["row_a"])
        sc.op("scalar", lambda e: e.activation(out=self.row_b[:, 0:NST],
                                               in_=self.row_a[:, 0:NST], func=AF.Exp),
              reads=["row_a"], writes=["row_b"], tag="Aexp")
        sc.op("vector", lambda e: e.tensor_scalar_mul(out=self.row_b[:, 0:NST],
                                                      in0=self.row_b[:, 0:NST],
                                                      scalar1=-1.0),
              reads=["row_b"], writes=["row_b"], tag="Aneg")
        sc.op("tensor", lambda e: e.matmul(out=self.ps_a[:, 0:NST],
                                           lhsT=self.c_onescol_f[:],
                                           rhs=self.row_b[:, 0:NST],
                                           start=True, stop=True),
              reads=["row_b", "ones_col_f"], writes=["ps_a"], tag="Abc")
        fA = sc.op("tensor", lambda e: e.matmul(out=self.ps_b[:, 0:NST],
                                                lhsT=self.c_onescol_f[:],
                                                rhs=self.row_b[:, 0:NST],
                                                start=True, stop=True),
                   reads=["row_b", "ones_col_f"], writes=["ps_b"], tag="Abc2")
        sc.op("vector", lambda e: e.tensor_copy(out=self.p_A[:], in_=self.ps_a[:, 0:NST]),
              reads=["ps_a"], writes=["p_A", "ps_a", "ps_b"], deps=[fA], tag="Aevac")

        stage_f = self.arena("f", 4096, "setup")
        stage_b = self.arena("w", 4096, "setup")
        for (src, dst, R, C) in [
            ("w_inT", "bw_inT", D, E2), ("w_dtT", "bw_dtT", D, D),
            ("w_outT", "bw_outT", D, D), ("w_bT", "bw_bT", D, NST),
            ("w_cT", "bw_cT", D, NST), ("routT", "broutT", D, EXP),
            ("sw1T", "bsw1T", D, SF), ("sw2T", "bsw2T", SF, D),
        ]:
            sa = self.din[src].rearrange("(a p) c -> a p c", p=128)
            da = self.dsc[dst].rearrange("(a p) c -> a p c", p=128)
            for a in range(R // 128):
                step = min(4096, C)
                for c0 in range(0, C, step):
                    n = min(step, C - c0)
                    self.dma(stage_f[:, 0:n], sa[a, :, c0:c0 + n],
                             writes=["wc_f"], tag="wcld")
                    sc.op("vector", lambda e, n=n: e.tensor_copy(
                        out=stage_b[:, 0:n], in_=stage_f[:, 0:n]),
                        reads=["wc_f"], writes=["wc_b"], tag="wccast")
                    self.dma(da[a, :, c0:c0 + n], stage_b[:, 0:n],
                             reads=["wc_b"], writes=[("bwd", dst)], tag="wcst")
        zrow = self.arena("b", 1024, "setup")
        sc.op("vector", lambda e: e.memset(zrow[0:1, :], 0.0),
              writes=["zrow"], tag="zrow")
        self.dma(self.dsc["y_tab"][NSLOT:NSLOT + 1, :], zrow[0:1, :],
                 reads=["zrow"], writes=["ytab_trash"], tag="ztrash")
        rt8 = self.dsc["broutT"].rearrange("(a p) c -> a p c", p=128)
        for a in range(DG):
            self.dma(self.wr_rt[:, a * EXP:(a + 1) * EXP], rt8[a, :, :],
                     reads=[("bwd", "broutT")], writes=["wr_rt"], tag="rtld")

    def _ln_tile(self, xin, kx, st, kst, sq, ksq, xout, kxo):
        for _ in self._ln_tile_gen(xin, kx, st, kst, sq, ksq, xout, kxo):
            pass

    def _ln_tile_gen(self, xin, kx, st, kst, sq, ksq, xout, kxo):
        sc = self.sched
        yield sc.op("vector", lambda e: e.reduce_sum(out=st[:, 0:1], in_=xin[:], axis=AX.X),
              reads=[kx], writes=[kst], tag="ln_sum")
        yield sc.op("vector", lambda e: e.tensor_mul(out=sq[:], in0=xin[:], in1=xin[:]),
            reads=[kx], writes=[ksq], tag="ln_sq")
        yield sc.op("vector", lambda e: e.reduce_sum(out=st[:, 1:2], in_=sq[:], axis=AX.X),
            reads=[ksq], writes=[kst], tag="ln_sq2")
        yield sc.op("vector", lambda e: e.tensor_scalar_mul(
            out=st[:, 2:3], in0=st[:, 0:1], scalar1=-1.0 / D),
            reads=[kst], writes=[kst], tag="ln_mu")
        yield sc.op("vector", lambda e: e.scalar_tensor_tensor(
            out=st[:, 3:4], in0=st[:, 2:3], scalar=1.0, in1=st[:, 2:3],
            op0=ALU.mult, op1=ALU.mult),
            reads=[kst], writes=[kst], tag="ln_musq")
        yield sc.op("vector", lambda e: e.tensor_scalar(
            out=st[:, 1:2], in0=st[:, 1:2], scalar1=1.0 / D, scalar2=None,
            op0=ALU.mult),
            reads=[kst], writes=[kst], tag="ln_ex2")
        yield sc.op("vector", lambda e: e.tensor_sub(
            out=st[:, 1:2], in0=st[:, 1:2], in1=st[:, 3:4]),
            reads=[kst], writes=[kst], tag="ln_var")
        yield sc.op("scalar", lambda e: e.activation(
            out=st[:, 4:5], in_=st[:, 1:2], func=AF.Ln,
            bias=self.c_kcol[:, 0:1]),
            reads=[kst, "kcol"], writes=[kst], tag="ln_ln")
        yield sc.op("scalar", lambda e: e.activation(
            out=st[:, 5:6], in_=st[:, 4:5], func=AF.Exp, scale=-0.5),
            reads=[kst], writes=[kst], tag="ln_rstd")
        yield sc.op("vector", lambda e: e.tensor_scalar(
            out=xout[:], in0=xin[:], scalar1=st[:, 2:3], scalar2=st[:, 5:6],
            op0=ALU.add, op1=ALU.mult),
            reads=[kx, kst], writes=[kxo], tag="ln_apply")

    def _gelu(self, pp, ppk, cols, gxe, kgx, gsc_, kgs, gu, kgu, outap, kout):
        sc = self.sched
        sc.op("scalar", lambda e: e.activation(out=gxe[:, 0:cols], in_=pp[:, 0:cols],
                                               func=AF.Copy),
              reads=[ppk], writes=[kgx], tag="ge_cp")
        sc.op("scalar", lambda e: e.activation(out=gsc_[:, 0:cols], in_=pp[:, 0:cols],
                                               func=AF.Square),
              reads=[ppk], writes=[kgs, ppk], tag="ge_sq")
        sc.op("vector", lambda e: e.tensor_scalar(
            out=gu[:, 0:cols], in0=gsc_[:, 0:cols], scalar1=GELU_B, scalar2=GELU_A,
            op0=ALU.mult, op1=ALU.add),
            reads=[kgs], writes=[kgu], tag="ge_u")
        sc.op("vector", lambda e: e.tensor_mul(
            out=gu[:, 0:cols], in0=gu[:, 0:cols], in1=gxe[:, 0:cols]),
            reads=[kgu, kgx], writes=[kgu], tag="ge_arg")
        sc.op("scalar", lambda e: e.activation(out=gu[:, 0:cols], in_=gu[:, 0:cols],
                                               func=AF.Sigmoid),
              reads=[kgu], writes=[kgu], tag="ge_sig")
        sc.op("vector", lambda e: e.tensor_mul(
            out=outap, in0=gu[:, 0:cols], in1=gxe[:, 0:cols]),
            reads=[kgu, kgx], writes=[kout], tag="ge_mul")

    def emit_ffn(self, call):
        sc = self.sched
        ph = f"ffn{call}"
        A = self.arena
        src_x = self.x_in if call == 0 else self.dsc["resid"]
        dst_x = self.dsc["resid"] if call == 0 else self.x_out
        src_key = "x_in" if call == 0 else "resid1"
        dst_key = "resid1" if call == 0 else "xout"

        xb = [A("f", 1024, ph) for _ in range(2)]
        stat = [A("f", 8, ph) for _ in range(2)]
        sbt = [A("f", 66, ph) for _ in range(2)]
        sct = [A("f", 64, ph) for _ in range(2)]
        sbm = [A("f", 66, ph) for _ in range(2)]
        m8t = [A("f", 8, ph) for _ in range(2)]
        scot = [A("f", 8, ph) for _ in range(2)]
        idx8 = self.idx8u
        pbt = [A("f", 8, ph) for _ in range(2)]
        vmt = [A("f", 8, ph) for _ in range(2)]
        slf = [A("f", 8, ph) for _ in range(2)]
        wkc = [A("f", 8, ph) for _ in range(2)]
        ccs = A("f", 128, ph)
        gsc_ = [A("f", 512, ph) for _ in range(2)]
        gsf = [A("f", 64, ph) for _ in range(2)]

        sqs = [A("b", 1024, ph) for _ in range(2)]
        xnt = [A("b", 1024, ph) for _ in range(2)]
        xnf = [A("b", 1024, ph) for _ in range(2)]
        oht = [A("f", 64, ph) for _ in range(2)]
        wsct = [A("f", 64, ph) for _ in range(2)]
        xtk = [A("b", 1024, ph) for _ in range(2)]
        xstk = [A("b", 1024, ph) for _ in range(2)]
        xsT = [A("b", 3072, ph) for _ in range(2)]
        hidE = A("b", 1536, ph)
        ytk = [A("b", 1024, ph) for _ in range(2)]
        gxe = [A("b", 512, ph) for _ in range(2)]
        gub = [A("b", 512, ph) for _ in range(2)]
        xnfr = A("b", 4096, ph)
        hidS = A("b", 8192, ph)
        sht = [A("b", 1024, ph) for _ in range(4)]
        macc = [A("f", 1024, ph) for _ in range(2)]
        gbuf = [A("b", 1024, ph) for _ in range(4)]
        w1b = self.ar_w[:, 0:4096]
        w2b = self.ar_w[:, 4096:8192]
        sw1b = [self.ar_w[:, 8192:10240], self.ar_w[:, 10240:12288]]
        w1o = self.ar_w[:, 8192:12288]

        xn_tok_d = self.dsc["xn_tok"]
        xn_feat_d = self.dsc["xn_feat"]

        sc.op("vector", lambda e: e.memset(sbt[0][:, 64:66], BIGF),
              writes=[f"{ph}_sbp0"], tag="sbpad")
        sc.op("vector", lambda e: e.memset(sbt[1][:, 64:66], BIGF),
              writes=[f"{ph}_sbp1"], tag="sbpad")

        loc_all = A("h", BL * NT * EXP, ph)
        cnts_row = A("f", BL * NT * EXP, ph)
        rcst = [A("f", 384, ph) for _ in range(2)]
        tb1 = [A("f", 64, ph) for _ in range(2)]
        tb2 = [A("f", 64, ph) for _ in range(2)]
        rc6 = [A("f", 8, ph) for _ in range(2)]

        def passa_tile(g):
            s, i = g // NT, g % NT
            r = g % 2
            gk = g * TOPK
            row0 = g * 128
            kx = f"{ph}_xb{r}"
            yield self.dma(xb[r][:], src_x[row0:row0 + 128, :],
                           reads=[src_key], writes=[kx], tag="ldx")
            for _o in self._ln_tile_gen(xb[r], kx, stat[r], f"{ph}_st{r}",
                                        sqs[r], f"{ph}_sq{r}", xnt[r], f"{ph}_xnt{r}"):
                yield _o
            yield self.dma(xn_tok_d[row0:row0 + 128, :], xnt[r][:],
                           reads=[f"{ph}_xnt{r}"], writes=[("xntd", g)], tag="stxn")
            for gd in range(DG):
                yield sc.op("tensor", lambda e, r=r, gd=gd: e.transpose(
                    out=self.ps_t1[:, gd * 128:(gd + 1) * 128] if r == 0 else
                        self.ps_t2[:, gd * 128:(gd + 1) * 128],
                    in_=xnt[r][:, gd * 128:(gd + 1) * 128], identity=self.c_id[:]),
                    reads=[f"{ph}_xnt{r}", "ident_b"], writes=[("pt", r, gd)],
                    tag="xT")
            yield sc.op("vector", lambda e, r=r: e.tensor_copy(
                out=xnf[r][:], in_=self.ps_t1[:] if r == 0 else self.ps_t2[:]),
                reads=[("pt", r, gd) for gd in range(DG)],
                writes=[f"{ph}_xnf{r}"] + [("pt", r, gd) for gd in range(DG)],
                tag="xTev")
            for gd in range(DG):
                yield self.dma(xn_feat_d[s * D + gd * 128: s * D + (gd + 1) * 128,
                                         i * 128:(i + 1) * 128],
                               xnf[r][:, gd * 128:(gd + 1) * 128],
                               reads=[f"{ph}_xnf{r}"], writes=[("xnfd", s, gd, i)],
                               tag="stxf")
            ppl, pplk = (self.ps_a, "ps_a") if r == 0 else (self.ps_b, "ps_b")
            for kd in range(DG):
                yield sc.op("tensor", lambda e, r=r, kd=kd, ppl=ppl: e.matmul(
                    out=ppl[:, 0:EXP],
                    lhsT=xnf[r][:, kd * 128:(kd + 1) * 128],
                    rhs=self.wr_rt[:, kd * EXP:(kd + 1) * EXP],
                    start=(kd == 0), stop=(kd == DG - 1)),
                    reads=[f"{ph}_xnf{r}", "wr_rt"], writes=[pplk], tag="rtmm")
            yield sc.op("scalar", lambda e, r=r, ppl=ppl: e.activation(
                out=sct[r][:], in_=ppl[:, 0:EXP], func=AF.Sigmoid),
                reads=[pplk], writes=[f"{ph}_sc{r}", pplk], tag="sigm")
            o_sb = sc.op("vector", lambda e, r=r: e.tensor_add(
                out=sbt[r][:, 0:EXP], in0=sct[r][:], in1=self.c_bias[:]),
                reads=[f"{ph}_sc{r}", "bias_bc", f"{ph}_sbp{r}"],
                writes=[f"{ph}_sbt{r}"], tag="sbadd")
            yield o_sb
            o_m8 = sc.op("vector", lambda e, r=r: e.max(
                out=m8t[r][:], in_=sbt[r][:]),
                reads=[f"{ph}_sbt{r}"], writes=[f"{ph}_m8{r}"], hard=[o_sb],
                tag="max8")
            yield o_m8
            o_mr = sc.op("vector", lambda e, r=r: e.match_replace(
                out=sbm[r][:], in_to_replace=m8t[r][:], in_values=sbt[r][:],
                imm_value=-BIGF),
                reads=[f"{ph}_m8{r}", f"{ph}_sbt{r}"], writes=[f"{ph}_sbm{r}"],
                hard=[o_m8], tag="matchrep")
            yield o_mr
            o_oh = sc.op("vector", lambda e, r=r: e.tensor_scalar(
                out=oht[r][:], in0=sbm[r][:, 0:EXP], scalar1=-BIGF, scalar2=None,
                op0=ALU.is_equal),
                reads=[f"{ph}_sbm{r}"], writes=[f"{ph}_oh{r}"], hard=[o_mr],
                tag="ohsum")
            yield o_oh
            yield sc.op("vector", lambda e, r=r: e.tensor_mul(
                out=wsct[r][:], in0=oht[r][:], in1=sct[r][:]),
                reads=[f"{ph}_oh{r}", f"{ph}_sc{r}"],
                writes=[f"{ph}_wsc{r}"], tag="wsc")
            yield sc.op("vector", lambda e, r=r: e.reduce_sum(
                out=stat[r][:, 6:7], in_=wsct[r][:], axis=AX.X),
                reads=[f"{ph}_wsc{r}"], writes=[f"{ph}_st{r}"], tag="wsc2")
            yield sc.op("vector", lambda e, r=r: e.tensor_scalar_add(
                out=stat[r][:, 6:7], in0=stat[r][:, 6:7], scalar1=1e-20),
                reads=[f"{ph}_st{r}"], writes=[f"{ph}_st{r}"], tag="rseps")
            yield sc.op("vector", lambda e, r=r: e.reciprocal(
                out=stat[r][:, 7:8], in_=stat[r][:, 6:7]),
                reads=[f"{ph}_st{r}"], writes=[f"{ph}_st{r}"], tag="rinv")
            ppc, ppck = (self.ps_c, "ps_c") if r == 0 else (self.ps_d, "ps_d")
            yield sc.op("tensor", lambda e, r=r, ppc=ppc: e.matmul(
                out=ppc[:, 0:EXP], lhsT=self.c_ltf[:], rhs=oht[r][:],
                start=True, stop=True),
                reads=[f"{ph}_oh{r}", "lt128f"], writes=[ppck], tag="ltmm")
            yield sc.op("tensor", lambda e, r=r, ppc=ppc: e.matmul(
                out=ppc[0:1, 256:256 + EXP], lhsT=self.c_ones128f[:],
                rhs=oht[r][:], start=True, stop=True),
                reads=[f"{ph}_oh{r}", "ones128f"], writes=[ppck], tag="cntmm")
            yield sc.op("vector", lambda e, g=g, ppc=ppc: e.tensor_copy(
                out=loc_all[:, g * EXP:(g + 1) * EXP], in_=ppc[:, 0:EXP]),
                reads=[ppck], writes=[("loc", g)], tag="locev")
            yield sc.op("vector", lambda e, g=g, ppc=ppc: e.tensor_copy(
                out=cnts_row[0:1, g * EXP:(g + 1) * EXP], in_=ppc[0:1, 256:256 + EXP]),
                reads=[ppck], writes=[("cnt", g), ppck], tag="cntev")
            yield sc.op("vector", lambda e, r=r: e.max_index(
                out=idx8[r][:], in_max=m8t[r][:], in_values=sbt[r][:]),
                reads=[f"{ph}_m8{r}", f"{ph}_sbt{r}"], writes=[f"{ph}_idx{r}"],
                hard=[o_oh], tag="maxidx")
            yield sc.op("vector", lambda e, r=r, gk=gk: e.tensor_copy(
                out=self.rec_e[:, gk:gk + TOPK], in_=idx8[r][:, 2:8]),
                reads=[f"{ph}_idx{r}"], writes=[("rec_e", g)], tag="idxcp")
            for k in range(TOPK):
                yield sc.op("vector", lambda e, r=r, k=k, gk=gk: e.scalar_tensor_tensor(
                    out=gsf[r][:], in0=self.c_iotaE[:],
                    scalar=self.rec_e[:, gk + k:gk + k + 1],
                    in1=loc_all[:, (gk // TOPK) * EXP:(gk // TOPK + 1) * EXP],
                    op0=ALU.is_equal, op1=ALU.mult),
                    reads=[("rec_e", g), ("loc", g), "iotaE_bc"],
                    writes=[f"{ph}_gsf{r}"], tag="gpos")
                yield sc.op("vector", lambda e, r=r, k=k, gk=gk: e.reduce_sum(
                    out=self.rec_pos[:, gk + k:gk + k + 1], in_=gsf[r][:],
                    axis=AX.X),
                    reads=[f"{ph}_gsf{r}"], writes=[("rec_pos", g)], tag="gpos2r")
                yield sc.op("vector", lambda e, r=r, k=k: e.scalar_tensor_tensor(
                    out=gsf[r][:], in0=sbt[r][:, 0:EXP],
                    scalar=m8t[r][:, 2 + k:3 + k], in1=sct[r][:],
                    op0=ALU.is_equal, op1=ALU.mult),
                    reads=[f"{ph}_sbt{r}", f"{ph}_m8{r}", f"{ph}_sc{r}"],
                    writes=[f"{ph}_gsf{r}"], tag="gsco")
                yield sc.op("vector", lambda e, r=r, k=k: e.reduce_sum(
                    out=scot[r][:, k:k + 1], in_=gsf[r][:], axis=AX.X),
                    reads=[f"{ph}_gsf{r}"], writes=[f"{ph}_sco{r}"], tag="gsco2r")
            yield sc.op("vector", lambda e, r=r, gk=gk: e.tensor_scalar(
                out=self.rec_w[:, gk:gk + TOPK], in0=scot[r][:, 0:TOPK],
                scalar1=stat[r][:, 7:8], scalar2=None, op0=ALU.mult),
                reads=[f"{ph}_sco{r}", f"{ph}_st{r}"], writes=[("rec_w", g)],
                tag="wraw")

        for g0 in range(0, BL * NT, 2):
            gens = [passa_tile(g0), passa_tile(g0 + 1)]
            alive = [True, True]
            while any(alive):
                for j in range(2):
                    if alive[j]:
                        try:
                            next(gens[j])
                        except StopIteration:
                            alive[j] = False

        if call == 0:
            self.emit_wcast()

        NTT = BL * NT
        self.dma(self.dsc["cnts_d"].rearrange("a b -> (a b)").rearrange("(o c) -> o c", o=1),
                 cnts_row[0:1, :], reads=[("cnt", g) for g in range(NTT)],
                 writes=["cntd"], tag="cntst")
        cm_sb = A("f", 64, ph)
        self.dma(cm_sb[0:64, 0:EXP], self.dsc["cnts_d"],
                 reads=["cntd"], writes=["cntm"], tag="cntld")
        sc.op("tensor", lambda e: e.matmul(
            out=self.ps_e[0:64, 0:EXP], lhsT=self.c_ltf[0:64, 0:64],
            rhs=cm_sb[0:64, 0:EXP], start=True, stop=True),
            reads=["cntm", "lt128f"], writes=["ps_e"], tag="rcallmm")
        rc_sb = A("f", 64, ph)
        sc.op("vector", lambda e: e.tensor_copy(out=rc_sb[0:64, 0:EXP],
                                                in_=self.ps_e[0:64, 0:EXP]),
              reads=["ps_e"], writes=["rcsb", "ps_e"], tag="rcallev")
        if self.debug and call == 0:
            self.dma(self.dbg["d_rc"], rc_sb[0:64, 0:EXP], reads=["rcsb"],
                     writes=[("dbg", "rc")], tag="dbgrc")
        sc.op("tensor", lambda e: e.matmul(
            out=self.ps_f[0:1, 0:EXP], lhsT=self.c_ones128f[0:64, :],
            rhs=cm_sb[0:64, 0:EXP], start=True, stop=True),
            reads=["cntm", "ones128f"], writes=["ps_f"], tag="totmm")
        sc.op("vector", lambda e: e.memset(ccs[0:1, :], 0.0),
              writes=[f"{ph}_cc"], tag="cc0")
        sc.op("vector", lambda e: e.tensor_copy(out=ccs[0:1, 0:EXP],
                                                in_=self.ps_f[0:1, 0:EXP]),
              reads=["ps_f"], writes=[f"{ph}_cc", "ps_f"], tag="cccp")
        self.dma(self.dsc["cc_in"], ccs[0:1, :], reads=[f"{ph}_cc"],
                 writes=[("ccin", call)], tag="ccst")
        ccsem = self.ctx.enter_context(self.nc.semaphore(f"ccsem{call}"))

        if os.environ.get("BK_NOCC", "") == "1":
            for rr_ in range(NCORE):
                self.dma(self.dsc["cc_out"][rr_:rr_ + 1, :], self.dsc["cc_in"],
                         reads=[("ccin", call)], writes=[("ccout", call)],
                         q="gpsimd", tag="fakecc")
        else:
            def cc_fn(e, ccsem=ccsem):
                e.collective_compute(
                    "AllGather", ALU.bypass, replica_groups=[list(range(NCORE))],
                    ins=[self.dsc["cc_in"].opt()],
                    outs=[self.dsc["cc_out"].opt()]).then_inc(ccsem)
                e.wait_ge(ccsem, 1)
                return e.engine_nop()
            sc.op("gpsimd", cc_fn, reads=[("ccin", call)], writes=[("ccout", call)],
                  tag="allgather")
        self.dma(self.row_c[:], self.dsc["cc_out"], reads=[("ccout", call)],
                 writes=["row_c"], tag="ccld")
        sc.op("tensor", lambda e: e.matmul(
            out=self.ps_a[0:1, 0:128], lhsT=self.c_rank[:], rhs=self.row_c[:],
            start=True, stop=True),
            reads=["row_c", "rankmask"], writes=["ps_a"], tag="basemm")
        sc.op("vector", lambda e: e.tensor_copy(out=self.row_b[:],
                                                in_=self.ps_a[0:1, 0:128]),
              reads=["ps_a"], writes=["row_b", "ps_a"], tag="baseev")
        sc.op("tensor", lambda e: e.matmul(
            out=self.ps_a[:, 0:EXP], lhsT=self.c_onescol_f[:],
            rhs=self.row_b[:, 0:EXP], start=True, stop=True),
            reads=["row_b", "ones_col_f"], writes=["ps_a"], tag="basebc")
        fence = sc.op("tensor", lambda e: e.matmul(
            out=self.ps_b[:, 0:EXP], lhsT=self.c_onescol_f[:],
            rhs=self.row_b[:, 0:EXP], start=True, stop=True),
            reads=["row_b", "ones_col_f"], writes=["ps_b"], tag="basebc2")
        sc.op("vector", lambda e: e.tensor_copy(out=self.t_base[:],
                                                in_=self.ps_a[:, 0:EXP]),
              reads=["ps_a"], writes=["t_base", "ps_a", "ps_b"], deps=[fence],
              tag="basebcev")

        sw1_d = self.dsc["bsw1T"].rearrange("(a p) c -> a p c", p=128)
        sw2_d = self.dsc["bsw2T"].rearrange("(a p) c -> a p c", p=128)
        NSF = SF // 128
        for s in range(BL):
            for tc in range(4):
                for kd in range(DG):
                    self.dma(xnfr[:, kd * 512:(kd + 1) * 512],
                             self.dsc["xn_feat"][s * D + kd * 128: s * D + (kd + 1) * 128,
                                                 tc * 512:(tc + 1) * 512],
                             reads=[("xnfd", s, kd, tc * 4 + q) for q in range(4)],
                             writes=[(f"{ph}_xnfr", kd)], tag="shxld")
                xnfr_all = [(f"{ph}_xnfr", kd) for kd in range(DG)]
                for mf in range(NSF):
                    wb = sw1b[mf % 2]
                    wkey = f"{ph}_swb{mf % 2}"
                    for a in range(DG):
                        self.dma(wb[:, a * 128:(a + 1) * 128],
                                 sw1_d[a, :, mf * 128:(mf + 1) * 128],
                                 reads=[("bwd", "bsw1T")], writes=[wkey], tag="sw1ld")
                    pp, ppk = (self.ps_a, "ps_a") if mf % 2 == 0 else (self.ps_b, "ps_b")
                    for kd in range(DG):
                        sc.op("tensor", lambda e, kd=kd, wb=wb, pp=pp: e.matmul(
                            out=pp[:], lhsT=wb[:, kd * 128:(kd + 1) * 128],
                            rhs=xnfr[:, kd * 512:(kd + 1) * 512],
                            start=(kd == 0), stop=(kd == DG - 1)),
                            reads=xnfr_all + [wkey], writes=[ppk], tag="sw1mm")
                    rr = mf % 2
                    self._gelu(pp, ppk, 512, gxe[rr], f"{ph}_gx{rr}", gsc_[rr],
                               f"{ph}_gs{rr}", gub[rr], f"{ph}_gu{rr}",
                               hidS[:, mf * 512:(mf + 1) * 512], (f"{ph}_hidS", mf))
                hidS_all = [(f"{ph}_hidS", mf) for mf in range(NSF)]
                psb = [(self.ps_a, "ps_a"), (self.ps_b, "ps_b"),
                       (self.ps_c, "ps_c"), (self.ps_d, "ps_d")]
                for hhf in range(2):
                    wb = sw1b[hhf]
                    wkey = f"{ph}_swc{hhf}"
                    for grp in range(4):
                        for q in range(4):
                            kf = grp * 4 + q
                            self.dma(wb[:, q * 512:(q + 1) * 512],
                                     sw2_d[kf, :, hhf * 512:(hhf + 1) * 512],
                                     reads=[("bwd", "bsw2T")], writes=[(wkey, q)],
                                     tag="sw2ld")
                            for tt in range(4):
                                pp, ppk = psb[tt]
                                sc.op("tensor", lambda e, kf=kf, q=q, wb=wb, pp=pp,
                                      tt=tt: e.matmul(
                                          out=pp[:],
                                          lhsT=hidS[:, kf * 512 + tt * 128:
                                                    kf * 512 + (tt + 1) * 128],
                                          rhs=wb[:, q * 512:(q + 1) * 512],
                                          start=(kf == 0), stop=(kf == NSF - 1)),
                                      reads=hidS_all + [(wkey, q)], writes=[ppk],
                                      tag="sw2mm")
                    for tt in range(4):
                        g = s * NT + tc * 4 + tt
                        pp, ppk = psb[tt]
                        sc.op("vector", lambda e, g=g, hhf=hhf, pp=pp: e.tensor_copy(
                            out=sht[g % 4][:, hhf * 512:(hhf + 1) * 512], in_=pp[:]),
                            reads=[ppk], writes=[(f"{ph}_sht", g % 4), ppk],
                            tag="shev")
                for tt in range(4):
                    g = s * NT + tc * 4 + tt
                    row0 = g * 128
                    self.dma(self.dsc["sh_tok"][row0:row0 + 128, :], sht[g % 4][:],
                             reads=[(f"{ph}_sht", g % 4)], writes=[("shd", call, g)],
                             tag="shst")

        def passb_tile(g):
            r = g % 2
            gk = g * TOPK
            row0 = g * 128
            yield self.dma(xtk[r][:], xn_tok_d[row0:row0 + 128, :],
                           reads=[("xntd", g)], writes=[f"{ph}_xtk{r}"], tag="ldxtk")
            yield self.dma(rcst[r][0:1, 0:EXP], rc_sb[g:g + 1, 0:EXP],
                           reads=["rcsb"], writes=[(f"{ph}_rcst", r)], tag="rcmv")
            ppb, ppbk = (self.ps_e, "ps_e") if r == 0 else (self.ps_f, "ps_f")
            yield sc.op("tensor", lambda e, r=r, ppb=ppb: e.matmul(
                out=ppb[:, 0:384], lhsT=self.c_onescol_f[:], rhs=rcst[r][0:1, 0:384],
                start=True, stop=True),
                reads=[(f"{ph}_rcst", r), "ones_col_f"], writes=[ppbk], tag="rcbc")
            yield sc.op("vector", lambda e, r=r, ppb=ppb: e.tensor_add(
                out=tb1[r][:], in0=self.c_iota384[:], in1=ppb[:, 0:EXP]),
                reads=[ppbk, "iota384_bc"], writes=[(f"{ph}_tb1", r)], tag="tbl1")
            yield sc.op("vector", lambda e, r=r, ppb=ppb: e.tensor_add(
                out=tb2[r][:], in0=self.t_base[:], in1=ppb[:, 0:EXP]),
                reads=[ppbk, "t_base"], writes=[(f"{ph}_tb2", r), ppbk], tag="tbl2")
            for k in range(TOPK):
                yield sc.op("vector", lambda e, r=r, k=k, gk=gk: e.scalar_tensor_tensor(
                    out=gsf[r][:], in0=self.c_iotaE[:],
                    scalar=self.rec_e[:, gk + k:gk + k + 1], in1=tb1[r][:],
                    op0=ALU.is_equal, op1=ALU.mult),
                    reads=[("rec_e", g), (f"{ph}_tb1", r), "iotaE_bc"],
                    writes=[f"{ph}_gsf{r}"], tag="gslot")
                yield sc.op("vector", lambda e, r=r, k=k: e.reduce_sum(
                    out=rc6[r][:, k:k + 1], in_=gsf[r][:], axis=AX.X),
                    reads=[f"{ph}_gsf{r}"], writes=[(f"{ph}_rc6", r)], tag="gslot2r")
                yield sc.op("vector", lambda e, r=r, k=k, gk=gk: e.scalar_tensor_tensor(
                    out=gsf[r][:], in0=self.c_iotaE[:],
                    scalar=self.rec_e[:, gk + k:gk + k + 1], in1=tb2[r][:],
                    op0=ALU.is_equal, op1=ALU.mult),
                    reads=[("rec_e", g), (f"{ph}_tb2", r), "iotaE_bc"],
                    writes=[f"{ph}_gsf{r}"], tag="gbase")
                yield sc.op("vector", lambda e, r=r, k=k: e.reduce_sum(
                    out=pbt[r][:, k:k + 1], in_=gsf[r][:], axis=AX.X),
                    reads=[f"{ph}_gsf{r}"], writes=[f"{ph}_pb{r}"], tag="gbase2r")
            yield sc.op("vector", lambda e, r=r, gk=gk: e.tensor_add(
                out=pbt[r][:, 0:TOPK], in0=pbt[r][:, 0:TOPK],
                in1=self.rec_pos[:, gk:gk + TOPK]),
                reads=[f"{ph}_pb{r}", ("rec_pos", g)], writes=[f"{ph}_pb{r}"],
                tag="gpos2")
            yield sc.op("vector", lambda e, r=r: e.tensor_scalar(
                out=vmt[r][:, 0:TOPK], in0=pbt[r][:, 0:TOPK],
                scalar1=float(CAP), scalar2=None, op0=ALU.is_lt),
                reads=[f"{ph}_pb{r}"], writes=[f"{ph}_vm{r}"], tag="vmask")
            yield sc.op("vector", lambda e, r=r, gk=gk: e.tensor_mul(
                out=self.rec_w[:, gk:gk + TOPK], in0=vmt[r][:, 0:TOPK],
                in1=self.rec_w[:, gk:gk + TOPK]),
                reads=[f"{ph}_vm{r}", ("rec_w", g)], writes=[("rec_w", g)],
                tag="wfin")
            yield sc.op("vector", lambda e, r=r, gk=gk: e.tensor_add(
                out=slf[r][:, 0:TOPK], in0=rc6[r][:, 0:TOPK],
                in1=self.rec_pos[:, gk:gk + TOPK]),
                reads=[(f"{ph}_rc6", r), ("rec_pos", g)], writes=[f"{ph}_slf{r}"],
                tag="slotp")
            yield sc.op("vector", lambda e, r=r: e.tensor_scalar_add(
                out=slf[r][:, 0:TOPK], in0=slf[r][:, 0:TOPK],
                scalar1=-float(NSLOT)),
                reads=[f"{ph}_slf{r}"], writes=[f"{ph}_slf{r}"], tag="slotsub")
            yield sc.op("vector", lambda e, r=r: e.tensor_mul(
                out=slf[r][:, 0:TOPK], in0=slf[r][:, 0:TOPK],
                in1=vmt[r][:, 0:TOPK]),
                reads=[f"{ph}_slf{r}", f"{ph}_vm{r}"],
                writes=[f"{ph}_slf{r}"], tag="slotmul")
            yield sc.op("vector", lambda e, r=r: e.tensor_scalar_add(
                out=slf[r][:, 0:TOPK], in0=slf[r][:, 0:TOPK],
                scalar1=float(NSLOT)),
                reads=[f"{ph}_slf{r}"], writes=[f"{ph}_slf{r}"], tag="slotadd")
            yield sc.op("vector", lambda e, r=r, gk=gk: e.tensor_copy(
                out=self.rec_slot[:, gk:gk + TOPK], in_=slf[r][:, 0:TOPK]),
                reads=[f"{ph}_slf{r}"], writes=[("rec_slot", g)], tag="sloti")
            for k in range(TOPK):
                yield sc.op("gpsimd", lambda e, r=r, gk=gk, k=k: e.indirect_dma_start(
                    out=self.dsc["xs_loc"],
                    out_offset=IndirectOffsetOnAxis(
                        ap=self.rec_slot[:, gk + k:gk + k + 1], axis=0),
                    in_=xtk[r][:], in_offset=None),
                    reads=[f"{ph}_xtk{r}", ("rec_slot", g)],
                    writes=[("xs", call)], dma=True, tag="scat")

        for g0 in range(0, BL * NT, 2):
            gens = [passb_tile(g0), passb_tile(g0 + 1)]
            alive = [True, True]
            while any(alive):
                for j in range(2):
                    if alive[j]:
                        try:
                            next(gens[j])
                        except StopIteration:
                            alive[j] = False

        sc.op("gpsimd", lambda e: e.engine_nop(),
              reads=[("xs", call)], writes=[("xs_done", call)], tag="xsbar")

        self._emit_experts(call, ph, xstk, xsT, hidE, ytk,
                           gxe, gsc_, gub, w1b, w1o, w2b)

        def combine_tile(g):
            r = g % 2
            gk = g * TOPK
            row0 = g * 128
            for k in range(TOPK):
                rb = (g % 2) * 2 + k % 2
                yield sc.op("gpsimd", lambda e, rb=rb, gk=gk, k=k: e.indirect_dma_start(
                    out=gbuf[rb][:], out_offset=None,
                    in_=self.dsc["y_tab"],
                    in_offset=IndirectOffsetOnAxis(
                        ap=self.rec_slot[:, gk + k:gk + k + 1], axis=0)),
                    reads=[("ytab_done", call), ("rec_slot", g), "ytab_trash"],
                    writes=[(f"{ph}_gb", rb)], dma=True, tag="moegat")
                if k == 0:
                    yield sc.op("vector", lambda e, r=r, rb=rb, gk=gk, k=k:
                                e.tensor_scalar(
                                    out=macc[r][:], in0=gbuf[rb][:],
                                    scalar1=self.rec_w[:, gk + k:gk + k + 1],
                                    scalar2=None, op0=ALU.mult),
                                reads=[(f"{ph}_gb", rb), ("rec_w", g)],
                                writes=[f"{ph}_macc{r}"], tag="mgw")
                else:
                    yield sc.op("vector", lambda e, r=r, rb=rb, gk=gk, k=k:
                                e.scalar_tensor_tensor(
                                    out=macc[r][:], in0=gbuf[rb][:],
                                    scalar=self.rec_w[:, gk + k:gk + k + 1],
                                    in1=macc[r][:], op0=ALU.mult, op1=ALU.add),
                                reads=[(f"{ph}_gb", rb), ("rec_w", g),
                                       f"{ph}_macc{r}"],
                                writes=[f"{ph}_macc{r}"], tag="mgw")
            yield self.dma(xb[r][:], src_x[row0:row0 + 128, :],
                           reads=[src_key], writes=[f"{ph}_xb{r}"], tag="ldx2")
            yield self.dma(sht[g % 4][:], self.dsc["sh_tok"][row0:row0 + 128, :],
                           reads=[("shd", call, g)], writes=[(f"{ph}_sht", g % 4)],
                           tag="shld")
            yield sc.op("vector", lambda e, r=r, g=g: e.tensor_add(
                out=macc[r][:], in0=macc[r][:], in1=sht[g % 4][:]),
                reads=[f"{ph}_macc{r}", (f"{ph}_sht", g % 4)],
                writes=[f"{ph}_macc{r}"], tag="cmb1")
            yield sc.op("vector", lambda e, r=r: e.scalar_tensor_tensor(
                out=xb[r][:], in0=macc[r][:], scalar=0.5, in1=xb[r][:],
                op0=ALU.mult, op1=ALU.add),
                reads=[f"{ph}_macc{r}", f"{ph}_xb{r}"], writes=[f"{ph}_xb{r}"],
                tag="cmb2")
            yield self.dma(dst_x[row0:row0 + 128, :], xb[r][:],
                           reads=[f"{ph}_xb{r}"], writes=[dst_key], tag="stx2")

        for g0 in range(0, BL * NT, 2):
            gens = [combine_tile(g0), combine_tile(g0 + 1)]
            alive = [True, True]
            while any(alive):
                for j in range(2):
                    if alive[j]:
                        try:
                            next(gens[j])
                        except StopIteration:
                            alive[j] = False
        return

    def _shared_sw2_combine(self, call, ph, s, tc, hidS, hidS_all, sw2_d,
                            swb, sht, macc, xb, src_x, dst_x, src_key, dst_key):
        sc = self.sched
        NSF = SF // 128
        psb = [(self.ps_a, "ps_a"), (self.ps_b, "ps_b"),
               (self.ps_c, "ps_c"), (self.ps_d, "ps_d")]
        for hhf in range(2):
            wb = swb[hhf]
            wkey = f"{ph}_swc{hhf}"
            for grp in range(4):
                for q in range(4):
                    kf = grp * 4 + q
                    self.dma(wb[:, q * 512:(q + 1) * 512],
                             sw2_d[kf, :, hhf * 512:(hhf + 1) * 512],
                             reads=[("bwd", "bsw2T")], writes=[(wkey, q)],
                             tag="sw2ld")
                    for tt in range(4):
                        pp, ppk = psb[tt]
                        sc.op("tensor", lambda e, kf=kf, q=q, wb=wb, pp=pp, tt=tt:
                              e.matmul(
                                  out=pp[:],
                                  lhsT=hidS[:, kf * 512 + tt * 128:
                                            kf * 512 + (tt + 1) * 128],
                                  rhs=wb[:, q * 512:(q + 1) * 512],
                                  start=(kf == 0), stop=(kf == NSF - 1)),
                              reads=hidS_all + [(wkey, q)], writes=[ppk],
                              tag="sw2mm")
            for tt in range(4):
                g = s * NT + tc * 4 + tt
                r = g % 2
                pp, ppk = psb[tt]
                sc.op("vector", lambda e, g=g, hhf=hhf, pp=pp: e.tensor_copy(
                    out=sht[g % 4][:, hhf * 512:(hhf + 1) * 512], in_=pp[:]),
                    reads=[ppk], writes=[(f"{ph}_sht", g % 4), ppk], tag="shev")
        for tt in range(4):
            g = s * NT + tc * 4 + tt
            r = g % 2
            gk = g * TOPK
            row0 = g * 128
            om = sc.op("vector", lambda e, r=r: e.memset(macc[r][:], 0.0),
                       writes=[f"{ph}_macc{r}"], tag="macc0")
            for k in range(TOPK):
                sc.op("gpsimd", lambda e, r=r, gk=gk, k=k: e.indirect_dma_start(
                    out=macc[r][:], out_offset=None,
                    in_=self.dsc["y_tab"],
                    in_offset=IndirectOffsetOnAxis(
                        ap=self.rec_slot[:, gk + k:gk + k + 1], axis=0),
                    compute_op=ALU.add),
                    reads=[("ytab_done", call), ("rec_slot", g), "ytab_trash"],
                    writes=[f"{ph}_macc{r}"], deps=[om], dma=True, tag="moegat")
            self.dma(xb[r][:], src_x[row0:row0 + 128, :],
                     reads=[src_key], writes=[f"{ph}_xb{r}"], tag="ldx2")
            sc.op("vector", lambda e, r=r, g=g: e.tensor_add(
                out=macc[r][:], in0=macc[r][:], in1=sht[g % 4][:]),
                reads=[f"{ph}_macc{r}", (f"{ph}_sht", g % 4)],
                writes=[f"{ph}_macc{r}"], tag="cmb1")
            sc.op("vector", lambda e, r=r: e.scalar_tensor_tensor(
                out=xb[r][:], in0=macc[r][:], scalar=0.5, in1=xb[r][:],
                op0=ALU.mult, op1=ALU.add),
                reads=[f"{ph}_macc{r}", f"{ph}_xb{r}"], writes=[f"{ph}_xb{r}"],
                tag="cmb2")
            self.dma(dst_x[row0:row0 + 128, :], xb[r][:],
                     reads=[f"{ph}_xb{r}"], writes=[dst_key], tag="stx2")

    def emit_wcast(self):
        w1v = self.din["w1"].rearrange("e (a p) f -> e a p f", p=128)
        w2v = self.din["w2"].rearrange("e (a p) f -> e a p f", p=128)
        b1v = self.dsc["bw1"].rearrange("e p (a f) -> e a p f", a=DG)
        b2v = self.dsc["bw2"].rearrange("e p (a f) -> e a p f", a=FH // 128)
        for ex in range(EXP):
            gate = [("xntd", 8)] if ex == 16 else []
            self.dma(b1v[ex], w1v[ex], reads=gate, writes=[("bw1", ex)],
                     q="gpsimd", tag="wcst1")
            self.dma(b2v[ex], w2v[ex], reads=gate, writes=[("bw2", ex)],
                     q="gpsimd", tag="wcst2")

    def _emit_experts(self, call, ph, xstk, xsT, hidE, ytk,
                      gxe, gsc_, gub, w1b, w1o, w2b):
        sc = self.sched
        bw1d = self.dsc["bw1"]
        bw2d = self.dsc["bw2"]
        xs_d = self.dsc["xs_loc"]
        NF = FH // 128
        shared_keys = ([f"{ph}_swb0", f"{ph}_swb1"]
                       + [(f"{ph}_swc{h}", q) for h in range(2) for q in range(4)])
        for ex in range(EXP):
            par = ex % 2
            wb1, wkp = (w1b, "w1b") if par == 0 else (w1o, "w1o")
            extra = []
            if ex == 1:
                extra = shared_keys
            elif ex == 0 and call == 0:
                extra = ["wc_b"]
            for half in range(2):
                self.dma(wb1[:, half * 2048:(half + 1) * 2048],
                         bw1d[ex, :, half * 2048:(half + 1) * 2048],
                         reads=[("bw1", ex)],
                         writes=[f"{ph}_{wkp}{a}" for a in range(half * 4, half * 4 + 4)]
                         + (extra if half == 0 else []), tag="w1lb")
            for half in range(2):
                self.dma(w2b[:, half * 2048:(half + 1) * 2048],
                         bw2d[ex, :, half * 2048:(half + 1) * 2048],
                         reads=[("bw2", ex)],
                         writes=[f"{ph}_w2b{a}" for a in range(half * 2, half * 2 + 2)],
                         tag="w2lb")
            for st in range(3):
                rr = st % 2
                self.dma(xstk[rr][:],
                         xs_d[ex * CAP + st * 128: ex * CAP + (st + 1) * 128, :],
                         reads=[("xs_done", call)], writes=[f"{ph}_xstk{rr}"],
                         tag="xsld")
                for gd in range(DG):
                    sc.op("tensor", lambda e, rr=rr, gd=gd: e.transpose(
                        out=self.ps_t2[:, 0:128],
                        in_=xstk[rr][:, gd * 128:(gd + 1) * 128],
                        identity=self.c_id[:]),
                        reads=[f"{ph}_xstk{rr}", "ident_b"], writes=["pt2"],
                        tag="xsT")
                    sc.op("vector", lambda e, st=st, gd=gd, par=par: e.tensor_copy(
                        out=xsT[par][:, gd * CAP + st * 128: gd * CAP + (st + 1) * 128],
                        in_=self.ps_t2[:, 0:128]),
                        reads=["pt2"], writes=[f"{ph}_xsT{par}", "pt2"], tag="xsTev")
            for mf in range(NF):
                pp, ppk = (self.ps_a, "ps_a") if mf % 2 == 0 else (self.ps_b, "ps_b")
                for kd in range(DG):
                    sc.op("tensor", lambda e, mf=mf, kd=kd, pp=pp, wb1=wb1,
                          par=par: e.matmul(
                        out=pp[:, 0:CAP],
                        lhsT=wb1[:, kd * FH + mf * 128: kd * FH + (mf + 1) * 128],
                        rhs=xsT[par][:, kd * CAP:(kd + 1) * CAP],
                        start=(kd == 0), stop=(kd == DG - 1)),
                        reads=[f"{ph}_xsT{par}", f"{ph}_{wkp}{kd}"], writes=[ppk],
                        tag="w1mm")
                rr = mf % 2
                self._gelu(pp, ppk, CAP, gxe[rr], f"{ph}_gx{rr}", gsc_[rr],
                           f"{ph}_gs{rr}", gub[rr], f"{ph}_gu{rr}",
                           hidE[:, mf * CAP:(mf + 1) * CAP], f"{ph}_hidE")
            for st in range(3):
                rr = st % 2
                for hhf in range(2):
                    pp, ppk = (self.ps_c, "ps_c") if hhf == 0 else (self.ps_d, "ps_d")
                    for kf in range(NF):
                        sc.op("tensor", lambda e, st=st, hhf=hhf, kf=kf, pp=pp: e.matmul(
                            out=pp[:],
                            lhsT=hidE[:, kf * CAP + st * 128: kf * CAP + (st + 1) * 128],
                            rhs=w2b[:, kf * 1024 + hhf * 512: kf * 1024 + (hhf + 1) * 512],
                            start=(kf == 0), stop=(kf == NF - 1)),
                            reads=[f"{ph}_hidE", f"{ph}_w2b{kf}"], writes=[ppk],
                            tag="w2mm")
                    sc.op("vector", lambda e, rr=rr, hhf=hhf, pp=pp: e.tensor_copy(
                        out=ytk[rr][:, hhf * 512:(hhf + 1) * 512], in_=pp[:]),
                        reads=[ppk], writes=[f"{ph}_ytk{rr}", ppk], tag="yev")
                self.dma(self.dsc["y_tab"][ex * CAP + st * 128: ex * CAP + (st + 1) * 128, :],
                         ytk[rr][:], reads=[f"{ph}_ytk{rr}"], writes=[("ytab", call)],
                         tag="yst")

        sc.op("gpsimd", lambda e: e.engine_nop(),
              reads=[("ytab", call)], writes=[("ytab_done", call)], tag="ybar")

    def emit_mamba(self):
        sc = self.sched
        ph = "mam"
        A = self.arena
        resid = self.dsc["resid"]
        xn1_d = self.dsc["xn1_tok"]
        z_d = self.dsc["z_feat"]
        xnf_d = self.dsc["xn_feat"]

        xbm = [A("f", 1024, ph) for _ in range(2)]
        stat = [A("f", 8, ph) for _ in range(2)]
        spz = A("f", 512, ph)
        spa = A("f", 512, ph)
        ytot = A("f", DG * 1024, ph)
        hcar = A("f", 512, ph)
        sqm = A("b", 1024, ph)
        xn1t = [A("b", 1024, ph) for _ in range(2)]
        tri = A("b", 8192, ph)
        xm = A("b", DG * 1027, ph)
        xc = A("b", DG * 1024, ph)
        cacc = A("b", 1024, ph)
        spe = A("b", 512, ph)
        zstg = [A("b", 512, ph) for _ in range(2)]
        bmt = A("b", 1024, ph)
        cmt = A("b", 1024, ph)
        abuf = [A("b", 1024, ph) for _ in range(2)]
        ubuf = A("b", 1024, ph)
        hbuf = [A("b", 1024, ph) for _ in range(2)]
        wbuf = [A("b", 1024, ph) for _ in range(2)]
        xr = A("b", 4096, ph)
        tails = A("b", 24, ph)
        dtt = A("h", DG * 1024, ph)

        xn2t = tri[:, 0:1024]
        xnft = tri[:, 1024:2048]
        bbv = tri[:, 0:4096]
        cbv = tri[:, 4096:8192]
        ztv = [tri[:, 0:1024], tri[:, 1024:2048]]
        ymv = xm

        wst = [self.ar_w[:, 0:1024], self.ar_w[:, 1024:2048]]
        wdt = [self.ar_w[:, 2048:3072], self.ar_w[:, 3072:4096]]
        wbc = self.ar_w[:, 4096:4608]
        wcc = self.ar_w[:, 4608:5120]
        wout = [self.ar_w[:, 5120:6144], self.ar_w[:, 6144:7168]]
        wotk = [self.ar_w[:, 7168:8192], self.ar_w[:, 8192:9216]]

        bwin_d = self.dsc["bw_inT"].rearrange("(a p) c -> a p c", p=128)
        bwdt_d = self.dsc["bw_dtT"].rearrange("(a p) c -> a p c", p=128)
        bwout_d = self.dsc["bw_outT"].rearrange("(a p) c -> a p c", p=128)
        bwb_d = self.dsc["bw_bT"].rearrange("(a p) c -> a p c", p=128)
        bwc_d = self.dsc["bw_cT"].rearrange("(a p) c -> a p c", p=128)

        for a in range(DG):
            self.dma(wbc[:, a * NST:(a + 1) * NST], bwb_d[a, :, :],
                     reads=[("bwd", "bw_bT")], writes=["m_wbc"], tag="wbld")
            self.dma(wcc[:, a * NST:(a + 1) * NST], bwc_d[a, :, :],
                     reads=[("bwd", "bw_cT")], writes=["m_wcc"], tag="wcld2")

        TRI = "mam_tri"
        XMK = "mam_xm"
        SQK = "mam_sqm"

        for s in range(BL):
            for i in range(NT):
                g = s * NT + i
                r = g % 2
                row0 = g * 128
                kx = f"{ph}_xb{r}"
                self.dma(xbm[r][:], resid[row0:row0 + 128, :],
                         reads=["resid1"], writes=[kx], tag="mldx")
                self._ln_tile(xbm[r], kx, stat[r], f"{ph}_st{r}",
                              sqm, SQK, xn1t[r], f"{ph}_xn1{r}")
                self.dma(xn1_d[row0:row0 + 128, :], xn1t[r][:],
                         reads=[f"{ph}_xn1{r}"], writes=[("xn1d", g)], tag="stxn1")
                self._ln_tile(xn1t[r], f"{ph}_xn1{r}", stat[r], f"{ph}_st{r}",
                              sqm, SQK, xn2t, TRI)
                sc.op("vector", lambda e: e.tensor_mul(
                    out=xn2t[:], in0=xn2t[:], in1=self.c_lng[:]),
                    reads=[TRI, "lng_row"], writes=[TRI], tag="ln2g")
                sc.op("vector", lambda e: e.tensor_add(
                    out=xn2t[:], in0=xn2t[:], in1=self.c_lnb[:]),
                    reads=[TRI, "lnb_row"], writes=[TRI], tag="ln2b")
                for gd in range(DG):
                    sc.op("tensor", lambda e, gd=gd: e.transpose(
                        out=self.ps_t1[:, gd * 128:(gd + 1) * 128],
                        in_=xn2t[:, gd * 128:(gd + 1) * 128], identity=self.c_id[:]),
                        reads=[TRI, "ident_b"], writes=[("pt1", gd)], tag="x2T")
                sc.op("vector", lambda e: e.tensor_copy(out=xnft[:], in_=self.ps_t1[:]),
                      reads=[("pt1", gd) for gd in range(DG)],
                      writes=[TRI] + [("pt1", gd) for gd in range(DG)], tag="x2Tev")
                for gd in range(DG):
                    self.dma(xnf_d[s * D + gd * 128: s * D + (gd + 1) * 128,
                                   i * 128:(i + 1) * 128],
                             xnft[:, gd * 128:(gd + 1) * 128],
                             reads=[TRI], writes=[("x2fd", s, gd, i)], tag="stx2f")

            for hh in range(2):
                t0 = hh * TC
                if hh == 0:
                    sc.op("vector", lambda e: e.memset(tails[:], 0.0),
                          writes=["mam_tails"], tag="tail0")
                for nn in range(2):
                    for kd in range(DG):
                        self.dma(xr[:, kd * 512:(kd + 1) * 512],
                                 xnf_d[s * D + kd * 128: s * D + (kd + 1) * 128,
                                       t0 + nn * 512: t0 + (nn + 1) * 512],
                                 reads=[("x2fd", s, kd, q) for q in
                                        range(hh * 8 + nn * 4, hh * 8 + nn * 4 + 4)],
                                 writes=[("mam_xr", kd)], tag="xrld")
                    for me in range(16):
                        wreg = wst[me % 2]
                        wkey = f"mam_wst{me % 2}"
                        for a in range(DG):
                            self.dma(wreg[:, a * 128:(a + 1) * 128],
                                     bwin_d[a, :, me * 128:(me + 1) * 128],
                                     reads=[("bwd", "bw_inT")], writes=[wkey],
                                     tag="winld")
                        pp, ppk = (self.ps_a, "ps_a") if me % 2 == 0 else (self.ps_b, "ps_b")
                        for kd in range(DG):
                            sc.op("tensor", lambda e, kd=kd, wreg=wreg, pp=pp: e.matmul(
                                out=pp[:], lhsT=wreg[:, kd * 128:(kd + 1) * 128],
                                rhs=xr[:, kd * 512:(kd + 1) * 512],
                                start=(kd == 0), stop=(kd == DG - 1)),
                                reads=[("mam_xr", kd), wkey], writes=[ppk],
                                tag="winmm")
                        if me < 8:
                            sc.op("vector", lambda e, me=me, nn=nn, pp=pp: e.tensor_copy(
                                out=xm[:, me * 1027 + 3 + nn * 512:
                                       me * 1027 + 3 + (nn + 1) * 512],
                                in_=pp[:]),
                                reads=[ppk], writes=[XMK, ppk], tag="xmev")
                        else:
                            zr = me % 2
                            sc.op("vector", lambda e, zr=zr, pp=pp: e.tensor_copy(
                                out=zstg[zr][:], in_=pp[:]),
                                reads=[ppk], writes=[f"mam_zs{zr}", ppk], tag="zev")
                            self.dma(z_d[(me - 8) * 128:(me - 7) * 128,
                                         s * S + t0 + nn * 512: s * S + t0 + (nn + 1) * 512],
                                     zstg[zr][:],
                                     reads=[f"mam_zs{zr}"], writes=[("zd", s, hh)],
                                     tag="zst")
                if hh == 0:
                    sc.op("vector", lambda e: e.memset(
                        xm.rearrange("p (g c) -> p g c", g=DG)[:, :, 0:3], 0.0),
                        reads=[XMK], writes=[XMK], tag="convpad")
                else:
                    for gd in range(DG):
                        sc.op("vector", lambda e, gd=gd: e.tensor_copy(
                            out=xm[:, gd * 1027: gd * 1027 + 3],
                            in_=tails[:, gd * 3:(gd + 1) * 3]),
                            reads=["mam_tails"], writes=[XMK], tag="convcarry")
                for gd in range(DG):
                    base = gd * 1027
                    sc.op("vector", lambda e, base=base, gd=gd: e.tensor_scalar(
                        out=cacc[:], in0=xm[:, base:base + 1024],
                        scalar1=self.p_convw[:, 4 * gd:4 * gd + 1], scalar2=None,
                        op0=ALU.mult),
                        reads=[XMK, "p_convw"], writes=["mam_cacc"], tag="cv0")
                    for j in range(1, 4):
                        sc.op("vector", lambda e, base=base, gd=gd, j=j:
                              e.scalar_tensor_tensor(
                                  out=cacc[:], in0=xm[:, base + j:base + j + 1024],
                                  scalar=self.p_convw[:, 4 * gd + j:4 * gd + j + 1],
                                  in1=cacc[:], op0=ALU.mult, op1=ALU.add),
                              reads=[XMK, "p_convw", "mam_cacc"],
                              writes=["mam_cacc"], tag="cvj")
                    sc.op("vector", lambda e, base=base, gd=gd: e.tensor_copy(
                        out=tails[:, gd * 3:(gd + 1) * 3],
                        in_=xm[:, base + 1024: base + 1027]),
                        reads=[XMK], writes=["mam_tails"], tag="tailsv")
                    sc.op("scalar", lambda e, gd=gd: e.activation(
                        out=sqm[:], in_=cacc[:], func=AF.Sigmoid,
                        bias=self.p_convb[:, gd:gd + 1]),
                        reads=["mam_cacc", "p_convb"], writes=[SQK], tag="cvsig")
                    sc.op("vector", lambda e, gd=gd: e.scalar_tensor_tensor(
                        out=xc[:, gd * 1024:(gd + 1) * 1024], in0=cacc[:],
                        scalar=self.p_convb[:, gd:gd + 1], in1=sqm[:],
                        op0=ALU.add, op1=ALU.mult),
                        reads=["mam_cacc", "p_convb", SQK], writes=["mam_xc"],
                        tag="cvmul")
                for md in range(DG):
                    wreg = wdt[md % 2]
                    wkey = f"mam_wdt{md % 2}"
                    for a in range(DG):
                        self.dma(wreg[:, a * 128:(a + 1) * 128],
                                 bwdt_d[a, :, md * 128:(md + 1) * 128],
                                 reads=[("bwd", "bw_dtT")], writes=[wkey], tag="wdtld")
                    for nn in range(2):
                        pp, ppk = (self.ps_c, "ps_c") if nn == 0 else (self.ps_d, "ps_d")
                        for kd in range(DG):
                            sc.op("tensor", lambda e, kd=kd, wreg=wreg, pp=pp, nn=nn:
                                  e.matmul(out=pp[:],
                                           lhsT=wreg[:, kd * 128:(kd + 1) * 128],
                                           rhs=xc[:, kd * 1024 + nn * 512:
                                                  kd * 1024 + (nn + 1) * 512],
                                           start=(kd == 0), stop=(kd == DG - 1)),
                                  reads=["mam_xc", wkey], writes=[ppk], tag="wdtmm")
                        sc.op("vector", lambda e, md=md, pp=pp: e.tensor_scalar(
                            out=spz[:], in0=pp[:],
                            scalar1=self.p_bdt[:, md:md + 1], scalar2=None,
                            op0=ALU.add),
                            reads=[ppk, "p_bdt"], writes=["mam_spz", ppk], tag="spz")
                        sc.op("vector", lambda e: e.tensor_scalar_mul(
                            out=spa[:], in0=spz[:], scalar1=-1.0),
                            reads=["mam_spz"], writes=["mam_spa"], tag="spneg")
                        sc.op("vector", lambda e: e.tensor_max(
                            out=spa[:], in0=spa[:], in1=spz[:]),
                            reads=["mam_spa", "mam_spz"], writes=["mam_spa"],
                            tag="spabs")
                        sc.op("scalar", lambda e: e.activation(
                            out=spe[:], in_=spa[:], func=AF.Exp, scale=-1.0),
                            reads=["mam_spa"], writes=["mam_spe"], tag="spexp")
                        sc.op("scalar", lambda e: e.activation(
                            out=spa[:], in_=spe[:], func=AF.Ln,
                            bias=self.c_kcol[:, 1:2]),
                            reads=["mam_spe", "kcol"], writes=["mam_spa"],
                            tag="spln")
                        sc.op("vector", lambda e: e.tensor_relu(out=spz[:], in_=spz[:]),
                              reads=["mam_spz"], writes=["mam_spz"], tag="sprelu")
                        sc.op("vector", lambda e, md=md, nn=nn: e.tensor_add(
                            out=dtt[:, md * 1024 + nn * 512: md * 1024 + (nn + 1) * 512],
                            in0=spz[:], in1=spa[:]),
                            reads=["mam_spz", "mam_spa"], writes=["mam_dt"],
                            tag="spadd")
                for nn in range(2):
                    for kd in range(DG):
                        sc.op("tensor", lambda e, kd=kd, nn=nn: e.matmul(
                            out=self.ps_e[0:NST, :],
                            lhsT=wbc[:, kd * NST:(kd + 1) * NST],
                            rhs=xc[:, kd * 1024 + nn * 512: kd * 1024 + (nn + 1) * 512],
                            start=(kd == 0), stop=(kd == DG - 1)),
                            reads=["mam_xc", "m_wbc"], writes=["ps_e"], tag="wbmm")
                    sc.op("vector", lambda e, nn=nn: e.tensor_copy(
                        out=bmt[0:NST, nn * 512:(nn + 1) * 512],
                        in_=self.ps_e[0:NST, :]),
                        reads=["ps_e"], writes=["mam_bm", "ps_e"], tag="bmev")
                    for kd in range(DG):
                        sc.op("tensor", lambda e, kd=kd, nn=nn: e.matmul(
                            out=self.ps_f[0:NST, :],
                            lhsT=wcc[:, kd * NST:(kd + 1) * NST],
                            rhs=xc[:, kd * 1024 + nn * 512: kd * 1024 + (nn + 1) * 512],
                            start=(kd == 0), stop=(kd == DG - 1)),
                            reads=["mam_xc", "m_wcc"], writes=["ps_f"], tag="wcmm")
                    sc.op("vector", lambda e, nn=nn: e.tensor_copy(
                        out=cmt[0:NST, nn * 512:(nn + 1) * 512],
                        in_=self.ps_f[0:NST, :]),
                        reads=["ps_f"], writes=["mam_cm", "ps_f"], tag="cmev")

                if self.debug and s == 0 and hh == 0:
                    self.dma(self.dbg["d_xc"].rearrange("(g p) c -> p g c", p=128),
                             xc.rearrange("p (g c) -> p g c", g=DG),
                             reads=["mam_xc"], writes=[("dbg", "xc")])
                    self.dma(self.dbg["d_dt"].rearrange("(g p) c -> p g c", p=128),
                             dtt.rearrange("p (g c) -> p g c", g=DG),
                             reads=["mam_dt"], writes=[("dbg", "dt")])
                    self.dbg_dump("d_bm", bmt[0:NST, :], ["mam_bm"])
                for blk in range(NST // NBLK):
                    for q in range(NBLK):
                        n = blk * NBLK + q
                        for nn in range(2):
                            pp, ppk = (self.ps_e, "ps_e") if nn == 0 else (self.ps_f, "ps_f")
                            sc.op("tensor", lambda e, n=n, nn=nn, pp=pp: e.matmul(
                                out=pp[:],
                                lhsT=self.c_oneh[:, n * 128:(n + 1) * 128],
                                rhs=bmt[0:NST, nn * 512:(nn + 1) * 512],
                                start=True, stop=True),
                                reads=["mam_bm", "onehot_b"], writes=[ppk],
                                tag="bbmm")
                            sc.op("scalar", lambda e, q=q, nn=nn, pp=pp: e.activation(
                                out=bbv[:, q * 1024 + nn * 512: q * 1024 + (nn + 1) * 512],
                                in_=pp[:], func=AF.Copy),
                                reads=[ppk], writes=[TRI, ppk], tag="bbev")
                            sc.op("tensor", lambda e, n=n, nn=nn, pp=pp: e.matmul(
                                out=pp[:],
                                lhsT=self.c_oneh[:, n * 128:(n + 1) * 128],
                                rhs=cmt[0:NST, nn * 512:(nn + 1) * 512],
                                start=True, stop=True),
                                reads=["mam_cm", "onehot_b"], writes=[ppk],
                                tag="cbmm")
                            sc.op("scalar", lambda e, q=q, nn=nn, pp=pp: e.activation(
                                out=cbv[:, q * 1024 + nn * 512: q * 1024 + (nn + 1) * 512],
                                in_=pp[:], func=AF.Copy),
                                reads=[ppk], writes=[TRI, ppk], tag="cbev")
                    for gd in range(DG):
                        for q in range(NBLK):
                            n = blk * NBLK + q
                            ra = (gd * NBLK + q) % 2
                            rh = ra
                            sc.op("scalar", lambda e, gd=gd, n=n, ra=ra: e.activation(
                                out=abuf[ra][:], in_=dtt[:, gd * 1024:(gd + 1) * 1024],
                                func=AF.Exp, scale=self.p_A[:, n:n + 1]),
                                reads=["mam_dt", "p_A"], writes=[f"mam_ab{ra}"],
                                tag="aexp")
                            sc.op("vector", lambda e, gd=gd, q=q: e.tensor_mul(
                                out=ubuf[:], in0=xc[:, gd * 1024:(gd + 1) * 1024],
                                in1=bbv[:, q * 1024:(q + 1) * 1024]),
                                reads=["mam_xc", TRI], writes=["mam_ub"], tag="umul")
                            if hh == 0:
                                sc.op("vector", lambda e, ra=ra, rh=rh:
                                      e.tensor_tensor_scan(
                                          out=hbuf[rh][:], data0=abuf[ra][:],
                                          data1=ubuf[:], initial=0.0,
                                          op0=ALU.mult, op1=ALU.add),
                                      reads=[f"mam_ab{ra}", "mam_ub"],
                                      writes=[f"mam_hb{rh}"], tag="tts")
                                sc.op("vector", lambda e, gd=gd, n=n, rh=rh:
                                      e.tensor_copy(
                                          out=hcar[:, gd * NST + n: gd * NST + n + 1],
                                          in_=hbuf[rh][:, TC - 1:TC]),
                                      reads=[f"mam_hb{rh}"], writes=["mam_hc"],
                                      tag="hcar")
                            else:
                                ini = hcar[:, gd * NST + n: gd * NST + n + 1]
                                sc.op("vector", lambda e, ra=ra, rh=rh, ini=ini:
                                      e.tensor_tensor_scan(
                                          out=hbuf[rh][:], data0=abuf[ra][:],
                                          data1=ubuf[:], initial=ini,
                                          op0=ALU.mult, op1=ALU.add),
                                      reads=[f"mam_ab{ra}", "mam_ub", "mam_hc"],
                                      writes=[f"mam_hb{rh}"], tag="tts")
                            sc.op("vector", lambda e, rh=rh, q=q: e.tensor_mul(
                                out=wbuf[rh][:], in0=hbuf[rh][:],
                                in1=cbv[:, q * 1024:(q + 1) * 1024]),
                                reads=[f"mam_hb{rh}", TRI], writes=[f"mam_wb{rh}"],
                                tag="wmul")
                            if q == 0:
                                self.dma(ymv[:, gd * 1024:(gd + 1) * 1024],
                                         wbuf[rh][:],
                                         reads=[f"mam_wb{rh}"],
                                         writes=[("mam_yg", gd), XMK],
                                         q="gpsimd", tag="yg0")
                            else:
                                sc.op("gpsimd", lambda e, gd=gd, rh=rh: e.dma_start(
                                    out=ymv[:, gd * 1024:(gd + 1) * 1024],
                                    in_=wbuf[rh][:], accum_op=ALU.add),
                                    reads=[f"mam_wb{rh}", ("mam_yg", gd)],
                                    writes=[("mam_yg", gd), XMK], dma=True, tag="ygacc")
                    for gd in range(DG):
                        if blk == 0:
                            sc.op("vector", lambda e, gd=gd: e.tensor_copy(
                                out=ytot[:, gd * 1024:(gd + 1) * 1024],
                                in_=ymv[:, gd * 1024:(gd + 1) * 1024]),
                                reads=[("mam_yg", gd)], writes=[("mam_yt", gd)],
                                tag="ytcp")
                        else:
                            sc.op("vector", lambda e, gd=gd: e.tensor_add(
                                out=ytot[:, gd * 1024:(gd + 1) * 1024],
                                in0=ytot[:, gd * 1024:(gd + 1) * 1024],
                                in1=ymv[:, gd * 1024:(gd + 1) * 1024]),
                                reads=[("mam_yg", gd), ("mam_yt", gd)],
                                writes=[("mam_yt", gd)], tag="ytadd")

                if self.debug and s == 0 and hh == 0:
                    self.dma(self.dbg["d_ytot"].rearrange("(g p) c -> p g c", p=128),
                             ytot.rearrange("p (g c) -> p g c", g=DG),
                             reads=[("mam_yt", gd) for gd in range(DG)],
                             writes=[("dbg", "ytot")])
                for gd in range(DG):
                    sc.op("vector", lambda e, gd=gd: e.scalar_tensor_tensor(
                        out=ytot[:, gd * 1024:(gd + 1) * 1024],
                        in0=xc[:, gd * 1024:(gd + 1) * 1024],
                        scalar=self.p_dp[:, gd:gd + 1],
                        in1=ytot[:, gd * 1024:(gd + 1) * 1024],
                        op0=ALU.mult, op1=ALU.add),
                        reads=["mam_xc", "p_dp", ("mam_yt", gd)],
                        writes=[("mam_yt", gd)], tag="dpterm")
                for gd in range(DG):
                    rz = gd % 2
                    self.dma(ztv[rz][:],
                             z_d[gd * 128:(gd + 1) * 128, s * S + t0: s * S + t0 + TC],
                             reads=[("zd", s, hh)], writes=[f"mam_zt{rz}", TRI],
                             tag="zld")
                    sc.op("scalar", lambda e, rz=rz: e.activation(
                        out=sqm[:], in_=ztv[rz][:], func=AF.Sigmoid),
                        reads=[f"mam_zt{rz}"], writes=[SQK], tag="zsig")
                    sc.op("vector", lambda e, rz=rz: e.tensor_mul(
                        out=ztv[rz][:], in0=ztv[rz][:], in1=sqm[:]),
                        reads=[f"mam_zt{rz}", SQK], writes=[f"mam_zt{rz}"],
                        tag="zmul")
                    sc.op("vector", lambda e, gd=gd, rz=rz: e.tensor_mul(
                        out=ymv[:, gd * 1024:(gd + 1) * 1024],
                        in0=ytot[:, gd * 1024:(gd + 1) * 1024], in1=ztv[rz][:]),
                        reads=[("mam_yt", gd), f"mam_zt{rz}"],
                        writes=[("mam_ym", gd), ("mam_yg", gd), "mam_xm"],
                        tag="ygate")
                self._wout_combine(s, hh, ymv, wout, bwout_d, xbm, tri, resid,
                                   xn1_d)

    def _wout_combine(self, s, hh, ymv, wout, bwout_d, xbm, tri, resid, xn1_d):
        sc = self.sched
        xn1rv = [tri[:, 2048 + i * 1024: 2048 + (i + 1) * 1024] for i in range(2)]
        psb0 = [(self.ps_a, "ps_a"), (self.ps_b, "ps_b")]
        psb1 = [(self.ps_c, "ps_c"), (self.ps_d, "ps_d")]
        for ttg in range(4):
            tts = [ttg * 2, ttg * 2 + 1]
            for tt in tts:
                g = s * NT + hh * 8 + tt
                row0 = g * 128
                self.dma(xbm[tt % 2][:], resid[row0:row0 + 128, :],
                         reads=["resid1"], writes=[("mam_xbt", tt % 2)], tag="mldx2")
                self.dma(xn1rv[tt % 2][:], xn1_d[row0:row0 + 128, :],
                         reads=[("xn1d", g)], writes=[("mam_x1r", tt % 2), "mam_tri"],
                         tag="ldxn1")
            for hhf in range(2):
                wreg = wout[hhf]
                wkey = f"mam_wo{hhf}"
                psb = psb0 if hhf == 0 else psb1
                for grp in range(4):
                    for qq in range(2):
                        kd = grp * 2 + qq
                        self.dma(wreg[:, qq * 512:(qq + 1) * 512],
                                 bwout_d[kd, :, hhf * 512:(hhf + 1) * 512],
                                 reads=[("bwd", "bw_outT")], writes=[(wkey, qq)],
                                 tag="wold")
                        for ii, tt in enumerate(tts):
                            pp, ppk = psb[ii]
                            sc.op("tensor", lambda e, kd=kd, qq=qq, wreg=wreg,
                                  pp=pp, tt=tt: e.matmul(
                                      out=pp[:],
                                      lhsT=ymv[:, kd * 1024 + tt * 128:
                                               kd * 1024 + (tt + 1) * 128],
                                      rhs=wreg[:, qq * 512:(qq + 1) * 512],
                                      start=(kd == 0), stop=(kd == DG - 1)),
                                  reads=[("mam_ym", kd), "mam_xm", (wkey, qq)],
                                  writes=[ppk], tag="womm")
                for ii, tt in enumerate(tts):
                    pp, ppk = psb[ii]
                    sc.op("vector", lambda e, tt=tt, hhf=hhf, pp=pp:
                          e.tensor_add(
                              out=xbm[tt % 2][:, hhf * 512:(hhf + 1) * 512],
                              in0=xbm[tt % 2][:, hhf * 512:(hhf + 1) * 512],
                              in1=pp[:]),
                          reads=[("mam_xbt", tt % 2), ppk],
                          writes=[("mam_xbt", tt % 2), ppk], tag="mcmb1")
            for tt in tts:
                g = s * NT + hh * 8 + tt
                row0 = g * 128
                sc.op("vector", lambda e, tt=tt: e.tensor_add(
                    out=xbm[tt % 2][:], in0=xbm[tt % 2][:], in1=xn1rv[tt % 2][:]),
                    reads=[("mam_xbt", tt % 2), ("mam_x1r", tt % 2)],
                    writes=[("mam_xbt", tt % 2)], tag="mcmb2")
                self.dma(resid[row0:row0 + 128, :], xbm[tt % 2][:],
                         reads=[("mam_xbt", tt % 2)], writes=["resid1"], tag="mstx")

    def emit_finish(self):
        sc = self.sched
        for eng in ("vector", "gpsimd"):
            sc.op(eng, lambda e: e.engine_nop(), reads=["xout"], tag="fin")


if __name__ == "__main__":
    nc = build_program()
    print("built ok")
